# version 12
# baseline (speedup 1.0000x reference)
import numpy as np
import ml_dtypes
from contextlib import ExitStack
import concourse.bass as bass
import concourse.mybir as mybir
from concourse.bass_utils import run_bass_kernel_spmd

F32 = mybir.dt.float32
BF16 = mybir.dt.bfloat16
I32 = mybir.dt.int32
AF = mybir.ActivationFunctionType
ALU = mybir.AluOpType
AX = mybir.AxisListType
BFNP = ml_dtypes.bfloat16

B, S, D = 4, 4096, 1024
T = 2048
LN_EPS = 1e-5


def l0_rope_cos_sin(positions):
    inv_freq = 1.0 / 10000.0 ** (np.arange(0, 64, 2, dtype=np.float32) / 64)
    t = np.asarray(positions, dtype=np.float32)
    freqs = np.outer(t, inv_freq)
    cos = np.concatenate([np.cos(freqs), np.cos(freqs)], -1)[:, ::2]
    sin = np.concatenate([np.sin(freqs), np.sin(freqs)], -1)[:, ::2]
    return cos.astype(np.float32), sin.astype(np.float32)


def l0_qk_perm():
    perm = []
    for h in range(16):
        perm += [h * 64 + 2 * i for i in range(32)]
        perm += [h * 64 + 2 * i + 1 for i in range(32)]
    return np.array(perm, dtype=np.int64)


def l0_rope_tiles(positions, scale):
    cos, sin = l0_rope_cos_sin(positions)
    n = len(positions)
    CS = np.empty((128, n), np.float32)
    SS = np.empty((128, n), np.float32)
    for b0 in (0, 64):
        CS[b0:b0 + 32] = cos.T
        CS[b0 + 32:b0 + 64] = cos.T
        SS[b0:b0 + 32] = -sin.T
        SS[b0 + 32:b0 + 64] = sin.T
    return (CS * scale).astype(np.float32), (SS * scale).astype(np.float32)



class Sem:
    __slots__ = ("h", "n")
    def __init__(self, h):
        self.h = h
        self.n = 0


class Prog:
    ENGINES = ("tensor", "vector", "scalar", "sync", "gpsimd")

    def __init__(self, nc, gstack):
        self.nc = nc
        self.gstack = gstack
        self.blk = gstack.enter_context(nc.Block())
        self.sems = {n: Sem(gstack.enter_context(nc.semaphore(n)))
                     for n in ("sPE", "sV", "sSC", "sX", "sW", "sST", "sG")}
        self.q = {e: [] for e in self.ENGINES}
        self._barrier = {}
        self._tails = {}

    def sem(self, name):
        return self.sems[name]

    def on(self, eng, fn):
        self.q[eng].append(fn)

    def wait(self, eng, semname, val=None):
        s = self.sems[semname]
        v = s.n if val is None else val
        self.q[eng].append(lambda e: e.wait_ge(s.h, v))

    def inc(self, eng, fn, semname, k):
        if eng == "vector":
            sv = self.sems["sV"]
            v = sv.n
            self.q[eng].append(lambda e: e.wait_ge(sv.h, v))
        elif eng == "scalar":
            ss = self.sems["sSC"]
            v = ss.n
            self.q[eng].append(lambda e: e.wait_ge(ss.h, v))
        elif eng == "gpsimd" and semname == "sG":
            sg = self.sems["sG"]
            v = sg.n
            self.q[eng].append(lambda e: e.wait_ge(sg.h, v))
        s = self.sems[semname]
        s.n += k
        self.q[eng].append(lambda e: fn(e).then_inc(s.h, k))

    def snap(self):
        return {n: s.n for n, s in self.sems.items()}

    def bw(self, eng, key, idx, sems=("sV",)):
        t = self._tails.setdefault(key, {})
        for s in sems:
            v = t.get((idx, s), 0)
            if v > 0:
                self.wait(eng, s, v)

    def bd(self, key, idx, sems=("sV",)):
        t = self._tails.setdefault(key, {})
        for s in sems:
            t[(idx, s)] = self.sems[s].n

    def flush(self, barrier=True):
        bar = self._barrier if barrier else {}
        for eng in self.ENGINES:
            closures = self.q[eng]
            if not closures and not bar:
                continue
            waits = [(self.sems[n].h, v) for n, v in bar.items() if v > 0]
            def body(e, closures=list(closures), waits=waits):
                for h, v in waits:
                    e.wait_ge(h, v)
                for fn in closures:
                    fn(e)
            getattr(self.blk, eng)(body)
        self.q = {e: [] for e in self.ENGINES}
        self._barrier = self.snap()


def ceil_div(a, b):
    return (a + b - 1) // b

def mk_globals(nc, p, gst):
    SBg = lambda n, sh, dt=BF16: gst.enter_context(nc.sbuf_tensor(n, sh, dt))
    ident = SBg("g_ident_sb", [128, 128])
    ones1 = SBg("g_ones1_sb", [1, 128])
    iotaf = SBg("g_iotaf_sb", [128, 128], I32)
    iotap = SBg("g_iotap_sb", [128, 128], I32)
    epsb = SBg("g_epsb_sb", [128, 1], F32)
    p.inc("gpsimd", lambda e: e.iota(iotaf[:, :], pattern=[[1, 128]], base=0, channel_multiplier=0), "sW", 1)
    p.inc("gpsimd", lambda e: e.iota(iotap[:, :], pattern=[[0, 128]], base=0, channel_multiplier=1), "sW", 1)
    p.wait("vector", "sW")
    p.inc("vector", lambda e: e.tensor_tensor(ident[:, :], iotaf[:, :], iotap[:, :], ALU.is_equal), "sV", 1)
    p.inc("vector", lambda e: e.memset(ones1[:, :], 1.0), "sV", 1)
    p.inc("vector", lambda e: e.memset(epsb[:, :], LN_EPS), "sV", 1)
    eps24 = SBg("g_eps24_sb", [128, 1], F32)
    p.inc("vector", lambda e: e.memset(eps24[:, :], 1e-24), "sV", 1)
    p.flush()
    return dict(ident=ident, ones1=ones1, epsb=epsb, eps24=eps24)



def build_A(nq, phases=99, ctx=None):
    fused = ctx is not None
    if fused:
        nc = ctx[0]
    else:
        nc = bass.Bass("TRN2", target_bir_lowering=False)
    ein = lambda n, sh, dt=F32: nc.dram_tensor(n, sh, dt, kind="ExternalInput").ap()
    x_h = ein("x", [S, D])
    xq_h = ein("xq", [nq, D])
    qcs_h = ein("qcs", [128, nq]); qss_h = ein("qss", [128, nq])
    kcs_h = ein("kcs", [128, S]);  kss_h = ein("kss", [128, S])
    wdq_h = ein("wdq_t", [D, 512]); wdkv_h = ein("wdkv_t", [D, 512])
    wuq_h = ein("wuq_tp", [512, D]); wuq2_h = ein("wuq_tps", [512, D])
    wuk_h = ein("wuk_tp", [512, D]); wuk2_h = ein("wuk_tps", [512, D])
    wuv_h = ein("wuv_t", [512, D])
    wout_h = ein("wout_t", [D, D])
    gam_h = ein("gamma8", [128, 8]); bet_h = ein("beta8", [128, 8])
    inda_h = ein("ind_a", [128, 1]); indb_h = ein("ind_b", [128, 1])
    wg_h = ein("wg_t", [D, 3072], I32); wu_h = ein("wu_t", [D, 3072], I32)
    wd_h = ein("wd_t", [3072, D], I32)
    if fused:
        x2_h = ctx[3]
        xq_h = x_h
        qall_scr = nc.dram_tensor("qall_scr", [128, 8 * nq], BF16, kind="Internal").ap()
    else:
        x2_h = nc.dram_tensor("x2", [nq, D], F32, kind="ExternalOutput").ap()
        qall_scr = None
    x1s_h = nc.dram_tensor("x1_scr", [nq, D], F32, kind="Internal").ap()

    NQT = nq // 512
    NQC = nq // 128
    NKT = S // 512
    NKC = S // 128

    with ExitStack() as gst:
        if fused:
            p = ctx[1]
            g = ctx[4]
        else:
            p = Prog(nc, gst)
            g = mk_globals(nc, p, gst)
        ident = g["ident"]; ones1 = g["ones1"]; epsb = g["epsb"]
        p.flush()

        def ld_sync(dst, src):
            p.inc("sync", lambda e, d=dst, s=src: e.dma_start(out=d, in_=s), "sX", 16)
        def ld_cast(dst, src):
            p.inc("gpsimd", lambda e, d=dst, s=src: e.dma_start(out=d, in_=s), "sW", 16)

        qstat = gst.enter_context(nc.sbuf_tensor("qstat_sb", [128, 2 * NQC], F32))
        mx = gst.enter_context(nc.sbuf_tensor("mx_sb", [1, 36], F32))
        negm = gst.enter_context(nc.sbuf_tensor("negm_sb", [128, 1], F32))
        s_cq = ExitStack(); s_ckv = ExitStack(); s_qk = ExitStack(); s_o = ExitStack()
        cq = s_cq.enter_context(nc.sbuf_tensor("cq_sb", [128, 4 * nq], BF16, side="right"))
        ckv = s_ckv.enter_context(nc.sbuf_tensor("ckv_sb", [128, 4 * S], BF16))
        p.on("vector", lambda e: e.memset(mx[:, :], 0.0))

        with ExitStack() as sp:
            SB2 = lambda n, sh, dt=BF16: sp.enter_context(nc.sbuf_tensor(n, sh, dt))
            PS2 = lambda n, sh, dt=F32: sp.enter_context(nc.psum_tensor(n, sh, dt))
            wdq = SB2("wdq_sb", [128, 8 * 512]); wdkv = SB2("wdkv_sb", [128, 8 * 512])
            gam = SB2("gam_sb", [128, 8], F32); bet = SB2("bet_sb", [128, 8], F32)
            xsl = [SB2(f"xsl{i}", [128, D], F32) for i in range(4)]
            h1b = [SB2(f"h1b_sb{i}", [128, D]) for i in range(2)]
            zsl = [SB2(f"zsl_sb{i}", [128, D]) for i in range(4)]
            stat = SB2("stat_sb", [128, 64], F32)
            xs32 = SB2("xs32_sb", [128, 1024], F32)
            q2s = SB2("q2s_sb", [128, 1024])
            h2sl = SB2("h2sl_sb", [128, 8 * 512])
            ps_c = [PS2(f"ps_c{i}", [128, 512]) for i in range(2)]
            ps_tr = [PS2(f"ps_tr{i}", [128, 512], BF16) for i in range(2)]

            ld_sync(gam[:, :], gam_h[:, :]); ld_sync(bet[:, :], bet_h[:, :])
            for kc in range(8):
                ld_cast(wdq[:, kc * 512:(kc + 1) * 512], wdq_h[kc * 128:(kc + 1) * 128, :])
                ld_cast(wdkv[:, kc * 512:(kc + 1) * 512], wdkv_h[kc * 128:(kc + 1) * 128, :])
            p.wait("tensor", "sW"); p.wait("tensor", "sX")

            cap = {}

            def pair_chain(i, srcs, qidx):
                pp = i % 2
                xb = [xsl[2 * pp + a] for a in (0, 1)]
                p.wait("sync", "sSC", cap.get(("D", i - 2), 0))
                for a in (0, 1):
                    p.inc("sync", lambda e, s=srcs[a], d=xb[a]: e.dma_start(out=d[:, :], in_=s), "sX", 16)
                for c in (0, 1):
                    b = 32 * pp + 16 * c
                    src2 = xb if c == 0 else h1b
                    p.wait("scalar", "sX")
                    for a in (0, 1):
                        p.inc("scalar", lambda e, s=src2[a], b_=b, a_=a: e.activation(
                            q2s[:, 0:1024], s[:, :], AF.Square,
                            accum_out=stat[:, b_ + 2 + a_:b_ + 3 + a_]), "sSC", 1)
                    for a in (0, 1):
                        p.inc("scalar", lambda e, s=src2[a], b_=b, a_=a: e.activation(
                            xs32[:, 0:1024], s[:, :], AF.Identity,
                            accum_out=stat[:, b_ + a_:b_ + a_ + 1]), "sSC", 1)
                    if c == 0:
                        cap[("Cx", i)] = p.sems["sSC"].n
                    p.wait("vector", "sSC")
                    p.inc("vector", lambda e, b_=b: e.tensor_scalar_mul(
                        stat[:, b_ + 4:b_ + 6], stat[:, b_:b_ + 2], -1.0 / D), "sV", 1)
                    p.inc("vector", lambda e, b_=b: e.tensor_mul(
                        stat[:, b_ + 6:b_ + 8], stat[:, b_ + 4:b_ + 6], stat[:, b_ + 4:b_ + 6]), "sV", 1)
                    p.inc("vector", lambda e, b_=b: e.scalar_tensor_tensor(
                        stat[:, b_ + 8:b_ + 10], stat[:, b_ + 2:b_ + 4], 1.0 / D,
                        stat[:, b_ + 6:b_ + 8], ALU.mult, ALU.subtract), "sV", 1)
                    p.wait("scalar", "sV")
                    p.inc("scalar", lambda e, b_=b: e.activation(
                        stat[:, b_ + 10:b_ + 12], stat[:, b_ + 8:b_ + 10], AF.Sqrt,
                        bias=epsb[:, 0:1]), "sSC", 1)
                    p.wait("vector", "sSC")
                    p.inc("vector", lambda e, b_=b: e.reciprocal(
                        stat[:, b_ + 12:b_ + 14], stat[:, b_ + 10:b_ + 12]), "sV", 1)
                    p.inc("vector", lambda e, b_=b: e.tensor_mul(
                        stat[:, b_ + 14:b_ + 16], stat[:, b_ + 4:b_ + 6], stat[:, b_ + 12:b_ + 14]), "sV", 1)
                    if c == 0 and qidx[0] >= 0:
                        for a in (0, 1):
                            p.inc("vector", lambda e, j=qidx[a], b_=b, a_=a: e.tensor_scalar_add(
                                qstat[:, 2 * j:2 * j + 1], stat[:, b_ + 12 + a_:b_ + 13 + a_], 1.0), "sV", 1)
                            p.inc("vector", lambda e, j=qidx[a], b_=b, a_=a: e.tensor_copy(
                                qstat[:, 2 * j + 1:2 * j + 2], stat[:, b_ + 14 + a_:b_ + 15 + a_]), "sV", 1)
                    dst2 = h1b if c == 0 else [zsl[2 * pp + a] for a in (0, 1)]
                    p.wait("scalar", "sV")
                    if c == 1:
                        p.wait("scalar", "sPE", cap.get(("H", i - 2), 0))
                    for a in (0, 1):
                        p.inc("scalar", lambda e, s=src2[a], d=dst2[a], b_=b, a_=a: e.activation(
                            d[:, :], s[:, :], AF.Identity,
                            bias=stat[:, b_ + 14 + a_:b_ + 15 + a_],
                            scale=stat[:, b_ + 12 + a_:b_ + 13 + a_]), "sSC", 1)
                    if c == 0:
                        cap[("D", i)] = p.sems["sSC"].n
                    else:
                        cap[("G", i)] = p.sems["sSC"].n

            def pair_H(i, sl0):
                pp = i % 2
                for a in (0, 1):
                    p.wait("tensor", "sSC", cap[("G", i)])
                    for kc in range(8):
                        pt = ps_tr[kc % 2]
                        p.bw("tensor", "Atr", kc % 2)
                        p.inc("tensor", lambda e, k=kc, b=pt, z=zsl[2 * pp + a]: e.transpose(
                            b[:, 0:128], z[:, k * 128:(k + 1) * 128], ident[:, :]), "sPE", 1)
                        p.wait("vector", "sPE")
                        p.inc("vector", lambda e, k=kc, b=pt, s=sl0 + a: e.tensor_scalar(
                            h2sl[:, k * 512 + s * 128: k * 512 + (s + 1) * 128],
                            b[:, 0:128],
                            gam[:, k:k + 1], bet[:, k:k + 1], ALU.mult, ALU.add), "sV", 1)
                        p.bd("Atr", kc % 2)
                cap[("H", i)] = p.sems["sPE"].n

            def emit_cmms(g, groups):
                src_h, tt, dest_c, dest_ntok, wsb, qbase = groups[g]
                p.wait("tensor", "sV")
                for mc in range(4):
                    pb = ps_c[mc % 2]
                    p.bw("tensor", "Ac", mc % 2)
                    for kc in range(8):
                        p.inc("tensor", lambda e, m=mc, k=kc, b=pb, w=wsb: e.matmul(
                            b[:, 0:512], w[:, k * 512 + m * 128:k * 512 + (m + 1) * 128],
                            h2sl[:, k * 512:(k + 1) * 512],
                            start=(k == 0), stop=(k == 7)), "sPE", 1)
                    p.wait("vector", "sPE")
                    p.inc("vector", lambda e, m=mc, b=pb, d=dest_c, nt=dest_ntok, t=tt: e.tensor_copy(
                        d[:, m * nt + t * 512: m * nt + (t + 1) * 512], b[:, 0:512]), "sV", 1)
                    p.bd("Ac", mc % 2)

            if phases >= 2:
                groups = []
                for tt in range(NKT):
                    groups.append((x_h, tt, ckv, S, wdkv, -1))
                for tt in range(NQT):
                    groups.append((xq_h, tt, cq, nq, wdq, 0))
                npairs = 2 * len(groups)

                def pair_meta(i):
                    g = i // 2
                    sl0 = 2 * (i % 2)
                    src_h, tt, dest_c, dest_ntok, wsb, qbase = groups[g]
                    srcs = [src_h[(tt * 4 + sl0 + a) * 128:(tt * 4 + sl0 + a + 1) * 128, :]
                            for a in (0, 1)]
                    qidx = ([qbase + tt * 4 + sl0 + a for a in (0, 1)]
                            if qbase >= 0 else [-1, -1])
                    return srcs, qidx, sl0

                for i in range(npairs + 1):
                    if i >= 1:
                        _, _, sl0m = pair_meta(i - 1)
                        pair_H(i - 1, sl0m)
                        if (i - 1) % 2 == 1:
                            emit_cmms((i - 1) // 2, groups)
                    if i < npairs:
                        srcs, qidx, sl0 = pair_meta(i)
                        pair_chain(i, srcs, qidx)
            p.flush()

        kall = s_qk.enter_context(nc.sbuf_tensor("kall_sb", [128, 8 * S], BF16))
        if fused:
            qall = None
        else:
            qall = s_qk.enter_context(nc.sbuf_tensor("qall_sb", [128, 8 * nq], BF16))

        def qk_rope(dst, n_toks, csrc, c_ntok, w1h, w2h, csh, ssh, mxcol_base, tag, hbm_dst=None):
            with ExitStack() as sp:
                SB3 = lambda n, sh, dt=BF16: sp.enter_context(nc.sbuf_tensor(n, sh, dt))
                PS3 = lambda n, sh, dt=F32: sp.enter_context(nc.psum_tensor(n + str(tag), sh, dt))
                w1 = SB3(f"w1_sb{tag}", [128, 4 * 1024]); w2 = SB3(f"w2_sb{tag}", [128, 4 * 1024])
                cs_t = SB3(f"cs_sb{tag}", [128, n_toks], F32); ss_t = SB3(f"ss_sb{tag}", [128, n_toks], F32)
                t1s = SB3(f"t1s_sb{tag}", [128, 512])
                qsq = SB3(f"qsq_sb{tag}", [128, 512])
                nrm = SB3(f"nrm_sb{tag}", [1, 1024], F32)
                qdst = [SB3(f"qdst{i}{tag}" if False else "qdst%d" % i + str(tag), [128, 512]) for i in range(2)]
                mxsc = SB3(f"mxsc_sb{tag}", [1, 4], F32)
                inda = SB3(f"inda_sb{tag}", [128, 1]); indb = SB3(f"indb_sb{tag}", [128, 1])
                ps_ab = [PS3(f"ps_r{i}", [128, 1024]) for i in range(2)]
                ps_n = PS3("ps_n", [128, 512])
                ld_sync(cs_t[:, :], csh[:, :]); ld_sync(ss_t[:, :], ssh[:, :])
                ld_cast(inda[:, :], inda_h[:, :]); ld_cast(indb[:, :], indb_h[:, :])
                for kc in range(4):
                    ld_cast(w1[:, kc * 1024:(kc + 1) * 1024], w1h[kc * 128:(kc + 1) * 128, :])
                    ld_cast(w2[:, kc * 1024:(kc + 1) * 1024], w2h[kc * 128:(kc + 1) * 128, :])
                p.wait("tensor", "sW"); p.wait("tensor", "sX")
                p.wait("vector", "sX"); p.wait("vector", "sW")
                for g in range(8):
                    for tt in range(n_toks // 512):
                        pb = ps_ab[tt % 2]
                        p.bw("tensor", "Aqr" + tag, tt % 2)
                        for kc in range(4):
                            p.inc("tensor", lambda e, b=pb, k=kc, g_=g, t_=tt: e.matmul(
                                b[:, 0:512], w1[:, k * 1024 + g_ * 128:k * 1024 + (g_ + 1) * 128],
                                csrc[:, k * c_ntok + t_ * 512:k * c_ntok + (t_ + 1) * 512],
                                start=(k == 0), stop=(k == 3)), "sPE", 1)
                        for kc in range(4):
                            p.inc("tensor", lambda e, b=pb, k=kc, g_=g, t_=tt: e.matmul(
                                b[:, 512:1024], w2[:, k * 1024 + g_ * 128:k * 1024 + (g_ + 1) * 128],
                                csrc[:, k * c_ntok + t_ * 512:k * c_ntok + (t_ + 1) * 512],
                                start=(k == 0), stop=(k == 3)), "sPE", 1)
                        p.wait("vector", "sPE")
                        p.wait("vector", "sSC")
                        p.inc("vector", lambda e, b=pb, t_=tt: e.tensor_mul(
                            t1s[:, :], b[:, 0:512], cs_t[:, t_ * 512:(t_ + 1) * 512]), "sV", 1)
                        if hbm_dst is None:
                            _mt = dst[:, g * n_toks + tt * 512:g * n_toks + (tt + 1) * 512]
                        else:
                            _mt = qdst[tt % 2][:, :]
                            p.wait("vector", "sST", max(0, p.sems["sST"].n - 16))
                        p.inc("vector", lambda e, b=pb, mt=_mt, t_=tt: e.tensor_mul(
                            mt, b[:, 512:1024], ss_t[:, t_ * 512:(t_ + 1) * 512]), "sV", 1)
                        if hbm_dst is None:
                            dtile = dst[:, g * n_toks + tt * 512:g * n_toks + (tt + 1) * 512]
                        else:
                            dtile = qdst[tt % 2][:, :]
                            p.wait("vector", "sST", max(0, p.sems["sST"].n - 16))
                        p.inc("vector", lambda e, dt_=dtile: e.tensor_add(
                            dt_, dt_, t1s[:, :]), "sV", 1)
                        p.bd("Aqr" + tag, tt % 2)
                        if hbm_dst is not None:
                            p.wait("sync", "sV")
                            p.inc("sync", lambda e, dt_=dtile, g_=g, t_=tt: e.dma_start(
                                out=hbm_dst[:, g_ * n_toks + t_ * 512:g_ * n_toks + (t_ + 1) * 512],
                                in_=dt_), "sST", 16)
                        p.wait("scalar", "sV")
                        p.inc("scalar", lambda e, dt_=dtile: e.activation(
                            qsq[:, :], dt_, AF.Square), "sSC", 1)
                        p.wait("tensor", "sSC")
                        p.inc("tensor", lambda e: e.matmul(ps_n[0:1, 0:512], inda[:, 0:1], qsq[:, :],
                                                           start=True, stop=True), "sPE", 1)
                        p.inc("tensor", lambda e: e.matmul(ps_n[0:1, 0:512], indb[:, 0:1], qsq[:, :],
                                                           start=False, stop=True,
                                                           skip_group_check=True), "sPE", 1)
                        p.wait("scalar", "sPE")
                        p.inc("scalar", lambda e: e.activation(nrm[0:1, 0:512], ps_n[0:1, 0:512], AF.Sqrt), "sSC", 1)
                        p.wait("vector", "sSC")
                        p.inc("vector", lambda e: e.reduce_max(mxsc[0:1, 0:1], nrm[0:1, 0:512], axis=AX.X), "sV", 1)
                        p.inc("vector", lambda e, c=mxcol_base + g: e.tensor_max(
                            mx[0:1, c:c + 1], mx[0:1, c:c + 1], mxsc[0:1, 0:1]), "sV", 1)
                        p.wait("tensor", "sV")
                p.flush()

        if phases >= 3:
            qk_rope(kall, S, ckv, S, wuk_h, wuk2_h, kcs_h, kss_h, 16, "k")
            qk_rope(qall, nq, cq, nq, wuq_h, wuq2_h, qcs_h, qss_h, 0, "q",
                    hbm_dst=qall_scr if fused else None)
        s_cq.close()

        with ExitStack() as sp:
          if phases >= 3:
            mxbf = sp.enter_context(nc.sbuf_tensor("mxbf_sb", [1, 1], BF16))
            ps_m = sp.enter_context(nc.psum_tensor("ps_m", [128, 8], F32))
            p.inc("vector", lambda e: e.reduce_max(mx[0:1, 32:33], mx[0:1, 0:16], axis=AX.X), "sV", 1)
            p.inc("vector", lambda e: e.reduce_max(mx[0:1, 33:34], mx[0:1, 16:32], axis=AX.X), "sV", 1)
            p.inc("vector", lambda e: e.tensor_mul(mx[0:1, 34:35], mx[0:1, 32:33], mx[0:1, 33:34]), "sV", 1)
            p.inc("vector", lambda e: e.tensor_scalar_mul(mx[0:1, 35:36], mx[0:1, 34:35], -1.0), "sV", 1)
            p.inc("vector", lambda e: e.tensor_copy(mxbf[0:1, 0:1], mx[0:1, 35:36]), "sV", 1)
            p.wait("tensor", "sV")
            p.inc("tensor", lambda e: e.matmul(ps_m[:, 0:1], ones1[0:1, :], mxbf[0:1, 0:1],
                                               start=True, stop=True), "sPE", 1)
            p.wait("vector", "sPE")
            p.inc("vector", lambda e: e.tensor_copy(negm[:, 0:1], ps_m[:, 0:1]), "sV", 1)
          p.flush()

        oacc = s_o.enter_context(nc.sbuf_tensor("oacc_sb", [128, 8 * nq], BF16, side="right"))
        with ExitStack() as sp:
            SB4 = lambda n, sh, dt=BF16: sp.enter_context(nc.sbuf_tensor(n, sh, dt))
            PS4 = lambda n, sh, dt=F32: sp.enter_context(nc.psum_tensor(n, sh, dt))
            wuv = SB4("wuv_sb", [128, 4 * 1024])
            vg = SB4("vg_sb", [128, NKC * 130])
            eab = [SB4(f"eab{i}", [128, 1024]) for i in range(2)]
            rsc = SB4("rsc_sb", [1, 1024], F32)
            rbh = SB4("rbh_sb", [1, 1024])
            rlf = SB4("rlf_sb", [1, 1024], F32)
            rbl = SB4("rbl_sb", [1, 1024])
            rbs = SB4("rbs_sb", [128, 512], F32)
            dcp = SB4("dcp_sb", [1, 1024], F32)
            qtl = [SB4(f"qtl{i}", [128, 512]) for i in range(2)]
            ps_ab = [PS4(f"ps_s{i}", [128, 1024]) for i in range(2)]
            ps_c4 = [PS4(f"ps_v{i}", [128, 512]) for i in range(2)]
            ps_oa = [PS4("ps_oa", [128, 512]), ps_c4[0]]
            ps_ob = [PS4("ps_ob", [128, 512]), ps_c4[1]]
            for kc in range(4):
                ld_cast(wuv[:, kc * 1024:(kc + 1) * 1024], wuv_h[kc * 128:(kc + 1) * 128, :])
            p.wait("tensor", "sW")
            exp_tail = [0, 0]
            omm_tail = [0, 0]
            oev_tail = [0, 0]
            qS_tail = [0, 0]
            qx_cap = {}
            vb_tail = [0]

            def pf_q(tg):
                g_, t_ = divmod(tg, NQT)
                d = qtl[tg % 2]
                p.wait("sync", "sPE", qS_tail[tg % 2])
                p.inc("sync", lambda e, g2=g_, t2=t_, d2=d: e.dma_start(
                    out=d2[:, :], in_=qall_scr[:, g2 * nq + t2 * 512:g2 * nq + (t2 + 1) * 512]), "sX", 16)
                qx_cap[tg] = p.sems["sX"].n

            if fused and phases >= 4:
                pf_q(0)
            for g in range(8 if phases >= 4 else 0):
                p.wait("vector", "sPE")
                p.inc("vector", lambda e: e.memset(
                    vg[:, :].rearrange("p (t h c) -> p t h c", h=2, c=65)[:, :, :, 64:65], 1.0), "sV", 1)
                p.wait("tensor", "sV", oev_tail[1])
                for tc in range(NKC):
                    pb = ps_c4[tc % 2]
                    p.bw("tensor", "Av", tc % 2)
                    for kc in range(4):
                        p.inc("tensor", lambda e, b=pb, k=kc, t_=tc, g_=g: e.matmul(
                            b[:, 0:128], ckv[:, k * S + t_ * 128:k * S + (t_ + 1) * 128],
                            wuv[:, k * 1024 + g_ * 128:k * 1024 + (g_ + 1) * 128],
                            start=(k == 0), stop=(k == 3)), "sPE", 1)
                    p.wait("vector", "sPE")
                    p.inc("vector", lambda e, b=pb, t_=tc: e.tensor_copy(
                        vg[:, t_ * 130:(t_ + 1) * 130].rearrange("p (h c) -> p h c", c=65)[:, :, 0:64],
                        b[:, 0:128].rearrange("p (h c) -> p h c", c=64)), "sV", 1)
                    p.bd("Av", tc % 2)
                vb_tail[0] = p.sems["sV"].n
                for tt in range(NQT):
                    tg = g * NQT + tt
                    if fused:
                        qsrc = qtl[tg % 2]
                        p.wait("tensor", "sX", qx_cap[tg])
                        qs0 = 0
                    else:
                        qsrc = qall
                        qs0 = g * nq + tt * 512
                    poa = ps_oa[tg % 2]; pob = ps_ob[tg % 2]
                    s_done = [0] * NKC
                    def emit_S(kt):
                        pb = ps_ab[kt % 2]
                        p.wait("tensor", "sSC", exp_tail[kt % 2])
                        p.inc("tensor", lambda e, b=pb, g_=g, k=kt, qv=qsrc, q0_=qs0: e.matmul(
                            b[:, 0:512], kall[0:64, g_ * S + k * 128:g_ * S + (k + 1) * 128],
                            qv[0:64, q0_:q0_ + 512],
                            start=True, stop=True, tile_position=(0, 0)), "sPE", 1)
                        p.inc("tensor", lambda e, b=pb, g_=g, k=kt, qv=qsrc, q0_=qs0: e.matmul(
                            b[:, 512:1024], kall[64:128, g_ * S + k * 128:g_ * S + (k + 1) * 128],
                            qv[64:128, q0_:q0_ + 512],
                            start=True, stop=True, tile_position=(64, 0)), "sPE", 1)
                        s_done[kt] = p.sems["sPE"].n
                    def emit_exp(kt):
                        pb = ps_ab[kt % 2]
                        p.wait("scalar", "sPE", s_done[kt])
                        p.wait("scalar", "sPE", omm_tail[kt % 2])
                        p.inc("scalar", lambda e, b=pb, i=kt % 2: e.activation(
                            eab[i][:, :], b[:, :], AF.Exp, bias=negm[:, 0:1]), "sSC", 1)
                        exp_tail[kt % 2] = p.sems["sSC"].n
                    def emit_O(kt):
                        p.wait("tensor", "sSC", exp_tail[kt % 2] if kt == NKC - 1 or True else 0)
                        if kt == 0:
                            p.wait("tensor", "sV", oev_tail[tg % 2])
                            if tg % 2 == 1:
                                p.wait("tensor", "sV", vb_tail[0])
                        p.inc("tensor", lambda e, k=kt, i=kt % 2, po=poa: e.matmul(
                            po[0:65, 0:512], vg[:, k * 130:k * 130 + 65], eab[i][:, 0:512],
                            start=(k == 0), stop=(k == NKC - 1)), "sPE", 1)
                        p.inc("tensor", lambda e, k=kt, i=kt % 2, po=pob: e.matmul(
                            po[0:65, 0:512], vg[:, k * 130 + 65:(k + 1) * 130], eab[i][:, 512:1024],
                            start=(k == 0), stop=(k == NKC - 1)), "sPE", 1)
                        omm_tail[kt % 2] = p.sems["sPE"].n
                    exp_save = {}
                    emit_S(0); emit_exp(0); exp_save[0] = exp_tail[0]
                    for kt in range(1, NKC):
                        emit_S(kt); emit_exp(kt); exp_save[kt] = exp_tail[kt % 2]
                        exp_tail_o = exp_tail[(kt - 1) % 2]
                        p.wait("tensor", "sSC", exp_save[kt - 1])
                        emit_O(kt - 1)
                    p.wait("tensor", "sSC", exp_save[NKC - 1])
                    emit_O(NKC - 1)
                    qS_tail[tg % 2] = s_done[NKC - 1]
                    if fused and tg + 1 < 8 * NQT:
                        pf_q(tg + 1)
                    p.wait("vector", "sPE")
                    p.inc("vector", lambda e, po=poa: e.tensor_copy(dcp[0:1, 0:512], po[64:65, 0:512]), "sV", 1)
                    p.inc("vector", lambda e, po=pob: e.tensor_copy(dcp[0:1, 512:1024], po[64:65, 0:512]), "sV", 1)
                    p.inc("vector", lambda e: e.reciprocal(rsc[0:1, :], dcp[0:1, :]), "sV", 1)
                    p.inc("vector", lambda e: e.tensor_copy(rbh[0:1, :], rsc[0:1, :]), "sV", 1)
                    p.inc("vector", lambda e: e.tensor_sub(rlf[0:1, :], rsc[0:1, :], rbh[0:1, :]), "sV", 1)
                    p.inc("vector", lambda e: e.tensor_copy(rbl[0:1, :], rlf[0:1, :]), "sV", 1)
                    for half, (pso, c0) in enumerate(((poa, 0), (pob, 512))):
                        p.wait("tensor", "sV")
                        p.inc("tensor", lambda e, c=c0, po=pso: e.matmul(
                            po[64:128, 0:512], ones1[0:1, 0:64], rbh[0:1, c:c + 512],
                            start=True, stop=False, tile_position=(0, 64),
                            skip_group_check=True), "sPE", 1)
                        p.inc("tensor", lambda e, c=c0, po=pso: e.matmul(
                            po[64:128, 0:512], ones1[0:1, 0:64], rbl[0:1, c:c + 512],
                            start=False, stop=True, tile_position=(0, 64),
                            skip_group_check=True), "sPE", 1)
                        p.wait("vector", "sPE")
                        p.inc("vector", lambda e, po=pso: e.tensor_copy(rbs[0:64, :], po[64:128, 0:512]), "sV", 1)
                        p.inc("vector", lambda e, g_=g, t_=tt, hb=half, po=pso: e.scalar_tensor_tensor(
                            oacc[hb * 64:(hb + 1) * 64, g_ * nq + t_ * 512:g_ * nq + (t_ + 1) * 512],
                            po[0:64, 0:512], 1.0, rbs[0:64, :], ALU.mult, ALU.mult), "sV", 1)
                    oev_tail[tg % 2] = p.sems["sV"].n
            p.flush()
        s_qk.close(); s_ckv.close()

        with ExitStack() as sp:
            SB5 = lambda n, sh, dt=F32: sp.enter_context(nc.sbuf_tensor(n, sh, dt))
            wout = sp.enter_context(nc.sbuf_tensor("wout_sb", [128, 8 * 1024], BF16))
            xqs = [SB5(f"xqs{i}", [128, D]) for i in range(2)]
            xh = SB5("xh_sb", [128, D])
            x1t = [SB5(f"x1t{i}", [128, D]) for i in range(2)]
            ps_c5 = [sp.enter_context(nc.psum_tensor(f"ps_w{i}", [128, 512], F32)) for i in range(2)]
            for kc in range(8):
                ld_cast(wout[:, kc * 1024:(kc + 1) * 1024], wout_h[kc * 128:(kc + 1) * 128, :])
            p.wait("tensor", "sW")
            for tc in range(NQC if phases >= 5 else 0):
                xb = xqs[tc % 2]
                ob = x1t[tc % 2]
                p.wait("sync", "sSC")
                p.inc("sync", lambda e, t_=tc, d=xb: e.dma_start(
                    out=d[:, :], in_=xq_h[t_ * 128:(t_ + 1) * 128, :]), "sX", 16)
                p.wait("scalar", "sX")
                p.wait("scalar", "sV")
                p.inc("scalar", lambda e, t_=tc, d=xb: e.activation(
                    xh[:, :], d[:, :], AF.Identity,
                    bias=qstat[:, 2 * t_ + 1:2 * t_ + 2], scale=qstat[:, 2 * t_:2 * t_ + 1]), "sSC", 1)
                for dt_ in range(2):
                    pb = ps_c5[dt_]
                    p.bw("tensor", "Aw", dt_)
                    p.wait("tensor", "sSC")
                    for kc in range(8):
                        p.inc("tensor", lambda e, b=pb, k=kc, t_=tc, d_=dt_: e.matmul(
                            b[:, 0:512], oacc[:, k * nq + t_ * 128:k * nq + (t_ + 1) * 128],
                            wout[:, k * 1024 + d_ * 512:k * 1024 + (d_ + 1) * 512],
                            start=(k == 0), stop=(k == 7)), "sPE", 1)
                    p.wait("vector", "sPE")
                    p.wait("vector", "sSC")
                    p.wait("vector", "sST", max(0, p.sems["sST"].n - 16))
                    p.inc("vector", lambda e, b=pb, d_=dt_, o=ob: e.tensor_add(
                        o[:, d_ * 512:(d_ + 1) * 512], b[:, 0:512], xh[:, d_ * 512:(d_ + 1) * 512]), "sV", 1)
                    p.bd("Aw", dt_)
                p.wait("sync", "sV")
                p.inc("sync", lambda e, t_=tc, o=ob: e.dma_start(
                    out=x1s_h[t_ * 128:(t_ + 1) * 128, :], in_=o[:, :]), "sST", 16)
            p.flush()
        s_o.close()

        with ExitStack() as sf:
            SBF = lambda n, sh, dt=BF16: sf.enter_context(nc.sbuf_tensor(n, sh, dt))
            hf = SBF("hf_sb", [128, 8 * 2048])
            act = SBF("act_sb", [128, 24 * 2048])
            wgu = [SBF(f"wgu{i}", [128, 2 * 1024]) for i in range(2)]

            for pass0 in range(0, nq if phases >= 6 else 0, 2048):
                ntok = min(2048, nq - pass0)
                NTT = ntok // 512
                NTC = ntok // 128
                with ExitStack() as sp:
                    SBP = lambda n, sh, dt=F32: sp.enter_context(nc.sbuf_tensor(n + str(pass0), sh, dt))
                    xsl = [SBP(f"fxsl{i}", [128, D]) for i in range(2)]
                    zsl = sp.enter_context(nc.sbuf_tensor(f"fzsl_sb{pass0}", [128, D], BF16))
                    stat = SBP("fstat_sb", [128, 8])
                    q2s = sp.enter_context(nc.sbuf_tensor(f"fq2s_sb{pass0}", [128, 1024], BF16))
                    ps_tf = [sp.enter_context(nc.psum_tensor(f"ps_tf{i}_{pass0}", [128, 512], BF16)) for i in range(2)]
                    for sl in range(NTC if phases >= 6 else 0):
                        xb = xsl[sl % 2]
                        p.wait("sync", "sSC")
                        p.inc("sync", lambda e, s_=sl, d=xb: e.dma_start(
                            out=d[:, :],
                            in_=x1s_h[(pass0 // 128 + s_) * 128:(pass0 // 128 + s_ + 1) * 128, :]), "sX", 16)
                        p.wait("scalar", "sX")
                        p.wait("scalar", "sV")
                        p.inc("scalar", lambda e, d=xb: e.activation(q2s[:, 0:1024], d[:, :], AF.Square,
                                                                     accum_out=stat[:, 1:2]), "sSC", 1)
                        p.wait("vector", "sSC")
                        p.inc("vector", lambda e, d=xb: e.reduce_sum(stat[:, 0:1], d[:, :], axis=AX.X), "sV", 1)
                        p.inc("vector", lambda e: e.tensor_scalar_mul(stat[:, 2:3], stat[:, 0:1], 1.0 / D), "sV", 1)
                        p.inc("vector", lambda e: e.tensor_mul(stat[:, 3:4], stat[:, 2:3], stat[:, 2:3]), "sV", 1)
                        p.inc("vector", lambda e: e.scalar_tensor_tensor(stat[:, 4:5], stat[:, 1:2], 1.0 / D,
                                                                         stat[:, 3:4], ALU.mult, ALU.subtract), "sV", 1)
                        p.wait("scalar", "sV")
                        p.inc("scalar", lambda e: e.activation(stat[:, 5:6], stat[:, 4:5], AF.Sqrt, bias=epsb[:, 0:1]), "sSC", 1)
                        p.wait("vector", "sSC")
                        p.inc("vector", lambda e: e.reciprocal(stat[:, 4:5], stat[:, 5:6]), "sV", 1)
                        p.inc("vector", lambda e: e.tensor_scalar(stat[:, 5:6], stat[:, 2:3], stat[:, 4:5], -1.0,
                                                                  ALU.mult, ALU.mult), "sV", 1)
                        p.wait("scalar", "sV")
                        p.wait("scalar", "sPE")
                        p.inc("scalar", lambda e, d=xb: e.activation(zsl[:, :], d[:, :], AF.Identity,
                                                                     bias=stat[:, 5:6], scale=stat[:, 4:5]), "sSC", 1)
                        p.wait("tensor", "sSC")
                        for kc in range(8):
                            pt = ps_tf[kc % 2]
                            p.wait("tensor", "sV")
                            p.inc("tensor", lambda e, k=kc, b=pt: e.transpose(
                                b[:, 0:128], zsl[:, k * 128:(k + 1) * 128],
                                ident[:, :]), "sPE", 1)
                            p.wait("vector", "sPE")
                            p.inc("vector", lambda e, k=kc, b=pt, s_=sl: e.tensor_copy(
                                hf[:, k * 2048 + s_ * 128:k * 2048 + (s_ + 1) * 128],
                                b[:, 0:128]), "sV", 1)
                    p.flush()

                with ExitStack() as sp:
                    atile = sp.enter_context(nc.sbuf_tensor(f"atile_sb{pass0}", [128, 512], BF16))
                    btile = sp.enter_context(nc.sbuf_tensor(f"btile_sb{pass0}", [128, 512], BF16))
                    ps_gu = [sp.enter_context(nc.psum_tensor(f"ps_gu{i}_{pass0}", [128, 1024], F32)) for i in range(2)]
                    for fc in range(24 if phases >= 7 else 0):
                        wb = wgu[fc % 2]
                        p.bw("gpsimd", "Agw", fc % 2, ("sPE",))
                        p.inc("gpsimd", lambda e, f_=fc, d=wb: e.dma_start(
                            out=d[:, 0:1024].rearrange("p (kc f) -> p kc f", f=128),
                            in_=wg_h[:, f_ * 128:(f_ + 1) * 128].rearrange("(kc pp) f -> pp kc f", pp=128)), "sW", 16)
                        p.inc("gpsimd", lambda e, f_=fc, d=wb: e.dma_start(
                            out=d[:, 1024:2048].rearrange("p (kc f) -> p kc f", f=128),
                            in_=wu_h[:, f_ * 128:(f_ + 1) * 128].rearrange("(kc pp) f -> pp kc f", pp=128)), "sW", 16)
                        p.wait("tensor", "sW")
                        for tt in range(NTT):
                            pb = ps_gu[tt % 2]
                            p.bw("tensor", "Agu", tt % 2, ("sV", "sSC"))
                            for kc in range(8):
                                p.inc("tensor", lambda e, b=pb, k=kc, d=wb, t_=tt: e.matmul(
                                    b[:, 0:512], d[:, k * 128:(k + 1) * 128],
                                    hf[:, k * 2048 + t_ * 512:k * 2048 + (t_ + 1) * 512],
                                    start=(k == 0), stop=(k == 7)), "sPE", 1)
                            for kc in range(8):
                                p.inc("tensor", lambda e, b=pb, k=kc, d=wb, t_=tt: e.matmul(
                                    b[:, 512:1024], d[:, 1024 + k * 128:1024 + (k + 1) * 128],
                                    hf[:, k * 2048 + t_ * 512:k * 2048 + (t_ + 1) * 512],
                                    start=(k == 0), stop=(k == 7)), "sPE", 1)
                            p.wait("scalar", "sPE")
                            p.wait("scalar", "sV")
                            p.inc("scalar", lambda e, b=pb: e.activation(
                                atile[:, :], b[:, 0:512], AF.Silu, scale=1.0 / 127.0), "sSC", 1)
                            p.wait("vector", "sPE")
                            p.inc("vector", lambda e, b=pb: e.tensor_scalar(
                                btile[:, :], b[:, 512:1024], 1.0 / 127.0, None, ALU.mult), "sV", 1)
                            p.wait("vector", "sSC")
                            p.inc("vector", lambda e, f_=fc, t_=tt: e.tensor_mul(
                                act[:, f_ * 2048 + t_ * 512:f_ * 2048 + (t_ + 1) * 512],
                                atile[:, :], btile[:, :]), "sV", 1)
                            p.bd("Agu", tt % 2, ("sV", "sSC"))
                        p.bd("Agw", fc % 2, ("sPE",))
                    p.flush()

                with ExitStack() as sp:
                    wdsb = sp.enter_context(nc.sbuf_tensor(f"wd_sb{pass0}", [128, 24 * 1024], BF16))
                    xsl2 = [sp.enter_context(nc.sbuf_tensor(f"cxsl{i}_{pass0}", [128, D], F32)) for i in range(2)]
                    x2t = [sp.enter_context(nc.sbuf_tensor(f"x2t{i}_{pass0}", [128, D], F32)) for i in range(2)]
                    ps_d = [sp.enter_context(nc.psum_tensor(f"ps_d{i}_{pass0}", [128, 512], F32)) for i in range(2)]
                    for fc in range(24):
                        p.inc("gpsimd", lambda e, f_=fc: e.dma_start(
                            out=wdsb[:, f_ * 1024:(f_ + 1) * 1024],
                            in_=wd_h[f_ * 128:(f_ + 1) * 128, :]), "sW", 16)
                    p.wait("tensor", "sW")
                    for tc in range(NTC if phases >= 8 else 0):
                        xb = xsl2[tc % 2]
                        ob = x2t[tc % 2]
                        p.wait("sync", "sV")
                        p.inc("sync", lambda e, t_=tc, d=xb: e.dma_start(
                            out=d[:, :],
                            in_=x1s_h[(pass0 // 128 + t_) * 128:(pass0 // 128 + t_ + 1) * 128, :]), "sX", 16)
                        for dt_ in range(2):
                            pb = ps_d[dt_]
                            p.bw("tensor", "Ad", dt_)
                            for fc in range(24):
                                p.inc("tensor", lambda e, b=pb, f_=fc, t_=tc, d_=dt_: e.matmul(
                                    b[:, 0:512], act[:, f_ * 2048 + t_ * 128:f_ * 2048 + (t_ + 1) * 128],
                                    wdsb[:, f_ * 1024 + d_ * 512:f_ * 1024 + (d_ + 1) * 512],
                                    start=(f_ == 0), stop=(f_ == 23)), "sPE", 1)
                            p.wait("vector", "sPE")
                            p.wait("vector", "sX")
                            p.wait("vector", "sST", max(0, p.sems["sST"].n - 16))
                            p.inc("vector", lambda e, b=pb, d_=dt_, o=ob: e.scalar_tensor_tensor(
                                o[:, d_ * 512:(d_ + 1) * 512], b[:, 0:512], 1.0 / 127.0,
                                xb[:, d_ * 512:(d_ + 1) * 512], ALU.mult, ALU.add), "sV", 1)
                            p.bd("Ad", dt_)
                        p.wait("sync", "sV")
                        p.inc("sync", lambda e, t_=tc, o=ob: e.dma_start(
                            out=x2_h[(pass0 // 128 + t_) * 128:(pass0 // 128 + t_ + 1) * 128, :],
                            in_=o[:, :]), "sST", 16)
                    p.flush()

        if not fused:
            p.wait("sync", "sST")
        p.flush()
    return nc


def _prep_A(inputs, nq=T):
    P1 = l0_qk_perm()
    P2 = np.empty(1024, np.int64)
    for h in range(16):
        P2[h * 64:h * 64 + 32] = [h * 64 + 2 * i + 1 for i in range(32)]
        P2[h * 64 + 32:h * 64 + 64] = [h * 64 + 2 * i for i in range(32)]
    f32 = lambda a: np.ascontiguousarray(a, dtype=np.float32)
    i32 = lambda a: np.ascontiguousarray(a, dtype=np.int32)
    wuq_t = np.asarray(inputs["L0_attn_w_up_q"]).T
    wuk_t = np.asarray(inputs["L0_attn_w_up_k"]).T
    shared = dict(
        wdq_t=f32(np.asarray(inputs["L0_attn_w_down_q"]).T),
        wdkv_t=f32(np.asarray(inputs["L0_attn_w_down_kv"]).T),
        wuq_tp=f32(wuq_t[:, P1]), wuq_tps=f32(wuq_t[:, P2]),
        wuk_tp=f32(wuk_t[:, P1]), wuk_tps=f32(wuk_t[:, P2]),
        wuv_t=f32(np.asarray(inputs["L0_attn_w_up_v"]).T),
        wout_t=f32(np.asarray(inputs["L0_attn_w_out"]).T),
        gamma8=f32(np.asarray(inputs["L0_attn_ln_scale"]).reshape(8, 128).T),
        beta8=f32(np.asarray(inputs["L0_attn_ln_bias"]).reshape(8, 128).T),
        ind_a=f32(np.r_[np.ones(64), np.zeros(64)].reshape(128, 1)),
        ind_b=f32(np.r_[np.zeros(64), np.ones(64)].reshape(128, 1)),
        wg_t=i32(np.asarray(inputs["L0_ffn_w_gate"]).T),
        wu_t=i32(np.asarray(inputs["L0_ffn_w_up"]).T),
        wd_t=i32(np.asarray(inputs["L0_ffn_w_down"]).T),
    )
    kcs, kss = l0_rope_tiles(np.arange(S), 1.0)
    x_full = np.ascontiguousarray(np.asarray(inputs["L0_attn_x"], dtype=np.float32))
    in_maps = []
    for c in range(8):
        b, hh = c // 2, c % 2
        if nq == T:
            qpos = np.arange(hh * T, (hh + 1) * T)
            xq = x_full[b, hh * T:(hh + 1) * T]
        else:
            qpos = np.arange(S)
            xq = x_full[b]
        qcs, qss = l0_rope_tiles(qpos, 1.0 / 8.0)
        m = dict(shared)
        m.update(x=x_full[b], xq=np.ascontiguousarray(xq),
                 qcs=qcs, qss=qss, kcs=kcs, kss=kss)
        in_maps.append(m)
    return in_maps


def build_B(ctx=None):
    fused = ctx is not None
    if fused:
        nc = ctx[0]
    else:
        nc = bass.Bass("TRN2", target_bir_lowering=False)
    ein = lambda n, sh, dt=F32: nc.dram_tensor(n, sh, dt, kind="ExternalInput").ap()
    x2f_h = ctx[3] if fused else ein("x2f", [S, D])
    wqkv_h = ein("wqkvT", [D, 3072])
    bq24_h = ein("bq24", [128, 24])
    bqrow_h = ein("bq_row", [1, 3072])
    proj_h = ein("proj_dup", [128, 1024])
    tri_h = ein("mask_tri", [128, 128])
    woutT_h = ein("l1a_woutT", [D, D])
    bout_h = ein("l1a_bout", [1, D])
    if fused:
        x3_h = ctx[5]
        ctx_scr = nc.dram_tensor("bctx_scr", [128, 8 * S], BF16,
                                 kind="ExternalOutput" if ctx[6] else "Internal").ap()
        NOWN = S
        FIRST_OWN = 0
    else:
        x3_h = nc.dram_tensor("x3", [T, D], F32, kind="ExternalOutput").ap()
        ctx_scr = None
        NOWN = T
        FIRST_OWN = 16

    NTC = S // 128
    DELTA = 1e-6

    with ExitStack() as gst:
        if fused:
            p = ctx[1]
            g = ctx[4]
        else:
            p = Prog(nc, gst)
            g = mk_globals(nc, p, gst)
        ident = g["ident"]; ones1 = g["ones1"]; epsb = g["epsb"]
        SBg = lambda n, sh, dt=BF16: gst.enter_context(nc.sbuf_tensor(n, sh, dt))
        tri = SBg("tri_sb", [128, 128], F32)
        proj = SBg("proj_sb", [128, 1024])
        bq24 = SBg("bq24_sb", [128, 24], F32)
        bqrow = SBg("bqrow_sb", [1, 3072])
        p.inc("gpsimd", lambda e: e.dma_start(out=proj[:, :], in_=proj_h[:, :]), "sW", 16)
        p.inc("gpsimd", lambda e: e.dma_start(out=bqrow[:, :], in_=bqrow_h[:, :]), "sW", 16)
        p.inc("sync", lambda e: e.dma_start(out=tri[:, :], in_=tri_h[:, :]), "sX", 16)
        p.inc("sync", lambda e: e.dma_start(out=bq24[:, :], in_=bq24_h[:, :]), "sX", 16)
        p.flush()

        s_y = ExitStack()
        s_ctx = ExitStack()
        y = s_y.enter_context(nc.sbuf_tensor("y_sb", [128, 8 * S], BF16))
        if fused and ctx[6]:
            ydbg_h = nc.dram_tensor("by_dbg", [128, 8 * S], BF16, kind="ExternalOutput").ap()
        else:
            ydbg_h = None
        if fused:
            ctxf = None
        else:
            ctxf = s_ctx.enter_context(nc.sbuf_tensor("ctxf_sb", [128, 8 * T], BF16, side="right"))

        with ExitStack() as sp:
            SBP = lambda n, sh, dt=F32: sp.enter_context(nc.sbuf_tensor(n, sh, dt))
            xsl = [SBP(f"bxsl{i}", [128, D]) for i in range(2)]
            zsl = sp.enter_context(nc.sbuf_tensor("bzsl_sb", [128, D], BF16))
            stat = SBP("bstat_sb", [128, 8])
            q2s = sp.enter_context(nc.sbuf_tensor("bq2s_sb", [128, 1024], BF16))
            ps_t = [sp.enter_context(nc.psum_tensor(f"bps_t{i}", [128, 512], BF16)) for i in range(2)]
            for sl in range(NTC):
                xb = xsl[sl % 2]
                p.wait("sync", "sSC")
                p.inc("sync", lambda e, s_=sl, d=xb: e.dma_start(
                    out=d[:, :], in_=x2f_h[s_ * 128:(s_ + 1) * 128, :]), "sX", 16)
                p.wait("scalar", "sX")
                p.wait("scalar", "sV")
                p.inc("scalar", lambda e, d=xb: e.activation(q2s[:, 0:1024], d[:, :], AF.Square,
                                                             accum_out=stat[:, 1:2]), "sSC", 1)
                p.wait("vector", "sSC")
                p.inc("vector", lambda e, d=xb: e.reduce_sum(stat[:, 0:1], d[:, :], axis=AX.X), "sV", 1)
                p.inc("vector", lambda e: e.tensor_scalar_mul(stat[:, 2:3], stat[:, 0:1], 1.0 / D), "sV", 1)
                p.inc("vector", lambda e: e.tensor_mul(stat[:, 3:4], stat[:, 2:3], stat[:, 2:3]), "sV", 1)
                p.inc("vector", lambda e: e.scalar_tensor_tensor(stat[:, 4:5], stat[:, 1:2], 1.0 / D,
                                                                 stat[:, 3:4], ALU.mult, ALU.subtract), "sV", 1)
                p.wait("scalar", "sV")
                p.inc("scalar", lambda e: e.activation(stat[:, 5:6], stat[:, 4:5], AF.Sqrt, bias=epsb[:, 0:1]), "sSC", 1)
                p.wait("vector", "sSC")
                p.inc("vector", lambda e: e.reciprocal(stat[:, 4:5], stat[:, 5:6]), "sV", 1)
                p.inc("vector", lambda e: e.tensor_scalar(stat[:, 5:6], stat[:, 2:3], stat[:, 4:5], -1.0,
                                                          ALU.mult, ALU.mult), "sV", 1)
                p.wait("scalar", "sV")
                p.wait("scalar", "sPE")
                p.inc("scalar", lambda e, d=xb: e.activation(zsl[:, :], d[:, :], AF.Identity,
                                                             bias=stat[:, 5:6], scale=stat[:, 4:5]), "sSC", 1)
                p.wait("tensor", "sSC")
                for kc in range(8):
                    pt = ps_t[kc % 2]
                    p.wait("tensor", "sV")
                    p.inc("tensor", lambda e, k=kc, b=pt: e.transpose(
                        b[:, 0:128], zsl[:, k * 128:(k + 1) * 128], ident[:, :]), "sPE", 1)
                    p.wait("vector", "sPE")
                    p.inc("vector", lambda e, k=kc, b=pt, s_=sl: e.tensor_copy(
                        y[:, k * S + s_ * 128:k * S + (s_ + 1) * 128], b[:, 0:128]), "sV", 1)
            if ydbg_h is not None:
                p.wait("sync", "sV")
                p.inc("sync", lambda e: e.dma_start(out=ydbg_h[:, :], in_=y[:, :]), "sST", 16)
            p.flush()

        for g4 in range(4):
            with ExitStack() as sp:
                tg = str(g4)
                SB2 = lambda n, sh, dt=BF16: sp.enter_context(nc.sbuf_tensor(n + tg, sh, dt))
                PS2 = lambda n, sh, dt=F32: sp.enter_context(nc.psum_tensor(n + tg, sh, dt))
                wq = SB2("wq_sb", [128, 8 * 256]); wk = SB2("wk_sb", [128, 8 * 256]); wv = SB2("wv_sb", [128, 8 * 256])
                qf = SB2("qf_sb", [128, 2 * NOWN])
                kf = SB2("kf_sb", [128, 2 * S])
                vt = SB2("vt_sb", [128, NTC * 260])
                qp = SB2("qp_sb", [128, 2 * NOWN])
                kpf = SB2("kpf_sb", [128, 2 * S])
                kpt = SB2("kpt_sb", [128, NTC * 256])
                stf = SB2("stf_sb", [128, 2 * 65], F32)
                stb = SB2("stb_sb", [128, 2 * 65])
                awm = SB2("awm_sb", [128, 2 * 128])
                ctm = SB2("ctm_sb", [128, 128])
                cto = SB2("cto_sb", [128, 128])
                rnv = SB2("rnv_sb", [128, 2], F32)
                ps_a = [PS2(f"ps_a{i}", [128, 512]) for i in range(2)]
                ps_ct = [PS2(f"ps_ct{i}", [128, 512]) for i in range(2)]
                ps_cb = [PS2(f"ps_cb{i}", [128, 512]) for i in range(2)]
                ps_tr = [PS2(f"ps_tr{i}", [128, 1024], BF16) for i in range(2)]

                for kc in range(8):
                    for wsb, c0 in ((wq, g4 * 256), (wk, 1024 + g4 * 256), (wv, 2048 + g4 * 256)):
                        p.inc("gpsimd", lambda e, d=wsb, k=kc, c=c0: e.dma_start(
                            out=d[:, k * 256:(k + 1) * 256],
                            in_=wqkv_h[k * 128:(k + 1) * 128, c:c + 256]), "sW", 16)
                p.wait("tensor", "sW"); p.wait("scalar", "sX")

                def qk_proj(dst, n_toks, wsb, bias_base, tok0):
                    for j in range(2):
                        for tt in range(n_toks // 512):
                            pb = ps_a[tt % 2]
                            p.bw("tensor", "Bqk", tt % 2, ("sSC",))
                            for kc in range(8):
                                p.inc("tensor", lambda e, b=pb, k=kc, j_=j, t_=tt: e.matmul(
                                    b[:, 0:512], wsb[:, k * 256 + j_ * 128:k * 256 + (j_ + 1) * 128],
                                    y[:, k * S + tok0 + t_ * 512:k * S + tok0 + (t_ + 1) * 512],
                                    start=(k == 0), stop=(k == 7)), "sPE", 1)
                            p.wait("scalar", "sPE")
                            p.inc("scalar", lambda e, b=pb, j_=j, t_=tt: e.activation(
                                dst[:, j_ * n_toks + t_ * 512:j_ * n_toks + (t_ + 1) * 512],
                                b[:, 0:512], AF.Identity,
                                bias=bq24[:, bias_base + g4 * 2 + j_:bias_base + g4 * 2 + j_ + 1]), "sSC", 1)
                            p.bd("Bqk", tt % 2, ("sSC",))
                qk_proj(qf, NOWN, wq, 0, FIRST_OWN * 128)
                qk_proj(kf, S, wk, 8, 0)

                p.wait("vector", "sPE")
                p.inc("vector", lambda e: e.memset(
                    vt[:, :].rearrange("p (t h c) -> p t h c", h=4, c=65)[:, :, :, 64:65], 1.0), "sV", 1)
                for tc in range(NTC):
                    pb = ps_cb[tc % 2]
                    p.bw("tensor", "Bv", tc % 2)
                    for kc in range(8):
                        p.inc("tensor", lambda e, b=pb, k=kc, t_=tc: e.matmul(
                            b[:, 0:256], y[:, k * S + t_ * 128:k * S + (t_ + 1) * 128],
                            wv[:, k * 256:(k + 1) * 256],
                            start=(k == 0), stop=False), "sPE", 1)
                    p.inc("tensor", lambda e, b=pb: e.matmul(
                        b[:, 0:256], ones1[0:1, 0:128], bqrow[0:1, 2048 + g4 * 256:2048 + (g4 + 1) * 256],
                        start=False, stop=True), "sPE", 1)
                    p.wait("vector", "sPE")
                    p.inc("vector", lambda e, b=pb, t_=tc: e.tensor_copy(
                        vt[:, t_ * 260:(t_ + 1) * 260].rearrange("p (h c) -> p h c", c=65)[:, :, 0:64],
                        b[:, 0:256].rearrange("p (h c) -> p h c", c=64)), "sV", 1)
                    p.bd("Bv", tc % 2)

                def p_proj(dst, srcf, n_toks):
                    for j in range(2):
                        he = (4 * g4 + 2 * j) * 64
                        ho = (4 * g4 + 2 * j + 1) * 64
                        for tt in range(n_toks // 512):
                            pa = ps_a[0]; pb2 = ps_a[1]
                            p.wait("tensor", "sSC")
                            p.inc("tensor", lambda e, j_=j, t_=tt, c=he: e.matmul(
                                pa[0:64, 0:512], proj[0:64, c:c + 64],
                                srcf[0:64, j_ * n_toks + t_ * 512:j_ * n_toks + (t_ + 1) * 512],
                                start=True, stop=True, tile_position=(0, 0)), "sPE", 1)
                            p.inc("tensor", lambda e, j_=j, t_=tt, c=ho: e.matmul(
                                pb2[0:64, 0:512], proj[64:128, c:c + 64],
                                srcf[64:128, j_ * n_toks + t_ * 512:j_ * n_toks + (t_ + 1) * 512],
                                start=True, stop=True, tile_position=(64, 0)), "sPE", 1)
                            p.wait("scalar", "sPE")
                            p.inc("scalar", lambda e, j_=j, t_=tt: e.activation(
                                dst[0:64, j_ * n_toks + t_ * 512:j_ * n_toks + (t_ + 1) * 512],
                                pa[0:64, 0:512], AF.Relu), "sSC", 1)
                            p.inc("scalar", lambda e, j_=j, t_=tt: e.activation(
                                dst[64:128, j_ * n_toks + t_ * 512:j_ * n_toks + (t_ + 1) * 512],
                                pb2[0:64, 0:512], AF.Relu), "sSC", 1)
                p_proj(qp, qf, NOWN)
                p_proj(kpf, kf, S)
                if fused and ctx[6] and g4 == 2:
                    dap2 = nc.dram_tensor("bqp_early", [128, 2 * NOWN], BF16, kind="ExternalOutput").ap()
                    p.wait("sync", "sSC")
                    p.inc("sync", lambda e, d=dap2: e.dma_start(out=d[:, :], in_=qp[:, :]), "sST", 16)

                for tc in range(NTC):
                    for h4 in range(4):
                        pb = ps_cb[h4 % 2]
                        hb = (h4 % 2) * 64
                        jj = h4 // 2
                        p.bw("tensor", "Bkpt", h4 % 2, ("sSC",))
                        p.inc("tensor", lambda e, t_=tc, h_=h4, hb_=hb, j_=jj, b=pb: e.matmul(
                            b[:, 0:64], kf[hb_:hb_ + 64, j_ * S + t_ * 128:j_ * S + (t_ + 1) * 128],
                            proj[hb_:hb_ + 64, (4 * g4 + h_) * 64:(4 * g4 + h_ + 1) * 64],
                            start=True, stop=True, tile_position=(hb_, 0)), "sPE", 1)
                        p.wait("scalar", "sPE")
                        p.inc("scalar", lambda e, t_=tc, h_=h4, b=pb: e.activation(
                            kpt[:, t_ * 256 + h_ * 64:t_ * 256 + (h_ + 1) * 64],
                            b[:, 0:64], AF.Relu), "sSC", 1)
                        p.bd("Bkpt", h4 % 2, ("sSC",))

                p.wait("vector", "sSC")
                p.inc("vector", lambda e: e.memset(stf[:, :], 0.0), "sV", 1)
                p.inc("vector", lambda e: e.memset(stb[:, :], 0.0), "sV", 1)
                for c in range(NTC):
                    own = c >= FIRST_OWN
                    if own:
                        for j in range(2):
                            for par in range(2):
                                h4 = 2 * j + par
                                hb = par * 64
                                pb = ps_a[par]
                                p.wait("tensor", "sV")
                                p.inc("tensor", lambda e, h_=h4, hb_=hb, j_=j, c_=c, b=pb: e.matmul(
                                    b[:, 0:128], kpf[hb_:hb_ + 64, j_ * S + c_ * 128:j_ * S + (c_ + 1) * 128],
                                    qp[hb_:hb_ + 64, j_ * NOWN + (c_ - FIRST_OWN) * 128:j_ * NOWN + (c_ - FIRST_OWN + 1) * 128],
                                    start=True, stop=True, tile_position=(hb_, 0)), "sPE", 1)
                            p.wait("vector", "sPE")
                            for par in range(2):
                                p.inc("vector", lambda e, par_=par: e.tensor_mul(
                                    awm[:, par_ * 128:(par_ + 1) * 128], ps_a[par_][:, 0:128], tri[:, :]), "sV", 1)
                            for par in range(2):
                                h4 = 2 * j + par
                                hb = par * 64
                                pc = ps_ct[par]
                                p.wait("tensor", "sV")
                                p.inc("tensor", lambda e, par_=par, h_=h4, c_=c, b=pc: e.matmul(
                                    b[:, 0:65], awm[:, par_ * 128:(par_ + 1) * 128],
                                    vt[:, c_ * 260 + h_ * 65:c_ * 260 + (h_ + 1) * 65],
                                    start=True, stop=False), "sPE", 1)
                                p.inc("tensor", lambda e, par_=par, hb_=hb, j_=j, c_=c, b=pc: e.matmul(
                                    b[:, 0:65], qp[hb_:hb_ + 64, j_ * NOWN + (c_ - FIRST_OWN) * 128:j_ * NOWN + (c_ - FIRST_OWN + 1) * 128],
                                    stb[hb_:hb_ + 64, j_ * 65:(j_ + 1) * 65],
                                    start=False, stop=True, tile_position=(hb_, 0)), "sPE", 1)
                            p.wait("vector", "sPE")
                            for par in range(2):
                                p.inc("vector", lambda e, par_=par, b=ps_ct[par]: e.tensor_scalar_add(
                                    rnv[:, par_:par_ + 1], b[:, 64:65], DELTA), "sV", 1)
                            p.inc("vector", lambda e: e.reciprocal(rnv[:, 0:2], rnv[:, 0:2]), "sV", 1)
                            for par in range(2):
                                p.inc("vector", lambda e, par_=par, b=ps_ct[par]: e.tensor_scalar(
                                    ctm[:, par_ * 64:(par_ + 1) * 64], b[:, 0:64],
                                    rnv[:, par_:par_ + 1], None, ALU.mult), "sV", 1)
                            pt = ps_tr[c % 2]
                            p.wait("tensor", "sV")
                            p.inc("tensor", lambda e, b=pt: e.transpose(
                                b[:, 0:128], ctm[:, :], ident[:, :]), "sPE", 1)
                            p.wait("vector", "sPE")
                            if fused:
                                p.wait("vector", "sST")
                                p.inc("vector", lambda e, b=pt: e.tensor_copy(cto[:, :], b[:, 0:128]), "sV", 1)
                                p.wait("sync", "sV")
                                p.inc("sync", lambda e, j_=j, c_=c: e.dma_start(
                                    out=ctx_scr[:, (2 * g4 + j_) * NOWN + (c_ - FIRST_OWN) * 128:
                                                (2 * g4 + j_) * NOWN + (c_ - FIRST_OWN + 1) * 128],
                                    in_=cto[:, :]), "sST", 16)
                            else:
                                p.inc("vector", lambda e, j_=j, c_=c, b=pt: e.tensor_copy(
                                    ctxf[:, (2 * g4 + j_) * T + (c_ - 16) * 128:(2 * g4 + j_) * T + (c_ - 15) * 128],
                                    b[:, 0:128]), "sV", 1)
                    for h4 in range(4):
                        pb = ps_cb[h4 % 2]
                        hb = (h4 % 2) * 64
                        jj = h4 // 2
                        p.wait("tensor", "sV")
                        p.inc("tensor", lambda e, t_=c, h_=h4, b=pb: e.matmul(
                            b[0:64, 0:65], kpt[:, t_ * 256 + h_ * 64:t_ * 256 + (h_ + 1) * 64],
                            vt[:, t_ * 260 + h_ * 65:t_ * 260 + (h_ + 1) * 65],
                            start=True, stop=True), "sPE", 1)
                        p.wait("vector", "sPE")
                        p.inc("vector", lambda e, hb_=hb, j_=jj, b=pb: e.tensor_add(
                            stf[hb_:hb_ + 64, j_ * 65:(j_ + 1) * 65],
                            stf[hb_:hb_ + 64, j_ * 65:(j_ + 1) * 65], b[0:64, 0:65]), "sV", 1)
                        p.inc("vector", lambda e, hb_=hb, j_=jj: e.tensor_copy(
                            stb[hb_:hb_ + 64, j_ * 65:(j_ + 1) * 65],
                            stf[hb_:hb_ + 64, j_ * 65:(j_ + 1) * 65]), "sV", 1)
                if fused and ctx[6] and g4 == 2:
                    for nm, buf in (("bqp_dbg", qp), ("bkpt_dbg", kpt), ("bvt_dbg", vt), ("bkpf_dbg", kpf), ("bqf_dbg", qf), ("bkf_dbg", kf)):
                        dap = nc.dram_tensor(nm, list(buf.shape), BF16, kind="ExternalOutput").ap()
                        p.wait("sync", "sV")
                        p.inc("sync", lambda e, d=dap, b=buf: e.dma_start(out=d[:, :], in_=b[:, :]), "sST", 16)
                    p.wait("sync", "sST")
                p.flush()
        s_y.close()

        with ExitStack() as sp:
            SB3 = lambda n, sh, dt=F32: sp.enter_context(nc.sbuf_tensor(n, sh, dt))
            wo = sp.enter_context(nc.sbuf_tensor("bwo_sb", [128, 8 * 1024], BF16))
            bo = sp.enter_context(nc.sbuf_tensor("bbo_sb", [1, 1024], BF16))
            xsl2 = [SB3(f"b3xsl{i}", [128, D]) for i in range(2)]
            x3t = [SB3(f"x3t{i}", [128, D]) for i in range(2)]
            ps_w = [sp.enter_context(nc.psum_tensor(f"bps_w{i}", [128, 512], F32)) for i in range(2)]
            ctl = [sp.enter_context(nc.sbuf_tensor(f"bctl{i}", [128, 8 * 128], BF16)) for i in range(2)]
            for kc in range(8):
                p.inc("gpsimd", lambda e, k=kc: e.dma_start(
                    out=wo[:, k * 1024:(k + 1) * 1024], in_=woutT_h[k * 128:(k + 1) * 128, :]), "sW", 16)
            p.inc("gpsimd", lambda e: e.dma_start(out=bo[:, :], in_=bout_h[:, :]), "sW", 16)
            p.wait("tensor", "sW")
            for tc in range(NOWN // 128):
                xb = xsl2[tc % 2]
                ob = x3t[tc % 2]
                p.wait("sync", "sV")
                x2row0 = (FIRST_OWN * 128) + tc * 128
                p.inc("sync", lambda e, r_=x2row0, d=xb: e.dma_start(
                    out=d[:, :], in_=x2f_h[r_:r_ + 128, :]), "sX", 16)
                if fused:
                    cb = ctl[tc % 2]
                    p.wait("sync", "sPE")
                    p.inc("sync", lambda e, t_=tc, d=cb: e.dma_start(
                        out=d[:, :].rearrange("p (kc n) -> p kc n", n=128),
                        in_=ctx_scr[:, :].rearrange("p (kc n) -> p kc n", n=NOWN)[
                            :, :, t_ * 128:(t_ + 1) * 128]), "sX", 16)
                    p.wait("tensor", "sX")
                for dt_ in range(2):
                    pb = ps_w[dt_]
                    p.bw("tensor", "Bwo", dt_)
                    for kc in range(8):
                        if fused:
                            cb = ctl[tc % 2]
                            p.inc("tensor", lambda e, b=pb, k=kc, d_=dt_, c_=cb: e.matmul(
                                b[:, 0:512], c_[:, k * 128:(k + 1) * 128],
                                wo[:, k * 1024 + d_ * 512:k * 1024 + (d_ + 1) * 512],
                                start=(k == 0), stop=False), "sPE", 1)
                        else:
                            p.inc("tensor", lambda e, b=pb, k=kc, t_=tc, d_=dt_: e.matmul(
                                b[:, 0:512], ctxf[:, k * T + t_ * 128:k * T + (t_ + 1) * 128],
                                wo[:, k * 1024 + d_ * 512:k * 1024 + (d_ + 1) * 512],
                                start=(k == 0), stop=False), "sPE", 1)
                    p.inc("tensor", lambda e, b=pb, d_=dt_: e.matmul(
                        b[:, 0:512], ones1[0:1, 0:128], bo[0:1, d_ * 512:(d_ + 1) * 512],
                        start=False, stop=True), "sPE", 1)
                    p.wait("vector", "sPE")
                    p.wait("vector", "sX")
                    p.wait("vector", "sST", max(0, p.sems["sST"].n - 16))
                    p.inc("vector", lambda e, b=pb, d_=dt_, o=ob, d2=xb: e.tensor_add(
                        o[:, d_ * 512:(d_ + 1) * 512], b[:, 0:512], d2[:, d_ * 512:(d_ + 1) * 512]), "sV", 1)
                    p.bd("Bwo", dt_)
                p.wait("sync", "sV")
                p.inc("sync", lambda e, t_=tc, o=ob: e.dma_start(
                    out=x3_h[t_ * 128:(t_ + 1) * 128, :], in_=o[:, :]), "sST", 16)
            p.flush()
        s_ctx.close()
        if not fused:
            p.wait("sync", "sST")
        p.flush()
    return nc


def _prep_B(inputs, x2_parts):
    f32 = lambda a: np.ascontiguousarray(a, dtype=np.float32)
    proj = np.asarray(inputs["L1_attn_proj"], dtype=np.float32)
    proj_dup = np.zeros((128, 1024), np.float32)
    for h in range(16):
        proj_dup[0:64, h * 64:(h + 1) * 64] = proj[h]
        proj_dup[64:128, h * 64:(h + 1) * 64] = proj[h]
    tri = np.tril(np.ones((128, 128), np.float32)).T.copy()
    shared = dict(
        wqkvT=f32(np.asarray(inputs["L1_attn_Wqkv"]).T),
        bq24=f32(np.asarray(inputs["L1_attn_bqkv"]).reshape(24, 128).T),
        bq_row=f32(np.asarray(inputs["L1_attn_bqkv"]).reshape(1, 3072)),
        proj_dup=proj_dup,
        mask_tri=tri,
        l1a_woutT=f32(np.asarray(inputs["L1_attn_Wout"]).T),
        l1a_bout=f32(np.asarray(inputs["L1_attn_bout"]).reshape(1, D)),
    )
    in_maps = []
    for c in range(8):
        b, hh = c // 2, c % 2
        x2f = np.zeros((S, D), np.float32)
        if hh == 1:
            x2f[0:T] = x2_parts[2 * b]
        x2f[T:S] = x2_parts[c]
        m = dict(shared); m.update(x2f=x2f)
        in_maps.append(m)
    return in_maps


EC = 4608
ECP = EC + 1
HID = 2730
NFC = 22

def _fcw(i):
    return 128 if i < 21 else 42

def xpos_tables(pos, is_q):
    n = len(pos)
    half = np.arange(0, 64, 2, dtype=np.float32)
    inv = 1.0 / 10000.0 ** (half / 64)
    t = np.asarray(pos, dtype=np.float32)
    fr = t[:, None] * inv[None, :]
    base = (half + 0.4 * 64) / (1.4 * 64)
    power = (t - S // 2) / float(512 // 2)
    sc = base[None, :] ** power[:, None]
    if not is_q:
        sc = 1.0 / sc
    cos = np.cos(fr) * sc
    sin = np.sin(fr) * sc
    scale = 8.0 if is_q else 1.0
    CS = np.empty((128, n), np.float32)
    SS = np.empty((128, n), np.float32)
    for b0 in (0, 64):
        CS[b0:b0 + 32] = cos.T; CS[b0 + 32:b0 + 64] = cos.T
        SS[b0:b0 + 32] = sin.T; SS[b0 + 32:b0 + 64] = -sin.T
    return (CS * scale).astype(np.float32), (SS * scale).astype(np.float32)


def build_C(ctx=None):
    fused = ctx is not None
    if fused:
        nc = ctx[0]
    else:
        nc = bass.Bass("TRN2", target_bir_lowering=False)
    ein = lambda n, sh, dt=F32: nc.dram_tensor(n, sh, dt, kind="ExternalInput").ap()
    if fused:
        x3scr_h = ctx[5]
        x3e_h = None
        x3own_h = None
    else:
        x3scr_h = None
        x3e_h = ein("x3e", [EC, D])
        x3own_h = ein("x3own", [T, D])
    wqkvT_h = [ein(f"f_wqkvT{l}", [D, 1536]) for l in range(2)]
    woT_h = [ein(f"f_woT{l}", [512, D]) for l in range(2)]
    w1T_h = [ein(f"f_w1T{l}", [D, 2 * HID]) for l in range(2)]
    b1c_h = [ein(f"f_b1c{l}", [128, 2 * NFC]) for l in range(2)]
    w2T_h = [ein(f"f_w2T{l}", [HID, D]) for l in range(2)]
    b2r_h = [ein(f"f_b2r{l}", [1, D]) for l in range(2)]
    lg8_h = [ein(f"f_lg8{l}", [128, 8]) for l in range(2)]
    lb8_h = [ein(f"f_lb8{l}", [128, 8]) for l in range(2)]
    qs2_h = [ein(f"f_qs2{l}", [128, 128]) for l in range(2)]
    ks2_h = [ein(f"f_ks2{l}", [128, 128]) for l in range(2)]
    qsr_h = [ein(f"f_qsr{l}", [1, 64]) for l in range(2)]
    ksr_h = [ein(f"f_ksr{l}", [1, 64]) for l in range(2)]
    qcs_h = [ein(f"f_qcs{l}", [128, EC]) for l in range(2)]
    qss_h = [ein(f"f_qss{l}", [128, EC]) for l in range(2)]
    kcs_h = [ein(f"f_kcs{l}", [128, EC]) for l in range(2)]
    kss_h = [ein(f"f_kss{l}", [128, EC]) for l in range(2)]
    pmk_h = ein("f_pmk", [128, 16])
    trim_h = ein("f_trim", [128, 8 * 512])
    trio_h = ein("f_trio", [128, 8 * 512])
    out_h = nc.dram_tensor("out", [S if fused else T, D], F32, kind="ExternalOutput").ap()
    xf = [nc.dram_tensor(f"c_xf{i}", [ECP, D], F32, kind="Internal").ap() for i in range(2)]
    xb0 = nc.dram_tensor("c_xb0", [128, 8 * ECP], BF16, kind="Internal").ap()

    if fused:
        Q0 = [512, 512]; NQ = [4096, 4096]
        KV0 = [0, 0]; NKV = [4608, 4608]
        F0 = [512, 512]; NF = [4096, 4096]
    else:
        Q0 = [1024, 2048]; NQ = [3584, 2560]
        KV0 = [0, 1536]; NKV = [4608, 3072]
        F0 = [1024, 2560]; NF = [3584, 2048]
    OWN0 = 2560

    with ExitStack() as gst:
        if fused:
            p = ctx[1]
            g = ctx[4]
        else:
            p = Prog(nc, gst)
            g = mk_globals(nc, p, gst)
        ident = g["ident"]; ones1 = g["ones1"]; epsb = g["epsb"]
        SBg = lambda n, sh, dt=BF16: gst.enter_context(nc.sbuf_tensor(n, sh, dt))
        negm = SBg("c_negm_sb", [128, 1], F32)
        negmp = SBg("negmp_sb", [128, 16], F32)
        pmk = SBg("pmk_sb", [128, 16], F32)
        trim = SBg("trim_sb", [128, 8 * 512], F32)
        trio = SBg("trio_sb", [128, 8 * 512], F32)
        mscr = SBg("mscr_sb", [1, 8], F32)
        ind2 = SBg("ind2_sb", [128, 2])
        p.inc("sync", lambda e: e.dma_start(out=trim[:, :], in_=trim_h[:, :]), "sX", 16)
        p.inc("sync", lambda e: e.dma_start(out=trio[:, :], in_=trio_h[:, :]), "sX", 16)
        p.inc("sync", lambda e: e.dma_start(out=pmk[:, :], in_=pmk_h[:, :]), "sX", 16)
        p.inc("vector", lambda e: e.memset(ind2[:, :], 0.0), "sV", 1)
        p.inc("vector", lambda e: e.memset(ind2[0:64, 0:1], 1.0), "sV", 1)
        p.inc("vector", lambda e: e.memset(ind2[64:128, 1:2], 1.0), "sV", 1)
        p.flush()

        with ExitStack() as sp:
            zslab = sp.enter_context(nc.sbuf_tensor("zslab_sb", [128, D], F32))
            zslab_b = sp.enter_context(nc.sbuf_tensor("zslabb_sb", [128, 8], BF16))
            p.inc("vector", lambda e: e.memset(zslab[:, :], 0.0), "sV", 1)
            p.inc("vector", lambda e: e.memset(zslab_b[:, :], 0.0), "sV", 1)
            p.wait("sync", "sV")
            p.inc("sync", lambda e: e.dma_start(out=xf[0][0:1, :], in_=zslab[0:1, :]), "sST", 16)
            p.inc("sync", lambda e: e.dma_start(out=xf[1][0:1, :], in_=zslab[0:1, :]), "sST", 16)
            p.inc("sync", lambda e: e.dma_start(out=xf[1][Q0[0]:Q0[0] + 1, :], in_=zslab[0:1, :]), "sST", 16)
            p.inc("sync", lambda e: e.dma_start(out=xf[1][Q0[1]:Q0[1] + 1, :], in_=zslab[0:1, :]), "sST", 16)
            p.inc("sync", lambda e: e.dma_start(
                out=xb0[:, :].rearrange("p (kc n) -> p kc n", n=ECP)[:, :, 0:1],
                in_=zslab_b[:, :].rearrange("p (kc o) -> p kc o", o=1),
                allow_slow_non_contiguous=True), "sST", 16)
            p.flush()

        def ln_stats_apply(xbuf, stat, q2s, zf):
            p.wait("scalar", "sX")
            p.wait("scalar", "sV")
            p.inc("scalar", lambda e: e.activation(q2s[:, 0:1024], xbuf[:, :], AF.Square,
                                                   accum_out=stat[:, 1:2]), "sSC", 1)
            p.wait("vector", "sSC")
            p.inc("vector", lambda e: e.reduce_sum(stat[:, 0:1], xbuf[:, :], axis=AX.X), "sV", 1)
            p.inc("vector", lambda e: e.tensor_scalar_mul(stat[:, 2:3], stat[:, 0:1], 1.0 / D), "sV", 1)
            p.inc("vector", lambda e: e.tensor_mul(stat[:, 3:4], stat[:, 2:3], stat[:, 2:3]), "sV", 1)
            p.inc("vector", lambda e: e.scalar_tensor_tensor(stat[:, 4:5], stat[:, 1:2], 1.0 / D,
                                                             stat[:, 3:4], ALU.mult, ALU.subtract), "sV", 1)
            p.wait("scalar", "sV")
            p.inc("scalar", lambda e: e.activation(stat[:, 5:6], stat[:, 4:5], AF.Sqrt, bias=epsb[:, 0:1]), "sSC", 1)
            p.wait("vector", "sSC")
            p.inc("vector", lambda e: e.reciprocal(stat[:, 4:5], stat[:, 5:6]), "sV", 1)
            p.inc("vector", lambda e: e.tensor_scalar(stat[:, 5:6], stat[:, 2:3], stat[:, 4:5], -1.0,
                                                      ALU.mult, ALU.mult), "sV", 1)
            p.wait("scalar", "sV")
            p.inc("scalar", lambda e: e.activation(zf[:, :], xbuf[:, :], AF.Identity,
                                                   bias=stat[:, 5:6], scale=stat[:, 4:5]), "sSC", 1)

        def store_master(idx, row0, sbf32, zb, fmsl, ps_tb, shadow):
            p.wait("sync", "sV")
            p.inc("sync", lambda e: e.dma_start(out=xf[idx][row0:row0 + 128, :], in_=sbf32[:, :]), "sST", 16)
            if shadow:
                p.wait("vector", "sST")
                p.inc("vector", lambda e: e.tensor_copy(zb[:, :], sbf32[:, :]), "sV", 1)
                p.wait("tensor", "sV")
                for kc in range(8):
                    pt = ps_tb[kc % 2]
                    p.wait("tensor", "sV")
                    p.inc("tensor", lambda e, k=kc, b=pt: e.transpose(
                        b[:, 0:128], zb[:, k * 128:(k + 1) * 128], ident[:, :]), "sPE", 1)
                    p.wait("vector", "sPE")
                    p.inc("vector", lambda e, k=kc, b=pt: e.tensor_copy(
                        fmsl[:, k * 128:(k + 1) * 128], b[:, 0:128]), "sV", 1)
                p.wait("sync", "sV")
                p.inc("sync", lambda e: e.dma_start(
                    out=xb0[:, :].rearrange("p (kc n) -> p kc n", n=ECP)[:, :, row0:row0 + 128],
                    in_=fmsl[:, :].rearrange("p (kc n) -> p kc n", n=128)), "sST", 16)

        with ExitStack() as sp:
            SBP = lambda n, sh, dt=F32: sp.enter_context(nc.sbuf_tensor(n, sh, dt))
            xsl = [SBP(f"cysl{i}", [128, D]) for i in range(2)]
            z32 = [SBP(f"cz32{i}", [128, D]) for i in range(2)]
            zb = sp.enter_context(nc.sbuf_tensor("cyzb_sb", [128, D], BF16))
            fmsl = sp.enter_context(nc.sbuf_tensor("cyfm_sb", [128, D], BF16))
            stat = SBP("cystat_sb", [128, 8])
            q2s = sp.enter_context(nc.sbuf_tensor("cyq2s_sb", [128, 1024], BF16))
            ps_tb = [sp.enter_context(nc.psum_tensor(f"cyps{i}", [128, 512], BF16)) for i in range(2)]
            for sl in range(EC // 128):
                xbuf = xsl[sl % 2]; zf = z32[sl % 2]
                if fused and sl < 4:
                    p.wait("vector", "sSC")
                    p.inc("vector", lambda e, d=xbuf: e.memset(d[:, :], 0.0), "sV", 1)
                else:
                    p.wait("sync", "sSC")
                    if fused:
                        p.inc("sync", lambda e, s_=sl, d=xbuf: e.dma_start(
                            out=d[:, :], in_=x3scr_h[(s_ - 4) * 128:(s_ - 3) * 128, :]), "sX", 16)
                    else:
                        p.inc("sync", lambda e, s_=sl, d=xbuf: e.dma_start(
                            out=d[:, :], in_=x3e_h[s_ * 128:(s_ + 1) * 128, :]), "sX", 16)
                p.wait("scalar", "sST", max(0, p.sems["sST"].n - 32))
                ln_stats_apply(xbuf, stat, q2s, zf)
                store_master(0, 1 + sl * 128, zf, zb, fmsl, ps_tb, True)
            p.flush()

        for l in range(2):
            q0, nq_l = Q0[l], NQ[l]
            kv0, nkv = KV0[l], NKV[l]
            f0, nf = F0[l], NF[l]

            with ExitStack() as sp:
                qsr = sp.enter_context(nc.sbuf_tensor(f"qsr{l}_sb", [1, 64], F32))
                ksr = sp.enter_context(nc.sbuf_tensor(f"ksr{l}_sb", [1, 64], F32))
                mbf = sp.enter_context(nc.sbuf_tensor(f"mbf{l}_sb", [1, 1], BF16))
                ps_m = sp.enter_context(nc.psum_tensor(f"psm{l}", [128, 8], F32))
                p.inc("sync", lambda e: e.dma_start(out=qsr[:, :], in_=qsr_h[l][:, :]), "sX", 16)
                p.inc("sync", lambda e: e.dma_start(out=ksr[:, :], in_=ksr_h[l][:, :]), "sX", 16)
                p.wait("vector", "sX")
                p.inc("vector", lambda e: e.tensor_reduce(mscr[0:1, 0:1], qsr[0:1, :], AX.X, ALU.max,
                                                          apply_absolute_value=True), "sV", 1)
                p.inc("vector", lambda e: e.tensor_reduce(mscr[0:1, 1:2], ksr[0:1, :], AX.X, ALU.max,
                                                          apply_absolute_value=True), "sV", 1)
                p.inc("vector", lambda e: e.tensor_mul(mscr[0:1, 2:3], mscr[0:1, 0:1], mscr[0:1, 1:2]), "sV", 1)
                p.inc("vector", lambda e: e.tensor_scalar_mul(mscr[0:1, 3:4], mscr[0:1, 2:3], -8.0), "sV", 1)
                p.inc("vector", lambda e: e.tensor_copy(mbf[0:1, 0:1], mscr[0:1, 3:4]), "sV", 1)
                p.wait("tensor", "sV")
                p.inc("tensor", lambda e: e.matmul(ps_m[:, 0:1], ones1[0:1, :], mbf[0:1, 0:1],
                                                   start=True, stop=True), "sPE", 1)
                p.wait("vector", "sPE")
                p.inc("vector", lambda e: e.tensor_copy(negm[:, 0:1], ps_m[:, 0:1]), "sV", 1)
                for w in range(16):
                    p.inc("vector", lambda e, w_=w: e.tensor_tensor(
                        negmp[:, w_:w_ + 1], negm[:, 0:1], pmk[:, w_:w_ + 1], ALU.add), "sV", 1)
                p.flush()

            s_qkv = ExitStack()
            kfm = s_qkv.enter_context(nc.sbuf_tensor(f"kfm{l}_sb", [128, 4 * nkv], BF16))
            vtm = s_qkv.enter_context(nc.sbuf_tensor(f"vtm{l}_sb", [128, (nkv // 128) * 520], BF16))
            qfm = s_qkv.enter_context(nc.sbuf_tensor(f"qfm{l}_sb", [128, 4 * nq_l], BF16))

            def qk_prep(dst, n_toks, tok0, wcol0, csh, ssh, s2h, do_v, tga):
                with ExitStack() as sp:
                    SBA = lambda n, sh, dt=BF16: sp.enter_context(nc.sbuf_tensor(n + tga, sh, dt))
                    PSA = lambda n, sh, dt=F32: sp.enter_context(nc.psum_tensor(n + tga, sh, dt))
                    wqk = [SBA(f"wqk{i}_sb", [128, 8 * 128]) for i in range(2)]
                    wv = SBA("wv_sb", [128, 8 * 512]) if do_v else None
                    cs_t = SBA("cs_sb", [128, n_toks])
                    ss_t = SBA("ss_sb", [128, n_toks])
                    s24 = SBA("s24_sb", [128, 128])
                    rhs = [SBA(f"rhs{i}_sb", [128, 8 * 512]) for i in range(2)]
                    kraw4 = SBA("kraw4_sb", [128, 4 * 512])
                    kscd = [SBA("kscd0_sb", [128, 512])] * 2
                    tmp = [SBA("tmp0_sb", [128, 512])] * 2
                    t2b = [SBA("t2b0_sb", [128, 512])] * 2
                    t1b = [SBA("t1b0_sb", [128, 512])] * 2
                    q2b = [SBA("q2b0_sb", [128, 512])] * 2
                    nrmf = SBA("nrmf_sb", [128, 512], F32)
                    rinv = SBA("rinv_sb", [128, 512], F32)
                    rnb = SBA("rnb_sb", [128, 512])
                    ps_r = [PSA(f"ps_r{i}", [128, 512]) for i in range(2)]
                    ps_n4 = PSA("ps_n4", [128, 512])
                    ps_b = [PSA(f"ps_b{i}", [128, 512]) for i in range(2)]
                    ps_v = [PSA(f"ps_v{i}", [128, 512]) for i in range(2)] if do_v else None
                    p.inc("gpsimd", lambda e: e.dma_start(out=cs_t[:, :], in_=csh[:, tok0:tok0 + n_toks]), "sX", 16)
                    p.inc("gpsimd", lambda e: e.dma_start(out=ss_t[:, :], in_=ssh[:, tok0:tok0 + n_toks]), "sX", 16)
                    p.inc("gpsimd", lambda e: e.dma_start(out=s24[:, :], in_=s2h[:, :]), "sW", 16)
                    if do_v:
                        for kc in range(8):
                            p.inc("gpsimd", lambda e, k=kc: e.dma_start(
                                out=wv[:, k * 512:(k + 1) * 512],
                                in_=wqkvT_h[l][k * 128:(k + 1) * 128, 1024:1536]), "sW", 16)
                        p.wait("vector", "sW")
                        p.inc("vector", lambda e: e.memset(
                            vtm[:, :].rearrange("p (t h c) -> p t h c", h=8, c=65)[:, :, :, 64:65], 1.0), "sV", 1)
                    ntt = n_toks // 512
                    rhs_cap = {}
                    pe_cap = {}
                    rnb_cap = {-1: 0}
                    qcap = {}

                    def emit_rhs(t_):
                        rb_ = rhs[t_ % 2]
                        p.wait("sync", "sPE", pe_cap.get(t_ - 2, 0))
                        for kc in range(8):
                            off = 0 if kc < 4 else -1
                            p.inc("sync", lambda e, k=kc, tx=t_, o=off: e.dma_start(
                                out=rb_[:, k * 512:(k + 1) * 512],
                                in_=xb0[:, :].rearrange("p (kc n) -> p kc n", n=ECP)[
                                    :, k, 1 + tok0 + tx * 512 + o:1 + tok0 + (tx + 1) * 512 + o]), "sX", 16)
                        rhs_cap[t_] = p.sems["sX"].n

                    emit_rhs(0)
                    for tt in range(n_toks // 512):
                        rb = rhs[tt % 2]
                        p.wait("tensor", "sX", rhs_cap[tt])
                        if do_v:
                            p.wait("tensor", "sW")
                            for tc4 in range(4):
                                pv = ps_v[tc4 % 2]
                                p.bw("tensor", "Cv", tc4 % 2)
                                for kc in range(8):
                                    p.inc("tensor", lambda e, b=pv, k=kc, t4=tc4, r_=rb: e.matmul(
                                        b[:, 0:512], r_[:, k * 512 + t4 * 128:k * 512 + (t4 + 1) * 128],
                                        wv[:, k * 512:(k + 1) * 512],
                                        start=(k == 0), stop=(k == 7)), "sPE", 1)
                                p.wait("vector", "sPE")
                                p.inc("vector", lambda e, b=pv, t_=tt, t4=tc4: e.tensor_copy(
                                    vtm[:, (t_ * 4 + t4) * 520:(t_ * 4 + t4 + 1) * 520].rearrange(
                                        "p (h c) -> p h c", c=65)[:, :, 0:64],
                                    b[:, 0:512].rearrange("p (h c) -> p h c", c=64)), "sV", 1)
                                p.bd("Cv", tc4 % 2)
                        for j in range(4):
                            wb = wqk[j % 2]
                            p.bw("gpsimd", "Cwq", j % 2, ("sPE",))
                            for kc in range(8):
                                p.inc("gpsimd", lambda e, k=kc, j_=j, d=wb: e.dma_start(
                                    out=d[:, k * 128:(k + 1) * 128],
                                    in_=wqkvT_h[l][k * 128:(k + 1) * 128,
                                                   wcol0 + j_ * 128:wcol0 + (j_ + 1) * 128]), "sW", 16)
                            p.wait("tensor", "sW")
                            pb = ps_r[j % 2]
                            p.bw("tensor", "Cr", j % 2, ("sV",))
                            for kc in range(8):
                                p.inc("tensor", lambda e, b=pb, k=kc, d=wb, r_=rb: e.matmul(
                                    b[:, 0:512], d[:, k * 128:(k + 1) * 128],
                                    r_[:, k * 512:(k + 1) * 512],
                                    start=(k == 0), stop=(k == 7)), "sPE", 1)
                            mm_tail = p.sems["sPE"].n
                            p.bd("Cwq", j % 2, ("sPE",))
                            p.wait("vector", "sPE", mm_tail)
                            p.inc("vector", lambda e, b=pb, j_=j: e.tensor_copy(
                                kraw4[:, j_ * 512:(j_ + 1) * 512], b[:, 0:512]), "sV", 1)
                            p.bd("Cr", j % 2, ("sV",))
                            p.wait("scalar", "sV")
                            p.wait("scalar", "sPE", qcap.get(0, 0))
                            p.inc("scalar", lambda e, j_=j, qb=q2b[j % 2]: e.activation(
                                qb[:, :], kraw4[:, j_ * 512:(j_ + 1) * 512], AF.Square), "sSC", 1)
                            p.wait("tensor", "sSC")
                            p.wait("tensor", "sV", rnb_cap[tt - 1])
                            p.inc("tensor", lambda e, j_=j, qb=q2b[j % 2]: e.matmul(
                                ps_n4[32 * j_:32 * j_ + 2, 0:512], ind2[:, 0:2], qb[:, :],
                                start=True, stop=True, tile_position=(0, 32 * j_),
                                skip_group_check=True), "sPE", 1)
                            qcap[0] = p.sems["sPE"].n
                        pe_cap[tt] = p.sems["sPE"].n
                        if tt + 1 < n_toks // 512:
                            emit_rhs(tt + 1)
                        p.wait("scalar", "sPE")
                        p.inc("scalar", lambda e: e.activation(nrmf[:, :], ps_n4[:, 0:512], AF.Sqrt,
                                                               bias=g["eps24"][:, 0:1]), "sSC", 1)
                        p.wait("vector", "sSC")
                        p.inc("vector", lambda e: e.reciprocal(rinv[:, :], nrmf[:, :]), "sV", 1)
                        p.inc("vector", lambda e: e.tensor_copy(rnb[:, :], rinv[:, :]), "sV", 1)
                        rnb_cap[tt] = p.sems["sV"].n
                        for j in range(4):
                            pc = ps_b[j % 2]
                            p.bw("tensor", "Cb", j % 2, ("sV",))
                            p.wait("tensor", "sV", rnb_cap[tt])
                            p.inc("tensor", lambda e, j_=j, b=pc: e.matmul(
                                b[:, 0:512], s24[32 * j_:32 * j_ + 2, 0:128],
                                rnb[32 * j_:32 * j_ + 2, :],
                                start=True, stop=True, tile_position=(32 * j_, 0),
                                skip_group_check=True), "sPE", 1)
                            psb_tail = p.sems["sPE"].n
                            p.wait("vector", "sPE", psb_tail)
                            ks = kscd[j % 2]
                            p.inc("vector", lambda e, b=pc, j_=j, k_=ks: e.tensor_mul(
                                k_[:, :], kraw4[:, j_ * 512:(j_ + 1) * 512], b[:, 0:512]), "sV", 1)
                            p.bd("Cb", j % 2, ("sV",))
                            tb2 = t2b[j % 2]; tb1 = t1b[j % 2]; tpb = tmp[j % 2]
                            p.inc("vector", lambda e, t_=tt, k_=ks, tp=tpb: e.tensor_mul(
                                tp[:, :], k_[:, :], ss_t[:, t_ * 512:(t_ + 1) * 512]), "sV", 1)
                            for b0, sb0 in ((0, 32), (32, 0), (64, 96), (96, 64)):
                                p.inc("vector", lambda e, a=b0, b=sb0, tp=tpb, t2=tb2: e.tensor_copy(
                                    t2[a:a + 32, :], tp[b:b + 32, :]), "sV", 1)
                            p.inc("vector", lambda e, t_=tt, k_=ks, t1=tb1: e.tensor_mul(
                                t1[:, :], k_[:, :], cs_t[:, t_ * 512:(t_ + 1) * 512]), "sV", 1)
                            p.inc("vector", lambda e, j_=j, t_=tt, t1=tb1, t2=tb2: e.tensor_add(
                                dst[:, j_ * n_toks + t_ * 512:j_ * n_toks + (t_ + 1) * 512],
                                t1[:, :], t2[:, :]), "sV", 1)
                    p.flush()

            qk_prep(kfm, nkv, kv0, 512, kcs_h[l], kss_h[l], ks2_h[l], True, f"ka{l}")
            qk_prep(qfm, nq_l, q0, 0, qcs_h[l], qss_h[l], qs2_h[l], False, f"qa{l}")

            s_of = ExitStack()
            ofm = s_of.enter_context(nc.sbuf_tensor(f"ofm{l}_sb", [128, 4 * nq_l], BF16, side="right"))
            with ExitStack() as sp:
                tgb = f"b{l}"
                SBB = lambda n, sh, dt=BF16: sp.enter_context(nc.sbuf_tensor(n + tgb, sh, dt))
                PSB = lambda n, sh, dt=F32: sp.enter_context(nc.psum_tensor(n + tgb, sh, dt))
                eab = [SBB(f"eab{i}", [128, 1024]) for i in range(2)]
                smf = SBB("smf_sb", [128, 1024], F32)
                rsc = SBB("rsc_sb", [1, 1024], F32)
                rbh = SBB("rbh_sb", [1, 1024])
                rlf = SBB("rlf_sb", [1, 1024], F32)
                rbl = SBB("rbl_sb", [1, 1024])
                dcp = SBB("dcp_sb", [1, 1024], F32)
                rbs = SBB("rbs_sb", [64, 512], F32)
                ps_s = [PSB(f"ps_s{i}", [128, 1024]) for i in range(2)]
                ps_oa = [PSB(f"ps_oa{i}", [128, 512]) for i in range(2)]
                ps_ob = [PSB(f"ps_ob{i}", [128, 512]) for i in range(2)]
                exp_tail = [0, 0]
                omm_tail = [0, 0]
                oev_tail = [0, 0]
                for j in range(4):
                    for w in range(nq_l // 512):
                        wslot = (8 if l == 1 else 0) + w
                        poa = ps_oa[w % 2]; pob = ps_ob[w % 2]
                        for kt in range(8):
                            kcol = q0 - kv0 + 512 * (w - 1) + 128 * kt
                            pb = ps_s[kt % 2]
                            p.wait("tensor", "sSC", exp_tail[kt % 2])
                            p.inc("tensor", lambda e, b=pb, j_=j, w_=w, kc_=kcol: e.matmul(
                                b[:, 0:512], kfm[0:64, j_ * nkv + kc_:j_ * nkv + kc_ + 128],
                                qfm[0:64, j_ * nq_l + w_ * 512:j_ * nq_l + (w_ + 1) * 512],
                                start=True, stop=True, tile_position=(0, 0)), "sPE", 1)
                            p.inc("tensor", lambda e, b=pb, j_=j, w_=w, kc_=kcol: e.matmul(
                                b[:, 512:1024], kfm[64:128, j_ * nkv + kc_:j_ * nkv + kc_ + 128],
                                qfm[64:128, j_ * nq_l + w_ * 512:j_ * nq_l + (w_ + 1) * 512],
                                start=True, stop=True, tile_position=(64, 0)), "sPE", 1)
                            p.wait("vector", "sPE")
                            p.wait("vector", "sPE", omm_tail[kt % 2])
                            p.inc("vector", lambda e, b=pb, k_=kt: e.tensor_add(
                                smf[:, 0:512], b[:, 0:512], trio[:, k_ * 512:(k_ + 1) * 512]), "sV", 1)
                            p.inc("vector", lambda e, b=pb, k_=kt: e.tensor_add(
                                smf[:, 512:1024], b[:, 512:1024], trio[:, k_ * 512:(k_ + 1) * 512]), "sV", 1)
                            p.wait("scalar", "sV")
                            if kt < 4:
                                p.inc("scalar", lambda e, i=kt % 2, ws=wslot: e.activation(
                                    eab[i][:, :], smf[:, :], AF.Exp, bias=negmp[:, ws:ws + 1]), "sSC", 1)
                            else:
                                p.inc("scalar", lambda e, i=kt % 2: e.activation(
                                    eab[i][:, :], smf[:, :], AF.Exp, bias=negm[:, 0:1]), "sSC", 1)
                            exp_tail[kt % 2] = p.sems["sSC"].n
                            p.wait("tensor", "sSC", exp_tail[kt % 2])
                            if kt == 0:
                                p.wait("tensor", "sV", oev_tail[w % 2])
                            vcol = (kcol // 128) * 520
                            p.inc("tensor", lambda e, k=kt, i=kt % 2, vc=vcol, j_=j, po=poa: e.matmul(
                                po[0:65, 0:512], vtm[:, vc + (2 * j_) * 65:vc + (2 * j_ + 1) * 65],
                                eab[i][:, 0:512],
                                start=(k == 0), stop=(k == 7)), "sPE", 1)
                            p.inc("tensor", lambda e, k=kt, i=kt % 2, vc=vcol, j_=j, po=pob: e.matmul(
                                po[0:65, 0:512], vtm[:, vc + (2 * j_ + 1) * 65:vc + (2 * j_ + 2) * 65],
                                eab[i][:, 512:1024],
                                start=(k == 0), stop=(k == 7)), "sPE", 1)
                            omm_tail[kt % 2] = p.sems["sPE"].n
                        p.wait("vector", "sPE")
                        p.inc("vector", lambda e, po=poa: e.tensor_copy(dcp[0:1, 0:512], po[64:65, 0:512]), "sV", 1)
                        p.inc("vector", lambda e, po=pob: e.tensor_copy(dcp[0:1, 512:1024], po[64:65, 0:512]), "sV", 1)
                        p.inc("vector", lambda e: e.reciprocal(rsc[0:1, :], dcp[0:1, :]), "sV", 1)
                        p.inc("vector", lambda e: e.tensor_copy(rbh[0:1, :], rsc[0:1, :]), "sV", 1)
                        p.inc("vector", lambda e: e.tensor_sub(rlf[0:1, :], rsc[0:1, :], rbh[0:1, :]), "sV", 1)
                        p.inc("vector", lambda e: e.tensor_copy(rbl[0:1, :], rlf[0:1, :]), "sV", 1)
                        for half, (pso, c0) in enumerate(((poa, 0), (pob, 512))):
                            p.wait("tensor", "sV")
                            p.inc("tensor", lambda e, c=c0, po=pso: e.matmul(
                                po[64:128, 0:512], ones1[0:1, 0:64], rbh[0:1, c:c + 512],
                                start=True, stop=False, tile_position=(0, 64),
                                skip_group_check=True), "sPE", 1)
                            p.inc("tensor", lambda e, c=c0, po=pso: e.matmul(
                                po[64:128, 0:512], ones1[0:1, 0:64], rbl[0:1, c:c + 512],
                                start=False, stop=True, tile_position=(0, 64),
                                skip_group_check=True), "sPE", 1)
                            p.wait("vector", "sPE")
                            p.inc("vector", lambda e, po=pso: e.tensor_copy(
                                rbs[0:64, :], po[64:128, 0:512]), "sV", 1)
                            p.inc("vector", lambda e, j_=j, w_=w, hb=half, po=pso: e.scalar_tensor_tensor(
                                ofm[hb * 64:(hb + 1) * 64, j_ * nq_l + w_ * 512:j_ * nq_l + (w_ + 1) * 512],
                                po[0:64, 0:512], 1.0, rbs[0:64, :], ALU.mult, ALU.mult), "sV", 1)
                        oev_tail[w % 2] = p.sems["sV"].n
                p.flush()
            s_qkv.close()

            with ExitStack() as sp:
                tgc = f"c{l}"
                wo = sp.enter_context(nc.sbuf_tensor("cwo_sb" + tgc, [128, 4 * 1024], BF16))
                xin = [sp.enter_context(nc.sbuf_tensor(f"cxin{i}" + tgc, [128, D], F32)) for i in range(2)]
                xm = [sp.enter_context(nc.sbuf_tensor(f"cxm{i}" + tgc, [128, D], F32)) for i in range(2)]
                ps_w = [sp.enter_context(nc.psum_tensor(f"cpsw{i}" + tgc, [128, 512], F32)) for i in range(2)]
                for kc in range(4):
                    p.inc("gpsimd", lambda e, k=kc: e.dma_start(
                        out=wo[:, k * 1024:(k + 1) * 1024], in_=woT_h[l][k * 128:(k + 1) * 128, :]), "sW", 16)
                p.wait("tensor", "sW")
                for tc in range(nq_l // 128):
                    xb_ = xin[tc % 2]
                    ob = xm[tc % 2]
                    p.wait("sync", "sV")
                    p.inc("sync", lambda e, t_=tc, d=xb_: e.dma_start(
                        out=d[:, :], in_=xf[0][1 + q0 + t_ * 128:1 + q0 + (t_ + 1) * 128, :]), "sX", 16)
                    for dt_ in range(2):
                        pb = ps_w[dt_]
                        p.bw("tensor", "Cwo", dt_)
                        for kc in range(4):
                            p.inc("tensor", lambda e, b=pb, k=kc, t_=tc, d_=dt_: e.matmul(
                                b[:, 0:512], ofm[:, k * nq_l + t_ * 128:k * nq_l + (t_ + 1) * 128],
                                wo[:, k * 1024 + d_ * 512:k * 1024 + (d_ + 1) * 512],
                                start=(k == 0), stop=(k == 3)), "sPE", 1)
                        p.wait("vector", "sPE")
                        p.wait("vector", "sX")
                        p.wait("vector", "sST", max(0, p.sems["sST"].n - 16))
                        p.inc("vector", lambda e, b=pb, d_=dt_, o=ob, d2=xb_: e.tensor_add(
                            o[:, d_ * 512:(d_ + 1) * 512], b[:, 0:512], d2[:, d_ * 512:(d_ + 1) * 512]), "sV", 1)
                        p.bd("Cwo", dt_)
                    p.wait("sync", "sV")
                    p.inc("sync", lambda e, t_=tc, o=ob: e.dma_start(
                        out=xf[1][1 + q0 + t_ * 128:1 + q0 + (t_ + 1) * 128, :], in_=o[:, :]), "sST", 16)
                p.flush()
            s_of.close()

            s_w2 = ExitStack()
            w2sb = s_w2.enter_context(nc.sbuf_tensor(f"w2sb{l}", [128, NFC * 1024], BF16))
            with ExitStack() as spw:
                for fc in range(NFC):
                    cw = _fcw(fc)
                    p.inc("gpsimd", lambda e, f_=fc, c_=cw: e.dma_start(
                        out=w2sb[0:c_, f_ * 1024:(f_ + 1) * 1024],
                        in_=w2T_h[l][f_ * 128:f_ * 128 + c_, :]), "sW", 16)
                p.flush()
            subs = []
            t0s = 0
            while t0s < nf:
                subs.append((t0s, min(1024, nf - t0s)))
                t0s += 1024
            with ExitStack() as sp:
                tgd = f"d{l}"
                SBD = lambda n, sh, dt=BF16: sp.enter_context(nc.sbuf_tensor(n + tgd, sh, dt))
                PSD = lambda n, sh, dt=F32: sp.enter_context(nc.psum_tensor(n + tgd, sh, dt))
                hn = SBD("hn_sb", [128, 8 * 1024])
                acts = SBD("acts_sb", [128, NFC * 1024])
                lg8 = SBD("lg8_sb", [128, 8], F32); lb8 = SBD("lb8_sb", [128, 8], F32)
                b1c = SBD("b1c_sb", [128, 2 * NFC], F32)
                b2r = SBD("b2r_sb", [1, D])
                wf = [SBD(f"wf{i}_sb", [128, 2 * 1024]) for i in range(2)]
                xsl = [SBD(f"dxsl{i}", [128, D], F32) for i in range(2)]
                zb = SBD("dzb_sb", [128, D])
                stat = SBD("dstat_sb", [128, 8], F32)
                q2s = SBD("dq2s_sb", [128, 1024])
                atile = SBD("atile_sb", [128, 512])
                gtile = SBD("gtile_sb", [128, 512])
                x3sl = [SBD(f"x3sl{i}", [128, D], F32) for i in range(2)]
                yout = [SBD(f"yout{i}", [128, D], F32) for i in range(2)]
                fmsl = SBD("dfm_sb", [128, D])
                ps_g = [PSD(f"dpsg{i}", [128, 1024]) for i in range(2)]
                ps_d = [PSD(f"dpsd{i}", [128, 512]) for i in range(2)]
                ps_tb = [PSD(f"dpst{i}", [128, 512], BF16) for i in range(2)]
                p.inc("sync", lambda e: e.dma_start(out=lg8[:, :], in_=lg8_h[l][:, :]), "sX", 16)
                p.inc("sync", lambda e: e.dma_start(out=lb8[:, :], in_=lb8_h[l][:, :]), "sX", 16)
                p.inc("sync", lambda e: e.dma_start(out=b1c[:, :], in_=b1c_h[l][:, :]), "sX", 16)
                p.inc("gpsimd", lambda e: e.dma_start(out=b2r[:, :], in_=b2r_h[l][:, :]), "sW", 16)
                w2cap = [0]
                for (st0, snt) in subs:
                    for sl in range(snt // 128):
                        xbuf = xsl[sl % 2]
                        r0 = 1 + f0 + st0 + sl * 128
                        p.wait("sync", "sSC")
                        p.wait("sync", "sV")
                        p.inc("sync", lambda e, r_=r0, d=xbuf: e.dma_start(
                            out=d[:, 0:512], in_=xf[1][r_:r_ + 128, 0:512]), "sX", 16)
                        p.inc("sync", lambda e, r_=r0, d=xbuf: e.dma_start(
                            out=d[:, 512:1024], in_=xf[1][r_ - 1:r_ + 127, 512:1024]), "sX", 16)
                        p.wait("scalar", "sX")
                        ln_stats_apply(xbuf, stat, q2s, xbuf)
                        p.wait("vector", "sSC")
                        p.inc("vector", lambda e, d=xbuf: e.tensor_copy(zb[:, :], d[:, :]), "sV", 1)
                        p.wait("tensor", "sV")
                        for kc in range(8):
                            pt = ps_tb[kc % 2]
                            p.wait("tensor", "sV")
                            p.inc("tensor", lambda e, k=kc, b=pt: e.transpose(
                                b[:, 0:128], zb[:, k * 128:(k + 1) * 128], ident[:, :]), "sPE", 1)
                            p.wait("vector", "sPE")
                            p.inc("vector", lambda e, k=kc, b=pt, s_=sl: e.tensor_scalar(
                                hn[:, k * 1024 + s_ * 128:k * 1024 + (s_ + 1) * 128], b[:, 0:128],
                                lg8[:, k:k + 1], lb8[:, k:k + 1], ALU.mult, ALU.add), "sV", 1)
                    p.wait("vector", "sPE", w2cap[0])
                    NTT = snt // 512
                    for fc in range(NFC):
                        cw = _fcw(fc)
                        wb = wf[fc % 2]
                        p.bw("gpsimd", "Cwf", fc % 2, ("sPE",))
                        for kc in range(8):
                            p.inc("gpsimd", lambda e, k=kc, f_=fc, c_=cw, d=wb: e.dma_start(
                                out=d[:, k * 128:k * 128 + c_],
                                in_=w1T_h[l][k * 128:(k + 1) * 128, f_ * 128:f_ * 128 + c_]), "sW", 16)
                            p.inc("gpsimd", lambda e, k=kc, f_=fc, c_=cw, d=wb: e.dma_start(
                                out=d[:, 1024 + k * 128:1024 + k * 128 + c_],
                                in_=w1T_h[l][k * 128:(k + 1) * 128, HID + f_ * 128:HID + f_ * 128 + c_]), "sW", 16)
                        p.wait("tensor", "sW")
                        for tt in range(NTT):
                            pb = ps_g[tt % 2]
                            p.bw("tensor", "Cw1", tt % 2, ("sV", "sSC"))
                            for kc in range(8):
                                p.inc("tensor", lambda e, b=pb, k=kc, c_=cw, d=wb, t_=tt: e.matmul(
                                    b[0:c_, 0:512], d[:, k * 128:k * 128 + c_],
                                    hn[:, k * 1024 + t_ * 512:k * 1024 + (t_ + 1) * 512],
                                    start=(k == 0), stop=(k == 7)), "sPE", 1)
                            for kc in range(8):
                                p.inc("tensor", lambda e, b=pb, k=kc, c_=cw, d=wb, t_=tt: e.matmul(
                                    b[0:c_, 512:1024], d[:, 1024 + k * 128:1024 + k * 128 + c_],
                                    hn[:, k * 1024 + t_ * 512:k * 1024 + (t_ + 1) * 512],
                                    start=(k == 0), stop=(k == 7)), "sPE", 1)
                            p.wait("scalar", "sPE")
                            p.wait("scalar", "sV")
                            p.inc("scalar", lambda e, b=pb, c_=cw, f_=fc: e.activation(
                                atile[0:c_, :], b[0:c_, 0:512], AF.Identity,
                                bias=b1c[0:c_, f_:f_ + 1]), "sSC", 1)
                            p.inc("scalar", lambda e, b=pb, c_=cw, f_=fc: e.activation(
                                gtile[0:c_, :], b[0:c_, 512:1024], AF.Gelu,
                                bias=b1c[0:c_, NFC + f_:NFC + f_ + 1]), "sSC", 1)
                            p.wait("vector", "sSC")
                            p.inc("vector", lambda e, c_=cw, f_=fc, t_=tt: e.tensor_mul(
                                acts[0:c_, f_ * 1024 + t_ * 512:f_ * 1024 + (t_ + 1) * 512],
                                atile[0:c_, :], gtile[0:c_, :]), "sV", 1)
                            p.bd("Cw1", tt % 2, ("sV", "sSC"))
                        p.bd("Cwf", fc % 2, ("sPE",))
                    p.wait("tensor", "sW")
                    for tc in range(snt // 128):
                        xmb = xsl[tc % 2]
                        ob = yout[tc % 2]
                        r0 = 1 + f0 + st0 + tc * 128
                        p.wait("sync", "sV")
                        p.inc("sync", lambda e, r_=r0, d=xmb: e.dma_start(
                            out=d[:, :], in_=xf[1][r_:r_ + 128, :]), "sX", 16)
                        if l == 1:
                            x3b = x3sl[tc % 2]
                            x3src = x3scr_h if fused else x3own_h
                            p.inc("sync", lambda e, t_=st0 + tc * 128, d=x3b, s3=x3src: e.dma_start(
                                out=d[:, :], in_=s3[t_:t_ + 128, :]), "sX", 16)
                        for dt_ in range(2):
                            pb = ps_d[dt_]
                            p.bw("tensor", "Cw2", dt_)
                            for fc in range(NFC):
                                cw = _fcw(fc)
                                p.inc("tensor", lambda e, b=pb, f_=fc, c_=cw, t_=tc, d_=dt_: e.matmul(
                                    b[:, 0:512], acts[0:c_, f_ * 1024 + t_ * 128:f_ * 1024 + (t_ + 1) * 128],
                                    w2sb[0:c_, f_ * 1024 + d_ * 512:f_ * 1024 + (d_ + 1) * 512],
                                    start=(f_ == 0), stop=False), "sPE", 1)
                            p.inc("tensor", lambda e, b=pb, d_=dt_: e.matmul(
                                b[:, 0:512], ones1[0:1, 0:128], b2r[0:1, d_ * 512:(d_ + 1) * 512],
                                start=False, stop=True), "sPE", 1)
                            p.wait("vector", "sPE")
                            p.wait("vector", "sX")
                            p.wait("vector", "sST")
                            p.inc("vector", lambda e, b=pb, d_=dt_, o=ob, d2=xmb: e.tensor_add(
                                o[:, d_ * 512:(d_ + 1) * 512], b[:, 0:512], d2[:, d_ * 512:(d_ + 1) * 512]), "sV", 1)
                            if l == 1:
                                x3b = x3sl[tc % 2]
                                p.inc("vector", lambda e, d_=dt_, o=ob, d3=x3b: e.tensor_add(
                                    o[:, d_ * 512:(d_ + 1) * 512], o[:, d_ * 512:(d_ + 1) * 512],
                                    d3[:, d_ * 512:(d_ + 1) * 512]), "sV", 1)
                            p.bd("Cw2", dt_)
                        if l == 0:
                            store_master(0, r0, ob, zb, fmsl, ps_tb, True)
                        else:
                            p.wait("sync", "sV")
                            p.inc("sync", lambda e, t_=st0 + tc * 128, o=ob: e.dma_start(
                                out=out_h[t_:t_ + 128, :], in_=o[:, :]), "sST", 16)
                    w2cap[0] = p.sems["sPE"].n
                p.flush()
            s_w2.close()
        p.wait("sync", "sST")
        p.flush()
    return nc


def build_fused(dbg=False):
    nc = bass.Bass("TRN2", target_bir_lowering=False)
    x2_scr = nc.dram_tensor("x2_scr", [S, D], F32, kind="Internal").ap()
    x3_scr = nc.dram_tensor("x3_scr", [S, D], F32, kind="Internal").ap()
    if dbg:
        x2_dbg = nc.dram_tensor("x2_dbg", [S, D], F32, kind="ExternalOutput").ap()
        x3_dbg = nc.dram_tensor("x3_dbg", [S, D], F32, kind="ExternalOutput").ap()
    with ExitStack() as gst:
        p = Prog(nc, gst)
        g = mk_globals(nc, p, gst)
        ctx = (nc, p, gst, x2_scr, g, x3_scr, dbg)
        build_A(S, ctx=ctx)
        build_B(ctx=ctx)
        build_C(ctx=ctx)
        if dbg:
            p.inc("sync", lambda e: e.dma_start(out=x2_dbg[:, :], in_=x2_scr[:, :]), "sST", 16)
            p.inc("sync", lambda e: e.dma_start(out=x3_dbg[:, :], in_=x3_scr[:, :]), "sST", 16)
            p.wait("sync", "sST")
        p.flush()
    return nc

def _prep_C(inputs, x3_parts):
    f32 = lambda a: np.ascontiguousarray(a, dtype=np.float32)
    wqkv = np.asarray(inputs["L1_ffn_wqkv"], np.float32)
    wo = np.asarray(inputs["L1_ffn_wo"], np.float32)
    w1 = np.asarray(inputs["L1_ffn_w1"], np.float32)
    b1 = np.asarray(inputs["L1_ffn_b1"], np.float32)
    w2 = np.asarray(inputs["L1_ffn_w2"], np.float32)
    b2 = np.asarray(inputs["L1_ffn_b2"], np.float32)
    lg = np.asarray(inputs["L1_ffn_ln_g"], np.float32)
    lb = np.asarray(inputs["L1_ffn_ln_b"], np.float32)
    qs = np.asarray(inputs["L1_ffn_q_scale"], np.float32)
    ks = np.asarray(inputs["L1_ffn_k_scale"], np.float32)
    shared = {}
    for l in range(2):
        shared[f"f_wqkvT{l}"] = f32(wqkv[l])
        shared[f"f_woT{l}"] = f32(wo[l])
        shared[f"f_w1T{l}"] = f32(w1[l])
        b1c = np.zeros((128, 2 * NFC), np.float32)
        for fc in range(NFC):
            cw = _fcw(fc)
            b1c[0:cw, fc] = b1[l][fc * 128:fc * 128 + cw]
            b1c[0:cw, NFC + fc] = b1[l][HID + fc * 128:HID + fc * 128 + cw]
        shared[f"f_b1c{l}"] = b1c
        shared[f"f_w2T{l}"] = f32(w2[l])
        shared[f"f_b2r{l}"] = f32(b2[l].reshape(1, D))
        shared[f"f_lg8{l}"] = f32(lg[l].reshape(8, 128).T)
        shared[f"f_lb8{l}"] = f32(lb[l].reshape(8, 128).T)
        qs2 = np.zeros((128, 128), np.float32)
        ks2 = np.zeros((128, 128), np.float32)
        for jj in range(4):
            qs2[32 * jj, 0:64] = qs[l]; qs2[32 * jj + 1, 64:128] = qs[l]
            ks2[32 * jj, 0:64] = ks[l]; ks2[32 * jj + 1, 64:128] = ks[l]
        shared[f"f_qs2{l}"] = qs2
        shared[f"f_ks2{l}"] = ks2
        shared[f"f_qsr{l}"] = f32(qs[l].reshape(1, 64))
        shared[f"f_ksr{l}"] = f32(ks[l].reshape(1, 64))
    trim = np.zeros((128, 8 * 512), np.float32)
    trio = np.zeros((128, 8 * 512), np.float32)
    kr = np.arange(128)[:, None]; qr = np.arange(512)[None, :]
    for kt in range(8):
        if kt < 4:
            keep = (qr <= 128 * kt + kr)
        else:
            keep = (128 * (kt - 4) + kr <= qr)
        trim[:, kt * 512:(kt + 1) * 512] = keep
        trio[:, kt * 512:(kt + 1) * 512] = -30000.0 * (~keep)
    shared["f_trim"] = trim
    shared["f_trio"] = trio
    x3f = np.zeros((B, S, D), np.float32)
    for c in range(8):
        b, hh = c // 2, c % 2
        x3f[b, hh * T:(hh + 1) * T] = x3_parts[c]
    in_maps = []
    for c in range(8):
        b, hh = c // 2, c % 2
        hT = hh * T
        pos = hT - 2560 + np.arange(EC)
        x3e = np.zeros((EC, D), np.float32)
        lo, hi = max(0, 2560 - hT), EC
        x3e[lo:hi] = x3f[b, pos[lo]:pos[hi - 1] + 1]
        m = dict(shared)
        qcs0, qss0 = xpos_tables(pos, True)
        kcs0, kss0 = xpos_tables(pos, False)
        pmkv = np.zeros((128, 16), np.float32)
        if hh == 0:
            pmkv[:, 3] = -30000.0
            pmkv[:, 8 + 1] = -30000.0
        m.update(x3e=x3e, x3own=np.ascontiguousarray(x3f[b, hT:hT + T]),
                 f_pmk=pmkv)
        for l in range(2):
            m[f"f_qcs{l}"] = qcs0; m[f"f_qss{l}"] = qss0
            m[f"f_kcs{l}"] = kcs0; m[f"f_kss{l}"] = kss0
        in_maps.append(m)
    return in_maps


_NC_CACHE = {}

def _get_programs():
    if "A" not in _NC_CACHE:
        _NC_CACHE["A"] = build_A(T)
        _NC_CACHE["B"] = build_B()
        _NC_CACHE["C"] = build_C()
    return _NC_CACHE["A"], _NC_CACHE["B"], _NC_CACHE["C"]


FUSED = True

def kernel(**inputs):
    if FUSED:
        return kernel_fused(**inputs)
    return kernel_unfused(**inputs)


def kernel_unfused(**inputs):
    ncA, ncB, ncC = _get_programs()
    resA = run_bass_kernel_spmd(ncA, _prep_A(inputs), core_ids=list(range(8)))
    x2_parts = [np.asarray(resA.results[c]["x2"]) for c in range(8)]
    resB = run_bass_kernel_spmd(ncB, _prep_B(inputs, x2_parts), core_ids=list(range(8)))
    x3_parts = [np.asarray(resB.results[c]["x3"]) for c in range(8)]
    resC = run_bass_kernel_spmd(ncC, _prep_C(inputs, x3_parts), core_ids=list(range(8)))
    out = np.empty((B, S, D), np.float32)
    for c in range(8):
        b, hh = c // 2, c % 2
        out[b, hh * T:(hh + 1) * T] = np.asarray(resC.results[c]["out"])
    return out

def _prep_F(inputs):
    in_maps_a = _prep_A(inputs, nq=S)
    b_shared = _prep_B(inputs, [np.zeros((T, D), np.float32)] * 8)[0]
    for k in ("x2f",):
        b_shared.pop(k, None)
    c_maps = None
    f32 = lambda a: np.ascontiguousarray(a, dtype=np.float32)
    cm = _prep_C(inputs, [np.zeros((T, D), np.float32)] * 8)[0]
    for k in ("x3e", "x3own"):
        cm.pop(k, None)
    pos = -512 + np.arange(EC)
    qcs0, qss0 = xpos_tables(pos, True)
    kcs0, kss0 = xpos_tables(pos, False)
    pmkv = np.zeros((128, 16), np.float32)
    pmkv[:, 0] = -30000.0
    pmkv[:, 8] = -30000.0
    cm["f_pmk"] = pmkv
    for l in range(2):
        cm[f"f_qcs{l}"] = qcs0; cm[f"f_qss{l}"] = qss0
        cm[f"f_kcs{l}"] = kcs0; cm[f"f_kss{l}"] = kss0
    in_maps = []
    for c in range(8):
        m = dict(in_maps_a[c])
        m.update(b_shared)
        m.update(cm)
        in_maps.append(m)
    return in_maps


def kernel_fused(**inputs):
    if "F" not in _NC_CACHE:
        _NC_CACHE["F"] = build_fused()
    ncf = _NC_CACHE["F"]
    res = run_bass_kernel_spmd(ncf, _prep_F(inputs), core_ids=list(range(8)))
    out = np.empty((B, S, D), np.float32)
    for c in range(8):
        b, hh = c // 2, c % 2
        out[b, hh * T:(hh + 1) * T] = np.asarray(res.results[c]["out"])[hh * T:(hh + 1) * T]
    return out


# version 14
# speedup vs baseline: 1.0217x; 1.0217x over previous
import numpy as np
import ml_dtypes
from contextlib import ExitStack
import concourse.bass as bass
import concourse.mybir as mybir
from concourse.bass_utils import run_bass_kernel_spmd

F32 = mybir.dt.float32
BF16 = mybir.dt.bfloat16
I32 = mybir.dt.int32
AF = mybir.ActivationFunctionType
ALU = mybir.AluOpType
AX = mybir.AxisListType
BFNP = ml_dtypes.bfloat16

B, S, D = 4, 4096, 1024
T = 2048
LN_EPS = 1e-5


def l0_rope_cos_sin(positions):
    inv_freq = 1.0 / 10000.0 ** (np.arange(0, 64, 2, dtype=np.float32) / 64)
    t = np.asarray(positions, dtype=np.float32)
    freqs = np.outer(t, inv_freq)
    cos = np.concatenate([np.cos(freqs), np.cos(freqs)], -1)[:, ::2]
    sin = np.concatenate([np.sin(freqs), np.sin(freqs)], -1)[:, ::2]
    return cos.astype(np.float32), sin.astype(np.float32)


def l0_qk_perm():
    perm = []
    for h in range(16):
        perm += [h * 64 + 2 * i for i in range(32)]
        perm += [h * 64 + 2 * i + 1 for i in range(32)]
    return np.array(perm, dtype=np.int64)


def l0_rope_tiles(positions, scale):
    cos, sin = l0_rope_cos_sin(positions)
    n = len(positions)
    CS = np.empty((128, n), np.float32)
    SS = np.empty((128, n), np.float32)
    for b0 in (0, 64):
        CS[b0:b0 + 32] = cos.T
        CS[b0 + 32:b0 + 64] = cos.T
        SS[b0:b0 + 32] = -sin.T
        SS[b0 + 32:b0 + 64] = sin.T
    return (CS * scale).astype(np.float32), (SS * scale).astype(np.float32)



class Sem:
    __slots__ = ("h", "n")
    def __init__(self, h):
        self.h = h
        self.n = 0


class Prog:
    ENGINES = ("tensor", "vector", "scalar", "sync", "gpsimd")

    def __init__(self, nc, gstack):
        self.nc = nc
        self.gstack = gstack
        self.blk = gstack.enter_context(nc.Block())
        self.sems = {n: Sem(gstack.enter_context(nc.semaphore(n)))
                     for n in ("sPE", "sV", "sSC", "sX", "sW", "sST", "sG")}
        self.q = {e: [] for e in self.ENGINES}
        self._barrier = {}
        self._tails = {}
        self._maxw = {e: {} for e in self.ENGINES}

    def sem(self, name):
        return self.sems[name]

    def on(self, eng, fn):
        self.q[eng].append(fn)

    def wait(self, eng, semname, val=None):
        s = self.sems[semname]
        v = s.n if val is None else val
        if v <= 0 or self._maxw[eng].get(semname, -1) >= v:
            return
        self._maxw[eng][semname] = v
        self.q[eng].append(lambda e: e.wait_ge(s.h, v))

    def inc(self, eng, fn, semname, k):
        if eng == "vector":
            if self._maxw[eng].get("sV", -1) < self.sems["sV"].n:
                self.wait(eng, "sV")
        elif eng == "scalar":
            if self._maxw[eng].get("sSC", -1) < self.sems["sSC"].n:
                self.wait(eng, "sSC")
        elif eng == "gpsimd" and semname == "sG":
            sg = self.sems["sG"]
            v = sg.n
            self.q[eng].append(lambda e: e.wait_ge(sg.h, v))
        s = self.sems[semname]
        s.n += k
        self.q[eng].append(lambda e: fn(e).then_inc(s.h, k))

    def snap(self):
        return {n: s.n for n, s in self.sems.items()}

    def bw(self, eng, key, idx, sems=("sV",)):
        t = self._tails.setdefault(key, {})
        for s in sems:
            v = t.get((idx, s), 0)
            if v > 0:
                self.wait(eng, s, v)

    def bd(self, key, idx, sems=("sV",)):
        t = self._tails.setdefault(key, {})
        for s in sems:
            t[(idx, s)] = self.sems[s].n

    def flush(self, barrier=True):
        bar = self._barrier if barrier else {}
        for eng in self.ENGINES:
            closures = self.q[eng]
            if not closures and not bar:
                continue
            waits = [(self.sems[n].h, v) for n, v in bar.items() if v > 0]
            def body(e, closures=list(closures), waits=waits):
                for h, v in waits:
                    e.wait_ge(h, v)
                for fn in closures:
                    fn(e)
            getattr(self.blk, eng)(body)
        self.q = {e: [] for e in self.ENGINES}
        self._barrier = self.snap()


def ceil_div(a, b):
    return (a + b - 1) // b

def mk_globals(nc, p, gst):
    SBg = lambda n, sh, dt=BF16: gst.enter_context(nc.sbuf_tensor(n, sh, dt))
    ident = SBg("g_ident_sb", [128, 128])
    ones1 = SBg("g_ones1_sb", [1, 128])
    iotaf = SBg("g_iotaf_sb", [128, 128], I32)
    iotap = SBg("g_iotap_sb", [128, 128], I32)
    epsb = SBg("g_epsb_sb", [128, 1], F32)
    p.inc("gpsimd", lambda e: e.iota(iotaf[:, :], pattern=[[1, 128]], base=0, channel_multiplier=0), "sW", 1)
    p.inc("gpsimd", lambda e: e.iota(iotap[:, :], pattern=[[0, 128]], base=0, channel_multiplier=1), "sW", 1)
    p.wait("vector", "sW")
    p.inc("vector", lambda e: e.tensor_tensor(ident[:, :], iotaf[:, :], iotap[:, :], ALU.is_equal), "sV", 1)
    p.inc("vector", lambda e: e.memset(ones1[:, :], 1.0), "sV", 1)
    p.inc("vector", lambda e: e.memset(epsb[:, :], LN_EPS), "sV", 1)
    eps24 = SBg("g_eps24_sb", [128, 1], F32)
    p.inc("vector", lambda e: e.memset(eps24[:, :], 1e-24), "sV", 1)
    p.flush()
    return dict(ident=ident, ones1=ones1, epsb=epsb, eps24=eps24)



def build_A(nq, phases=99, ctx=None):
    fused = ctx is not None
    if fused:
        nc = ctx[0]
    else:
        nc = bass.Bass("TRN2", target_bir_lowering=False)
    ein = lambda n, sh, dt=F32: nc.dram_tensor(n, sh, dt, kind="ExternalInput").ap()
    x_h = ein("x", [S, D])
    xq_h = ein("xq", [nq, D])
    qcs_h = ein("qcs", [128, nq]); qss_h = ein("qss", [128, nq])
    kcs_h = ein("kcs", [128, S]);  kss_h = ein("kss", [128, S])
    wdq_h = ein("wdq_t", [D, 512]); wdkv_h = ein("wdkv_t", [D, 512])
    wuq_h = ein("wuq_tp", [512, D]); wuq2_h = ein("wuq_tps", [512, D])
    wuk_h = ein("wuk_tp", [512, D]); wuk2_h = ein("wuk_tps", [512, D])
    wuv_h = ein("wuv_t", [512, D])
    wout_h = ein("wout_t", [D, D])
    gam_h = ein("gamma8", [128, 8]); bet_h = ein("beta8", [128, 8])
    inda_h = ein("ind_a", [128, 1]); indb_h = ein("ind_b", [128, 1])
    wg_h = ein("wg_t", [D, 3072], I32); wu_h = ein("wu_t", [D, 3072], I32)
    wd_h = ein("wd_t", [3072, D], I32)
    if fused:
        x2_h = ctx[3]
        xq_h = x_h
        qall_scr = nc.dram_tensor("qall_scr", [128, 8 * nq], BF16, kind="Internal").ap()
    else:
        x2_h = nc.dram_tensor("x2", [nq, D], F32, kind="ExternalOutput").ap()
        qall_scr = None
    x1s_h = nc.dram_tensor("x1_scr", [nq, D], F32, kind="Internal").ap()

    NQT = nq // 512
    NQC = nq // 128
    NKT = S // 512
    NKC = S // 128

    with ExitStack() as gst:
        if fused:
            p = ctx[1]
            g = ctx[4]
        else:
            p = Prog(nc, gst)
            g = mk_globals(nc, p, gst)
        ident = g["ident"]; ones1 = g["ones1"]; epsb = g["epsb"]
        p.flush()

        def ld_sync(dst, src):
            p.inc("sync", lambda e, d=dst, s=src: e.dma_start(out=d, in_=s), "sX", 16)
        def ld_cast(dst, src):
            p.inc("gpsimd", lambda e, d=dst, s=src: e.dma_start(out=d, in_=s), "sW", 16)

        qstat = gst.enter_context(nc.sbuf_tensor("qstat_sb", [128, 2 * NQC], F32))
        mx = gst.enter_context(nc.sbuf_tensor("mx_sb", [1, 36], F32))
        negm = gst.enter_context(nc.sbuf_tensor("negm_sb", [128, 1], F32))
        s_cq = ExitStack(); s_ckv = ExitStack(); s_qk = ExitStack(); s_o = ExitStack()
        cq = s_cq.enter_context(nc.sbuf_tensor("cq_sb", [128, 4 * nq], BF16, side="right"))
        ckv = s_ckv.enter_context(nc.sbuf_tensor("ckv_sb", [128, 4 * S], BF16))
        p.on("vector", lambda e: e.memset(mx[:, :], 0.0))

        with ExitStack() as sp:
            SB2 = lambda n, sh, dt=BF16: sp.enter_context(nc.sbuf_tensor(n, sh, dt))
            PS2 = lambda n, sh, dt=F32: sp.enter_context(nc.psum_tensor(n, sh, dt))
            wdq = SB2("wdq_sb", [128, 8 * 512]); wdkv = SB2("wdkv_sb", [128, 8 * 512])
            gam = SB2("gam_sb", [128, 8], F32); bet = SB2("bet_sb", [128, 8], F32)
            xsl = [SB2(f"xsl{i}", [128, D], F32) for i in range(4)]
            h1b = [SB2(f"h1b_sb{i}", [128, D]) for i in range(2)]
            zsl = [SB2(f"zsl_sb{i}", [128, D]) for i in range(4)]
            stat = SB2("stat_sb", [128, 64], F32)
            xs32 = SB2("xs32_sb", [128, 1024], F32)
            q2s = SB2("q2s_sb", [128, 1024])
            h2sl = SB2("h2sl_sb", [128, 8 * 512])
            ps_c = [PS2(f"ps_c{i}", [128, 512]) for i in range(2)]
            ps_tr = [PS2(f"ps_tr{i}", [128, 512], BF16) for i in range(2)]

            ld_sync(gam[:, :], gam_h[:, :]); ld_sync(bet[:, :], bet_h[:, :])
            for kc in range(8):
                ld_cast(wdq[:, kc * 512:(kc + 1) * 512], wdq_h[kc * 128:(kc + 1) * 128, :])
                ld_cast(wdkv[:, kc * 512:(kc + 1) * 512], wdkv_h[kc * 128:(kc + 1) * 128, :])
            p.wait("tensor", "sW"); p.wait("tensor", "sX")

            cap = {}

            def pair_chain(i, srcs, qidx):
                pp = i % 2
                xb = [xsl[2 * pp + a] for a in (0, 1)]
                p.wait("sync", "sSC", cap.get(("D", i - 2), 0))
                for a in (0, 1):
                    p.inc("sync", lambda e, s=srcs[a], d=xb[a]: e.dma_start(out=d[:, :], in_=s), "sX", 16)
                for c in (0, 1):
                    b = 32 * pp + 16 * c
                    src2 = xb if c == 0 else h1b
                    p.wait("scalar", "sX")
                    for a in (0, 1):
                        p.inc("scalar", lambda e, s=src2[a], b_=b, a_=a: e.activation(
                            q2s[:, 0:1024], s[:, :], AF.Square,
                            accum_out=stat[:, b_ + 2 + a_:b_ + 3 + a_]), "sSC", 1)
                    for a in (0, 1):
                        p.inc("scalar", lambda e, s=src2[a], b_=b, a_=a: e.activation(
                            xs32[:, 0:1024], s[:, :], AF.Identity,
                            accum_out=stat[:, b_ + a_:b_ + a_ + 1]), "sSC", 1)
                    if c == 0:
                        cap[("Cx", i)] = p.sems["sSC"].n
                    p.wait("vector", "sSC")
                    p.inc("vector", lambda e, b_=b: e.tensor_scalar_mul(
                        stat[:, b_ + 4:b_ + 6], stat[:, b_:b_ + 2], -1.0 / D), "sV", 1)
                    p.inc("vector", lambda e, b_=b: e.tensor_mul(
                        stat[:, b_ + 6:b_ + 8], stat[:, b_ + 4:b_ + 6], stat[:, b_ + 4:b_ + 6]), "sV", 1)
                    p.inc("vector", lambda e, b_=b: e.scalar_tensor_tensor(
                        stat[:, b_ + 8:b_ + 10], stat[:, b_ + 2:b_ + 4], 1.0 / D,
                        stat[:, b_ + 6:b_ + 8], ALU.mult, ALU.subtract), "sV", 1)
                    p.wait("scalar", "sV")
                    p.inc("scalar", lambda e, b_=b: e.activation(
                        stat[:, b_ + 10:b_ + 12], stat[:, b_ + 8:b_ + 10], AF.Sqrt,
                        bias=epsb[:, 0:1]), "sSC", 1)
                    p.wait("vector", "sSC")
                    p.inc("vector", lambda e, b_=b: e.reciprocal(
                        stat[:, b_ + 12:b_ + 14], stat[:, b_ + 10:b_ + 12]), "sV", 1)
                    p.inc("vector", lambda e, b_=b: e.tensor_mul(
                        stat[:, b_ + 14:b_ + 16], stat[:, b_ + 4:b_ + 6], stat[:, b_ + 12:b_ + 14]), "sV", 1)
                    if c == 0 and qidx[0] >= 0:
                        for a in (0, 1):
                            p.inc("vector", lambda e, j=qidx[a], b_=b, a_=a: e.tensor_scalar_add(
                                qstat[:, 2 * j:2 * j + 1], stat[:, b_ + 12 + a_:b_ + 13 + a_], 1.0), "sV", 1)
                            p.inc("vector", lambda e, j=qidx[a], b_=b, a_=a: e.tensor_copy(
                                qstat[:, 2 * j + 1:2 * j + 2], stat[:, b_ + 14 + a_:b_ + 15 + a_]), "sV", 1)
                    dst2 = h1b if c == 0 else [zsl[2 * pp + a] for a in (0, 1)]
                    p.wait("scalar", "sV")
                    if c == 1:
                        p.wait("scalar", "sPE", cap.get(("H", i - 2), 0))
                    for a in (0, 1):
                        p.inc("scalar", lambda e, s=src2[a], d=dst2[a], b_=b, a_=a: e.activation(
                            d[:, :], s[:, :], AF.Identity,
                            bias=stat[:, b_ + 14 + a_:b_ + 15 + a_],
                            scale=stat[:, b_ + 12 + a_:b_ + 13 + a_]), "sSC", 1)
                    if c == 0:
                        cap[("D", i)] = p.sems["sSC"].n
                    else:
                        cap[("G", i)] = p.sems["sSC"].n

            def pair_H(i, sl0):
                pp = i % 2
                for a in (0, 1):
                    p.wait("tensor", "sSC", cap[("G", i)])
                    for kc in range(8):
                        pt = ps_tr[kc % 2]
                        p.bw("tensor", "Atr", kc % 2)
                        p.inc("tensor", lambda e, k=kc, b=pt, z=zsl[2 * pp + a]: e.transpose(
                            b[:, 0:128], z[:, k * 128:(k + 1) * 128], ident[:, :]), "sPE", 1)
                        p.wait("vector", "sPE")
                        p.inc("vector", lambda e, k=kc, b=pt, s=sl0 + a: e.tensor_scalar(
                            h2sl[:, k * 512 + s * 128: k * 512 + (s + 1) * 128],
                            b[:, 0:128],
                            gam[:, k:k + 1], bet[:, k:k + 1], ALU.mult, ALU.add), "sV", 1)
                        p.bd("Atr", kc % 2)
                cap[("H", i)] = p.sems["sPE"].n

            def emit_cmms(g, groups):
                src_h, tt, dest_c, dest_ntok, wsb, qbase = groups[g]
                p.wait("tensor", "sV")
                for mc in range(4):
                    pb = ps_c[mc % 2]
                    p.bw("tensor", "Ac", mc % 2)
                    for kc in range(8):
                        p.inc("tensor", lambda e, m=mc, k=kc, b=pb, w=wsb: e.matmul(
                            b[:, 0:512], w[:, k * 512 + m * 128:k * 512 + (m + 1) * 128],
                            h2sl[:, k * 512:(k + 1) * 512],
                            start=(k == 0), stop=(k == 7)), "sPE", 1)
                    p.wait("vector", "sPE")
                    p.inc("vector", lambda e, m=mc, b=pb, d=dest_c, nt=dest_ntok, t=tt: e.tensor_copy(
                        d[:, m * nt + t * 512: m * nt + (t + 1) * 512], b[:, 0:512]), "sV", 1)
                    p.bd("Ac", mc % 2)

            if phases >= 2:
                groups = []
                for tt in range(NKT):
                    groups.append((x_h, tt, ckv, S, wdkv, -1))
                for tt in range(NQT):
                    groups.append((xq_h, tt, cq, nq, wdq, 0))
                npairs = 2 * len(groups)

                def pair_meta(i):
                    g = i // 2
                    sl0 = 2 * (i % 2)
                    src_h, tt, dest_c, dest_ntok, wsb, qbase = groups[g]
                    srcs = [src_h[(tt * 4 + sl0 + a) * 128:(tt * 4 + sl0 + a + 1) * 128, :]
                            for a in (0, 1)]
                    qidx = ([qbase + tt * 4 + sl0 + a for a in (0, 1)]
                            if qbase >= 0 else [-1, -1])
                    return srcs, qidx, sl0

                for i in range(npairs + 1):
                    if i >= 1:
                        _, _, sl0m = pair_meta(i - 1)
                        pair_H(i - 1, sl0m)
                        if (i - 1) % 2 == 1:
                            emit_cmms((i - 1) // 2, groups)
                    if i < npairs:
                        srcs, qidx, sl0 = pair_meta(i)
                        pair_chain(i, srcs, qidx)
            p.flush()

        kall = s_qk.enter_context(nc.sbuf_tensor("kall_sb", [128, 8 * S], BF16))
        if fused:
            qall = None
        else:
            qall = s_qk.enter_context(nc.sbuf_tensor("qall_sb", [128, 8 * nq], BF16))

        def qk_rope(dst, n_toks, csrc, c_ntok, w1h, w2h, csh, ssh, mxcol_base, tag, hbm_dst=None):
            with ExitStack() as sp:
                SB3 = lambda n, sh, dt=BF16: sp.enter_context(nc.sbuf_tensor(n, sh, dt))
                PS3 = lambda n, sh, dt=F32: sp.enter_context(nc.psum_tensor(n + str(tag), sh, dt))
                w1 = SB3(f"w1_sb{tag}", [128, 4 * 1024]); w2 = SB3(f"w2_sb{tag}", [128, 4 * 1024])
                cs_t = SB3(f"cs_sb{tag}", [128, n_toks], F32); ss_t = SB3(f"ss_sb{tag}", [128, n_toks], F32)
                t1s = SB3(f"t1s_sb{tag}", [128, 512])
                qsq = SB3(f"qsq_sb{tag}", [128, 512])
                nrm = SB3(f"nrm_sb{tag}", [1, 1024], F32)
                qdst = [SB3(f"qdst{i}{tag}" if False else "qdst%d" % i + str(tag), [128, 512]) for i in range(2)]
                mxsc = SB3(f"mxsc_sb{tag}", [1, 4], F32)
                inda = SB3(f"inda_sb{tag}", [128, 1]); indb = SB3(f"indb_sb{tag}", [128, 1])
                ps_ab = [PS3(f"ps_r{i}", [128, 1024]) for i in range(2)]
                ps_n = PS3("ps_n", [128, 512])
                ld_sync(cs_t[:, :], csh[:, :]); ld_sync(ss_t[:, :], ssh[:, :])
                ld_cast(inda[:, :], inda_h[:, :]); ld_cast(indb[:, :], indb_h[:, :])
                for kc in range(4):
                    ld_cast(w1[:, kc * 1024:(kc + 1) * 1024], w1h[kc * 128:(kc + 1) * 128, :])
                    ld_cast(w2[:, kc * 1024:(kc + 1) * 1024], w2h[kc * 128:(kc + 1) * 128, :])
                p.wait("tensor", "sW"); p.wait("tensor", "sX")
                p.wait("vector", "sX"); p.wait("vector", "sW")
                for g in range(8):
                    for tt in range(n_toks // 512):
                        pb = ps_ab[tt % 2]
                        p.bw("tensor", "Aqr" + tag, tt % 2)
                        for kc in range(4):
                            p.inc("tensor", lambda e, b=pb, k=kc, g_=g, t_=tt: e.matmul(
                                b[:, 0:512], w1[:, k * 1024 + g_ * 128:k * 1024 + (g_ + 1) * 128],
                                csrc[:, k * c_ntok + t_ * 512:k * c_ntok + (t_ + 1) * 512],
                                start=(k == 0), stop=(k == 3)), "sPE", 1)
                        for kc in range(4):
                            p.inc("tensor", lambda e, b=pb, k=kc, g_=g, t_=tt: e.matmul(
                                b[:, 512:1024], w2[:, k * 1024 + g_ * 128:k * 1024 + (g_ + 1) * 128],
                                csrc[:, k * c_ntok + t_ * 512:k * c_ntok + (t_ + 1) * 512],
                                start=(k == 0), stop=(k == 3)), "sPE", 1)
                        p.wait("vector", "sPE")
                        p.wait("vector", "sSC")
                        p.inc("vector", lambda e, b=pb, t_=tt: e.tensor_mul(
                            t1s[:, :], b[:, 0:512], cs_t[:, t_ * 512:(t_ + 1) * 512]), "sV", 1)
                        if hbm_dst is None:
                            _mt = dst[:, g * n_toks + tt * 512:g * n_toks + (tt + 1) * 512]
                        else:
                            _mt = qdst[tt % 2][:, :]
                            p.wait("vector", "sST", max(0, p.sems["sST"].n - 16))
                        p.inc("vector", lambda e, b=pb, mt=_mt, t_=tt: e.tensor_mul(
                            mt, b[:, 512:1024], ss_t[:, t_ * 512:(t_ + 1) * 512]), "sV", 1)
                        if hbm_dst is None:
                            dtile = dst[:, g * n_toks + tt * 512:g * n_toks + (tt + 1) * 512]
                        else:
                            dtile = qdst[tt % 2][:, :]
                            p.wait("vector", "sST", max(0, p.sems["sST"].n - 16))
                        p.inc("vector", lambda e, dt_=dtile: e.tensor_add(
                            dt_, dt_, t1s[:, :]), "sV", 1)
                        p.bd("Aqr" + tag, tt % 2)
                        if hbm_dst is not None:
                            p.wait("sync", "sV")
                            p.inc("sync", lambda e, dt_=dtile, g_=g, t_=tt: e.dma_start(
                                out=hbm_dst[:, g_ * n_toks + t_ * 512:g_ * n_toks + (t_ + 1) * 512],
                                in_=dt_), "sST", 16)
                        p.wait("scalar", "sV")
                        p.inc("scalar", lambda e, dt_=dtile: e.activation(
                            qsq[:, :], dt_, AF.Square), "sSC", 1)
                        p.wait("tensor", "sSC")
                        p.inc("tensor", lambda e: e.matmul(ps_n[0:1, 0:512], inda[:, 0:1], qsq[:, :],
                                                           start=True, stop=True), "sPE", 1)
                        p.inc("tensor", lambda e: e.matmul(ps_n[0:1, 0:512], indb[:, 0:1], qsq[:, :],
                                                           start=False, stop=True,
                                                           skip_group_check=True), "sPE", 1)
                        p.wait("scalar", "sPE")
                        p.inc("scalar", lambda e: e.activation(nrm[0:1, 0:512], ps_n[0:1, 0:512], AF.Sqrt), "sSC", 1)
                        p.wait("vector", "sSC")
                        p.inc("vector", lambda e: e.reduce_max(mxsc[0:1, 0:1], nrm[0:1, 0:512], axis=AX.X), "sV", 1)
                        p.inc("vector", lambda e, c=mxcol_base + g: e.tensor_max(
                            mx[0:1, c:c + 1], mx[0:1, c:c + 1], mxsc[0:1, 0:1]), "sV", 1)
                        p.wait("tensor", "sV")
                p.flush()

        if phases >= 3:
            qk_rope(kall, S, ckv, S, wuk_h, wuk2_h, kcs_h, kss_h, 16, "k")
            qk_rope(qall, nq, cq, nq, wuq_h, wuq2_h, qcs_h, qss_h, 0, "q",
                    hbm_dst=qall_scr if fused else None)
        s_cq.close()

        with ExitStack() as sp:
          if phases >= 3:
            mxbf = sp.enter_context(nc.sbuf_tensor("mxbf_sb", [1, 1], BF16))
            ps_m = sp.enter_context(nc.psum_tensor("ps_m", [128, 8], F32))
            p.inc("vector", lambda e: e.reduce_max(mx[0:1, 32:33], mx[0:1, 0:16], axis=AX.X), "sV", 1)
            p.inc("vector", lambda e: e.reduce_max(mx[0:1, 33:34], mx[0:1, 16:32], axis=AX.X), "sV", 1)
            p.inc("vector", lambda e: e.tensor_mul(mx[0:1, 34:35], mx[0:1, 32:33], mx[0:1, 33:34]), "sV", 1)
            p.inc("vector", lambda e: e.tensor_scalar_mul(mx[0:1, 35:36], mx[0:1, 34:35], -1.0), "sV", 1)
            p.inc("vector", lambda e: e.tensor_copy(mxbf[0:1, 0:1], mx[0:1, 35:36]), "sV", 1)
            p.wait("tensor", "sV")
            p.inc("tensor", lambda e: e.matmul(ps_m[:, 0:1], ones1[0:1, :], mxbf[0:1, 0:1],
                                               start=True, stop=True), "sPE", 1)
            p.wait("vector", "sPE")
            p.inc("vector", lambda e: e.tensor_copy(negm[:, 0:1], ps_m[:, 0:1]), "sV", 1)
          p.flush()

        oacc = s_o.enter_context(nc.sbuf_tensor("oacc_sb", [128, 8 * nq], BF16, side="right"))
        with ExitStack() as sp:
            SB4 = lambda n, sh, dt=BF16: sp.enter_context(nc.sbuf_tensor(n, sh, dt))
            PS4 = lambda n, sh, dt=F32: sp.enter_context(nc.psum_tensor(n, sh, dt))
            wuv = SB4("wuv_sb", [128, 4 * 1024])
            vg = SB4("vg_sb", [128, NKC * 130])
            eab = [SB4(f"eab{i}", [128, 1024]) for i in range(2)]
            rsc = SB4("rsc_sb", [1, 1024], F32)
            rbh = SB4("rbh_sb", [1, 1024])
            rlf = SB4("rlf_sb", [1, 1024], F32)
            rbl = SB4("rbl_sb", [1, 1024])
            rbs = SB4("rbs_sb", [128, 512], F32)
            qtl = [SB4(f"qtl{i}", [128, 512]) for i in range(2)]
            ps_ab = [PS4(f"ps_s{i}", [128, 1024]) for i in range(2)]
            ps_c4 = [PS4(f"ps_v{i}", [128, 512]) for i in range(2)]
            ps_oa = PS4("ps_oa", [128, 512]); ps_ob = PS4("ps_ob", [128, 512])
            for kc in range(4):
                ld_cast(wuv[:, kc * 1024:(kc + 1) * 1024], wuv_h[kc * 128:(kc + 1) * 128, :])
            p.wait("tensor", "sW")
            exp_tail = [0, 0]
            omm_tail = [0, 0]
            for g in range(8 if phases >= 4 else 0):
                p.wait("vector", "sPE")
                p.inc("vector", lambda e: e.memset(
                    vg[:, :].rearrange("p (t h c) -> p t h c", h=2, c=65)[:, :, :, 64:65], 1.0), "sV", 1)
                for tc in range(NKC):
                    pb = ps_c4[tc % 2]
                    p.bw("tensor", "Av", tc % 2)
                    for kc in range(4):
                        p.inc("tensor", lambda e, b=pb, k=kc, t_=tc, g_=g: e.matmul(
                            b[:, 0:128], ckv[:, k * S + t_ * 128:k * S + (t_ + 1) * 128],
                            wuv[:, k * 1024 + g_ * 128:k * 1024 + (g_ + 1) * 128],
                            start=(k == 0), stop=(k == 3)), "sPE", 1)
                    p.wait("vector", "sPE")
                    p.inc("vector", lambda e, b=pb, t_=tc: e.tensor_copy(
                        vg[:, t_ * 130:(t_ + 1) * 130].rearrange("p (h c) -> p h c", c=65)[:, :, 0:64],
                        b[:, 0:128].rearrange("p (h c) -> p h c", c=64)), "sV", 1)
                    p.bd("Av", tc % 2)
                for tt in range(NQT):
                    if fused:
                        qsrc = qtl[tt % 2]
                        p.wait("sync", "sPE")
                        p.inc("sync", lambda e, g_=g, t_=tt, d=qsrc: e.dma_start(
                            out=d[:, :], in_=qall_scr[:, g_ * nq + t_ * 512:g_ * nq + (t_ + 1) * 512]), "sX", 16)
                        p.wait("tensor", "sX")
                        qs0 = 0
                    else:
                        qsrc = qall
                        qs0 = g * nq + tt * 512
                    s_done = [0] * NKC
                    def emit_S(kt):
                        pb = ps_ab[kt % 2]
                        if kt == 0:
                            p.wait("tensor", "sV")
                        p.wait("tensor", "sSC", exp_tail[kt % 2])
                        p.inc("tensor", lambda e, b=pb, g_=g, k=kt, qv=qsrc, q0_=qs0: e.matmul(
                            b[:, 0:512], kall[0:64, g_ * S + k * 128:g_ * S + (k + 1) * 128],
                            qv[0:64, q0_:q0_ + 512],
                            start=True, stop=True, tile_position=(0, 0)), "sPE", 1)
                        p.inc("tensor", lambda e, b=pb, g_=g, k=kt, qv=qsrc, q0_=qs0: e.matmul(
                            b[:, 512:1024], kall[64:128, g_ * S + k * 128:g_ * S + (k + 1) * 128],
                            qv[64:128, q0_:q0_ + 512],
                            start=True, stop=True, tile_position=(64, 0)), "sPE", 1)
                        s_done[kt] = p.sems["sPE"].n
                    def emit_exp(kt):
                        pb = ps_ab[kt % 2]
                        p.wait("scalar", "sPE", s_done[kt])
                        p.wait("scalar", "sPE", omm_tail[kt % 2])
                        p.inc("scalar", lambda e, b=pb, i=kt % 2: e.activation(
                            eab[i][:, :], b[:, :], AF.Exp, bias=negm[:, 0:1]), "sSC", 1)
                        exp_tail[kt % 2] = p.sems["sSC"].n
                    def emit_O(kt):
                        p.wait("tensor", "sSC", exp_tail[kt % 2] if kt == NKC - 1 or True else 0)
                        p.inc("tensor", lambda e, k=kt, i=kt % 2: e.matmul(
                            ps_oa[0:65, 0:512], vg[:, k * 130:k * 130 + 65], eab[i][:, 0:512],
                            start=(k == 0), stop=(k == NKC - 1)), "sPE", 1)
                        p.inc("tensor", lambda e, k=kt, i=kt % 2: e.matmul(
                            ps_ob[0:65, 0:512], vg[:, k * 130 + 65:(k + 1) * 130], eab[i][:, 512:1024],
                            start=(k == 0), stop=(k == NKC - 1)), "sPE", 1)
                        omm_tail[kt % 2] = p.sems["sPE"].n
                    exp_save = {}
                    emit_S(0); emit_exp(0); exp_save[0] = exp_tail[0]
                    for kt in range(1, NKC):
                        emit_S(kt); emit_exp(kt); exp_save[kt] = exp_tail[kt % 2]
                        exp_tail_o = exp_tail[(kt - 1) % 2]
                        p.wait("tensor", "sSC", exp_save[kt - 1])
                        emit_O(kt - 1)
                    p.wait("tensor", "sSC", exp_save[NKC - 1])
                    emit_O(NKC - 1)
                    p.wait("vector", "sPE")
                    p.inc("vector", lambda e: e.reciprocal(rsc[0:1, 0:512], ps_oa[64:65, 0:512]), "sV", 1)
                    p.inc("vector", lambda e: e.reciprocal(rsc[0:1, 512:1024], ps_ob[64:65, 0:512]), "sV", 1)
                    p.inc("vector", lambda e: e.tensor_copy(rbh[0:1, :], rsc[0:1, :]), "sV", 1)
                    p.inc("vector", lambda e: e.tensor_sub(rlf[0:1, :], rsc[0:1, :], rbh[0:1, :]), "sV", 1)
                    p.inc("vector", lambda e: e.tensor_copy(rbl[0:1, :], rlf[0:1, :]), "sV", 1)
                    for half, (pso, c0) in enumerate(((ps_oa, 0), (ps_ob, 512))):
                        p.wait("tensor", "sV")
                        p.inc("tensor", lambda e, c=c0: e.matmul(
                            ps_ab[0][0:64, 0:512], ones1[0:1, 0:64], rbh[0:1, c:c + 512],
                            start=True, stop=False), "sPE", 1)
                        p.inc("tensor", lambda e, c=c0: e.matmul(
                            ps_ab[0][0:64, 0:512], ones1[0:1, 0:64], rbl[0:1, c:c + 512],
                            start=False, stop=True), "sPE", 1)
                        p.wait("vector", "sPE")
                        p.inc("vector", lambda e: e.tensor_copy(rbs[0:64, :], ps_ab[0][0:64, 0:512]), "sV", 1)
                        p.inc("vector", lambda e, g_=g, t_=tt, hb=half, po=pso: e.scalar_tensor_tensor(
                            oacc[hb * 64:(hb + 1) * 64, g_ * nq + t_ * 512:g_ * nq + (t_ + 1) * 512],
                            po[0:64, 0:512], 1.0, rbs[0:64, :], ALU.mult, ALU.mult), "sV", 1)
                    exp_tail = [p.sems["sSC"].n, p.sems["sSC"].n]
                    omm_tail = [p.sems["sPE"].n, p.sems["sPE"].n]
            p.flush()
        s_qk.close(); s_ckv.close()

        with ExitStack() as sp:
            SB5 = lambda n, sh, dt=F32: sp.enter_context(nc.sbuf_tensor(n, sh, dt))
            wout = sp.enter_context(nc.sbuf_tensor("wout_sb", [128, 8 * 1024], BF16))
            xqs = [SB5(f"xqs{i}", [128, D]) for i in range(2)]
            xh = SB5("xh_sb", [128, D])
            x1t = [SB5(f"x1t{i}", [128, D]) for i in range(2)]
            ps_c5 = [sp.enter_context(nc.psum_tensor(f"ps_w{i}", [128, 512], F32)) for i in range(2)]
            for kc in range(8):
                ld_cast(wout[:, kc * 1024:(kc + 1) * 1024], wout_h[kc * 128:(kc + 1) * 128, :])
            p.wait("tensor", "sW")
            for tc in range(NQC if phases >= 5 else 0):
                xb = xqs[tc % 2]
                ob = x1t[tc % 2]
                p.wait("sync", "sSC")
                p.inc("sync", lambda e, t_=tc, d=xb: e.dma_start(
                    out=d[:, :], in_=xq_h[t_ * 128:(t_ + 1) * 128, :]), "sX", 16)
                p.wait("scalar", "sX")
                p.wait("scalar", "sV")
                p.inc("scalar", lambda e, t_=tc, d=xb: e.activation(
                    xh[:, :], d[:, :], AF.Identity,
                    bias=qstat[:, 2 * t_ + 1:2 * t_ + 2], scale=qstat[:, 2 * t_:2 * t_ + 1]), "sSC", 1)
                for dt_ in range(2):
                    pb = ps_c5[dt_]
                    p.bw("tensor", "Aw", dt_)
                    p.wait("tensor", "sSC")
                    for kc in range(8):
                        p.inc("tensor", lambda e, b=pb, k=kc, t_=tc, d_=dt_: e.matmul(
                            b[:, 0:512], oacc[:, k * nq + t_ * 128:k * nq + (t_ + 1) * 128],
                            wout[:, k * 1024 + d_ * 512:k * 1024 + (d_ + 1) * 512],
                            start=(k == 0), stop=(k == 7)), "sPE", 1)
                    p.wait("vector", "sPE")
                    p.wait("vector", "sSC")
                    p.wait("vector", "sST", max(0, p.sems["sST"].n - 16))
                    p.inc("vector", lambda e, b=pb, d_=dt_, o=ob: e.tensor_add(
                        o[:, d_ * 512:(d_ + 1) * 512], b[:, 0:512], xh[:, d_ * 512:(d_ + 1) * 512]), "sV", 1)
                    p.bd("Aw", dt_)
                p.wait("sync", "sV")
                p.inc("sync", lambda e, t_=tc, o=ob: e.dma_start(
                    out=x1s_h[t_ * 128:(t_ + 1) * 128, :], in_=o[:, :]), "sST", 16)
            p.flush()
        s_o.close()

        with ExitStack() as sf:
            SBF = lambda n, sh, dt=BF16: sf.enter_context(nc.sbuf_tensor(n, sh, dt))
            hf = SBF("hf_sb", [128, 8 * 2048])
            act = SBF("act_sb", [128, 24 * 2048])
            wgu = [SBF(f"wgu{i}", [128, 2 * 1024]) for i in range(2)]

            for pass0 in range(0, nq if phases >= 6 else 0, 2048):
                ntok = min(2048, nq - pass0)
                NTT = ntok // 512
                NTC = ntok // 128
                with ExitStack() as sp:
                    SBP = lambda n, sh, dt=F32: sp.enter_context(nc.sbuf_tensor(n + str(pass0), sh, dt))
                    xsl = [SBP(f"fxsl{i}", [128, D]) for i in range(2)]
                    zsl = sp.enter_context(nc.sbuf_tensor(f"fzsl_sb{pass0}", [128, D], BF16))
                    stat = SBP("fstat_sb", [128, 8])
                    q2s = sp.enter_context(nc.sbuf_tensor(f"fq2s_sb{pass0}", [128, 1024], BF16))
                    ps_tf = [sp.enter_context(nc.psum_tensor(f"ps_tf{i}_{pass0}", [128, 512], BF16)) for i in range(2)]
                    for sl in range(NTC if phases >= 6 else 0):
                        xb = xsl[sl % 2]
                        p.wait("sync", "sSC")
                        p.inc("sync", lambda e, s_=sl, d=xb: e.dma_start(
                            out=d[:, :],
                            in_=x1s_h[(pass0 // 128 + s_) * 128:(pass0 // 128 + s_ + 1) * 128, :]), "sX", 16)
                        p.wait("scalar", "sX")
                        p.wait("scalar", "sV")
                        p.inc("scalar", lambda e, d=xb: e.activation(q2s[:, 0:1024], d[:, :], AF.Square,
                                                                     accum_out=stat[:, 1:2]), "sSC", 1)
                        p.wait("vector", "sSC")
                        p.inc("vector", lambda e, d=xb: e.reduce_sum(stat[:, 0:1], d[:, :], axis=AX.X), "sV", 1)
                        p.inc("vector", lambda e: e.tensor_scalar_mul(stat[:, 2:3], stat[:, 0:1], 1.0 / D), "sV", 1)
                        p.inc("vector", lambda e: e.tensor_mul(stat[:, 3:4], stat[:, 2:3], stat[:, 2:3]), "sV", 1)
                        p.inc("vector", lambda e: e.scalar_tensor_tensor(stat[:, 4:5], stat[:, 1:2], 1.0 / D,
                                                                         stat[:, 3:4], ALU.mult, ALU.subtract), "sV", 1)
                        p.wait("scalar", "sV")
                        p.inc("scalar", lambda e: e.activation(stat[:, 5:6], stat[:, 4:5], AF.Sqrt, bias=epsb[:, 0:1]), "sSC", 1)
                        p.wait("vector", "sSC")
                        p.inc("vector", lambda e: e.reciprocal(stat[:, 4:5], stat[:, 5:6]), "sV", 1)
                        p.inc("vector", lambda e: e.tensor_scalar(stat[:, 5:6], stat[:, 2:3], stat[:, 4:5], -1.0,
                                                                  ALU.mult, ALU.mult), "sV", 1)
                        p.wait("scalar", "sV")
                        p.wait("scalar", "sPE")
                        p.inc("scalar", lambda e, d=xb: e.activation(zsl[:, :], d[:, :], AF.Identity,
                                                                     bias=stat[:, 5:6], scale=stat[:, 4:5]), "sSC", 1)
                        p.wait("tensor", "sSC")
                        for kc in range(8):
                            pt = ps_tf[kc % 2]
                            p.wait("tensor", "sV")
                            p.inc("tensor", lambda e, k=kc, b=pt: e.transpose(
                                b[:, 0:128], zsl[:, k * 128:(k + 1) * 128],
                                ident[:, :]), "sPE", 1)
                            p.wait("vector", "sPE")
                            p.inc("vector", lambda e, k=kc, b=pt, s_=sl: e.tensor_copy(
                                hf[:, k * 2048 + s_ * 128:k * 2048 + (s_ + 1) * 128],
                                b[:, 0:128]), "sV", 1)
                    p.flush()

                with ExitStack() as sp:
                    atile = sp.enter_context(nc.sbuf_tensor(f"atile_sb{pass0}", [128, 512], BF16))
                    btile = sp.enter_context(nc.sbuf_tensor(f"btile_sb{pass0}", [128, 512], BF16))
                    ps_gu = [sp.enter_context(nc.psum_tensor(f"ps_gu{i}_{pass0}", [128, 1024], F32)) for i in range(2)]
                    for fc in range(24 if phases >= 7 else 0):
                        wb = wgu[fc % 2]
                        p.bw("gpsimd", "Agw", fc % 2, ("sPE",))
                        p.inc("gpsimd", lambda e, f_=fc, d=wb: e.dma_start(
                            out=d[:, 0:1024].rearrange("p (kc f) -> p kc f", f=128),
                            in_=wg_h[:, f_ * 128:(f_ + 1) * 128].rearrange("(kc pp) f -> pp kc f", pp=128)), "sW", 16)
                        p.inc("gpsimd", lambda e, f_=fc, d=wb: e.dma_start(
                            out=d[:, 1024:2048].rearrange("p (kc f) -> p kc f", f=128),
                            in_=wu_h[:, f_ * 128:(f_ + 1) * 128].rearrange("(kc pp) f -> pp kc f", pp=128)), "sW", 16)
                        p.wait("tensor", "sW")
                        for tt in range(NTT):
                            pb = ps_gu[tt % 2]
                            p.bw("tensor", "Agu", tt % 2, ("sV", "sSC"))
                            for kc in range(8):
                                p.inc("tensor", lambda e, b=pb, k=kc, d=wb, t_=tt: e.matmul(
                                    b[:, 0:512], d[:, k * 128:(k + 1) * 128],
                                    hf[:, k * 2048 + t_ * 512:k * 2048 + (t_ + 1) * 512],
                                    start=(k == 0), stop=(k == 7)), "sPE", 1)
                            for kc in range(8):
                                p.inc("tensor", lambda e, b=pb, k=kc, d=wb, t_=tt: e.matmul(
                                    b[:, 512:1024], d[:, 1024 + k * 128:1024 + (k + 1) * 128],
                                    hf[:, k * 2048 + t_ * 512:k * 2048 + (t_ + 1) * 512],
                                    start=(k == 0), stop=(k == 7)), "sPE", 1)
                            p.wait("scalar", "sPE")
                            p.wait("scalar", "sV")
                            p.inc("scalar", lambda e, b=pb: e.activation(
                                atile[:, :], b[:, 0:512], AF.Silu, scale=1.0 / 127.0), "sSC", 1)
                            p.wait("vector", "sPE")
                            p.inc("vector", lambda e, b=pb: e.tensor_scalar(
                                btile[:, :], b[:, 512:1024], 1.0 / 127.0, None, ALU.mult), "sV", 1)
                            p.wait("vector", "sSC")
                            p.inc("vector", lambda e, f_=fc, t_=tt: e.tensor_mul(
                                act[:, f_ * 2048 + t_ * 512:f_ * 2048 + (t_ + 1) * 512],
                                atile[:, :], btile[:, :]), "sV", 1)
                            p.bd("Agu", tt % 2, ("sV", "sSC"))
                        p.bd("Agw", fc % 2, ("sPE",))
                    p.flush()

                with ExitStack() as sp:
                    wdsb = sp.enter_context(nc.sbuf_tensor(f"wd_sb{pass0}", [128, 24 * 1024], BF16))
                    xsl2 = [sp.enter_context(nc.sbuf_tensor(f"cxsl{i}_{pass0}", [128, D], F32)) for i in range(2)]
                    x2t = [sp.enter_context(nc.sbuf_tensor(f"x2t{i}_{pass0}", [128, D], F32)) for i in range(2)]
                    ps_d = [sp.enter_context(nc.psum_tensor(f"ps_d{i}_{pass0}", [128, 512], F32)) for i in range(2)]
                    for fc in range(24):
                        p.inc("gpsimd", lambda e, f_=fc: e.dma_start(
                            out=wdsb[:, f_ * 1024:(f_ + 1) * 1024],
                            in_=wd_h[f_ * 128:(f_ + 1) * 128, :]), "sW", 16)
                    p.wait("tensor", "sW")
                    for tc in range(NTC if phases >= 8 else 0):
                        xb = xsl2[tc % 2]
                        ob = x2t[tc % 2]
                        p.wait("sync", "sV")
                        p.inc("sync", lambda e, t_=tc, d=xb: e.dma_start(
                            out=d[:, :],
                            in_=x1s_h[(pass0 // 128 + t_) * 128:(pass0 // 128 + t_ + 1) * 128, :]), "sX", 16)
                        for dt_ in range(2):
                            pb = ps_d[dt_]
                            p.bw("tensor", "Ad", dt_)
                            for fc in range(24):
                                p.inc("tensor", lambda e, b=pb, f_=fc, t_=tc, d_=dt_: e.matmul(
                                    b[:, 0:512], act[:, f_ * 2048 + t_ * 128:f_ * 2048 + (t_ + 1) * 128],
                                    wdsb[:, f_ * 1024 + d_ * 512:f_ * 1024 + (d_ + 1) * 512],
                                    start=(f_ == 0), stop=(f_ == 23)), "sPE", 1)
                            p.wait("vector", "sPE")
                            p.wait("vector", "sX")
                            p.wait("vector", "sST", max(0, p.sems["sST"].n - 16))
                            p.inc("vector", lambda e, b=pb, d_=dt_, o=ob: e.scalar_tensor_tensor(
                                o[:, d_ * 512:(d_ + 1) * 512], b[:, 0:512], 1.0 / 127.0,
                                xb[:, d_ * 512:(d_ + 1) * 512], ALU.mult, ALU.add), "sV", 1)
                            p.bd("Ad", dt_)
                        p.wait("sync", "sV")
                        p.inc("sync", lambda e, t_=tc, o=ob: e.dma_start(
                            out=x2_h[(pass0 // 128 + t_) * 128:(pass0 // 128 + t_ + 1) * 128, :],
                            in_=o[:, :]), "sST", 16)
                    p.flush()

        if not fused:
            p.wait("sync", "sST")
        p.flush()
    return nc


def _prep_A(inputs, nq=T):
    P1 = l0_qk_perm()
    P2 = np.empty(1024, np.int64)
    for h in range(16):
        P2[h * 64:h * 64 + 32] = [h * 64 + 2 * i + 1 for i in range(32)]
        P2[h * 64 + 32:h * 64 + 64] = [h * 64 + 2 * i for i in range(32)]
    f32 = lambda a: np.ascontiguousarray(a, dtype=np.float32)
    i32 = lambda a: np.ascontiguousarray(a, dtype=np.int32)
    wuq_t = np.asarray(inputs["L0_attn_w_up_q"]).T
    wuk_t = np.asarray(inputs["L0_attn_w_up_k"]).T
    shared = dict(
        wdq_t=f32(np.asarray(inputs["L0_attn_w_down_q"]).T),
        wdkv_t=f32(np.asarray(inputs["L0_attn_w_down_kv"]).T),
        wuq_tp=f32(wuq_t[:, P1]), wuq_tps=f32(wuq_t[:, P2]),
        wuk_tp=f32(wuk_t[:, P1]), wuk_tps=f32(wuk_t[:, P2]),
        wuv_t=f32(np.asarray(inputs["L0_attn_w_up_v"]).T),
        wout_t=f32(np.asarray(inputs["L0_attn_w_out"]).T),
        gamma8=f32(np.asarray(inputs["L0_attn_ln_scale"]).reshape(8, 128).T),
        beta8=f32(np.asarray(inputs["L0_attn_ln_bias"]).reshape(8, 128).T),
        ind_a=f32(np.r_[np.ones(64), np.zeros(64)].reshape(128, 1)),
        ind_b=f32(np.r_[np.zeros(64), np.ones(64)].reshape(128, 1)),
        wg_t=i32(np.asarray(inputs["L0_ffn_w_gate"]).T),
        wu_t=i32(np.asarray(inputs["L0_ffn_w_up"]).T),
        wd_t=i32(np.asarray(inputs["L0_ffn_w_down"]).T),
    )
    kcs, kss = l0_rope_tiles(np.arange(S), 1.0)
    x_full = np.ascontiguousarray(np.asarray(inputs["L0_attn_x"], dtype=np.float32))
    in_maps = []
    for c in range(8):
        b, hh = c // 2, c % 2
        if nq == T:
            qpos = np.arange(hh * T, (hh + 1) * T)
            xq = x_full[b, hh * T:(hh + 1) * T]
        else:
            qpos = np.arange(S)
            xq = x_full[b]
        qcs, qss = l0_rope_tiles(qpos, 1.0 / 8.0)
        m = dict(shared)
        m.update(x=x_full[b], xq=np.ascontiguousarray(xq),
                 qcs=qcs, qss=qss, kcs=kcs, kss=kss)
        in_maps.append(m)
    return in_maps


def build_B(ctx=None):
    fused = ctx is not None
    if fused:
        nc = ctx[0]
    else:
        nc = bass.Bass("TRN2", target_bir_lowering=False)
    ein = lambda n, sh, dt=F32: nc.dram_tensor(n, sh, dt, kind="ExternalInput").ap()
    x2f_h = ctx[3] if fused else ein("x2f", [S, D])
    wqkv_h = ein("wqkvT", [D, 3072])
    bq24_h = ein("bq24", [128, 24])
    bqrow_h = ein("bq_row", [1, 3072])
    proj_h = ein("proj_dup", [128, 1024])
    tri_h = ein("mask_tri", [128, 128])
    woutT_h = ein("l1a_woutT", [D, D])
    bout_h = ein("l1a_bout", [1, D])
    if fused:
        x3_h = ctx[5]
        ctx_scr = nc.dram_tensor("bctx_scr", [128, 8 * S], BF16,
                                 kind="ExternalOutput" if ctx[6] else "Internal").ap()
        NOWN = S
        FIRST_OWN = 0
    else:
        x3_h = nc.dram_tensor("x3", [T, D], F32, kind="ExternalOutput").ap()
        ctx_scr = None
        NOWN = T
        FIRST_OWN = 16

    NTC = S // 128
    DELTA = 1e-6

    with ExitStack() as gst:
        if fused:
            p = ctx[1]
            g = ctx[4]
        else:
            p = Prog(nc, gst)
            g = mk_globals(nc, p, gst)
        ident = g["ident"]; ones1 = g["ones1"]; epsb = g["epsb"]
        SBg = lambda n, sh, dt=BF16: gst.enter_context(nc.sbuf_tensor(n, sh, dt))
        tri = SBg("tri_sb", [128, 128], F32)
        proj = SBg("proj_sb", [128, 1024])
        bq24 = SBg("bq24_sb", [128, 24], F32)
        bqrow = SBg("bqrow_sb", [1, 3072])
        p.inc("gpsimd", lambda e: e.dma_start(out=proj[:, :], in_=proj_h[:, :]), "sW", 16)
        p.inc("gpsimd", lambda e: e.dma_start(out=bqrow[:, :], in_=bqrow_h[:, :]), "sW", 16)
        p.inc("sync", lambda e: e.dma_start(out=tri[:, :], in_=tri_h[:, :]), "sX", 16)
        p.inc("sync", lambda e: e.dma_start(out=bq24[:, :], in_=bq24_h[:, :]), "sX", 16)
        p.flush()

        s_y = ExitStack()
        s_ctx = ExitStack()
        y = s_y.enter_context(nc.sbuf_tensor("y_sb", [128, 8 * S], BF16))
        if fused and ctx[6]:
            ydbg_h = nc.dram_tensor("by_dbg", [128, 8 * S], BF16, kind="ExternalOutput").ap()
        else:
            ydbg_h = None
        if fused:
            ctxf = None
        else:
            ctxf = s_ctx.enter_context(nc.sbuf_tensor("ctxf_sb", [128, 8 * T], BF16, side="right"))

        with ExitStack() as sp:
            SBP = lambda n, sh, dt=F32: sp.enter_context(nc.sbuf_tensor(n, sh, dt))
            xsl = [SBP(f"bxsl{i}", [128, D]) for i in range(2)]
            zsl = sp.enter_context(nc.sbuf_tensor("bzsl_sb", [128, D], BF16))
            stat = SBP("bstat_sb", [128, 8])
            q2s = sp.enter_context(nc.sbuf_tensor("bq2s_sb", [128, 1024], BF16))
            ps_t = [sp.enter_context(nc.psum_tensor(f"bps_t{i}", [128, 512], BF16)) for i in range(2)]
            for sl in range(NTC):
                xb = xsl[sl % 2]
                p.wait("sync", "sSC")
                p.inc("sync", lambda e, s_=sl, d=xb: e.dma_start(
                    out=d[:, :], in_=x2f_h[s_ * 128:(s_ + 1) * 128, :]), "sX", 16)
                p.wait("scalar", "sX")
                p.wait("scalar", "sV")
                p.inc("scalar", lambda e, d=xb: e.activation(q2s[:, 0:1024], d[:, :], AF.Square,
                                                             accum_out=stat[:, 1:2]), "sSC", 1)
                p.wait("vector", "sSC")
                p.inc("vector", lambda e, d=xb: e.reduce_sum(stat[:, 0:1], d[:, :], axis=AX.X), "sV", 1)
                p.inc("vector", lambda e: e.tensor_scalar_mul(stat[:, 2:3], stat[:, 0:1], 1.0 / D), "sV", 1)
                p.inc("vector", lambda e: e.tensor_mul(stat[:, 3:4], stat[:, 2:3], stat[:, 2:3]), "sV", 1)
                p.inc("vector", lambda e: e.scalar_tensor_tensor(stat[:, 4:5], stat[:, 1:2], 1.0 / D,
                                                                 stat[:, 3:4], ALU.mult, ALU.subtract), "sV", 1)
                p.wait("scalar", "sV")
                p.inc("scalar", lambda e: e.activation(stat[:, 5:6], stat[:, 4:5], AF.Sqrt, bias=epsb[:, 0:1]), "sSC", 1)
                p.wait("vector", "sSC")
                p.inc("vector", lambda e: e.reciprocal(stat[:, 4:5], stat[:, 5:6]), "sV", 1)
                p.inc("vector", lambda e: e.tensor_scalar(stat[:, 5:6], stat[:, 2:3], stat[:, 4:5], -1.0,
                                                          ALU.mult, ALU.mult), "sV", 1)
                p.wait("scalar", "sV")
                p.wait("scalar", "sPE")
                p.inc("scalar", lambda e, d=xb: e.activation(zsl[:, :], d[:, :], AF.Identity,
                                                             bias=stat[:, 5:6], scale=stat[:, 4:5]), "sSC", 1)
                p.wait("tensor", "sSC")
                for kc in range(8):
                    pt = ps_t[kc % 2]
                    p.wait("tensor", "sV")
                    p.inc("tensor", lambda e, k=kc, b=pt: e.transpose(
                        b[:, 0:128], zsl[:, k * 128:(k + 1) * 128], ident[:, :]), "sPE", 1)
                    p.wait("vector", "sPE")
                    p.inc("vector", lambda e, k=kc, b=pt, s_=sl: e.tensor_copy(
                        y[:, k * S + s_ * 128:k * S + (s_ + 1) * 128], b[:, 0:128]), "sV", 1)
            if ydbg_h is not None:
                p.wait("sync", "sV")
                p.inc("sync", lambda e: e.dma_start(out=ydbg_h[:, :], in_=y[:, :]), "sST", 16)
            p.flush()

        for g4 in range(4):
            with ExitStack() as sp:
                tg = str(g4)
                SB2 = lambda n, sh, dt=BF16: sp.enter_context(nc.sbuf_tensor(n + tg, sh, dt))
                PS2 = lambda n, sh, dt=F32: sp.enter_context(nc.psum_tensor(n + tg, sh, dt))
                wq = SB2("wq_sb", [128, 8 * 256]); wk = SB2("wk_sb", [128, 8 * 256]); wv = SB2("wv_sb", [128, 8 * 256])
                qf = SB2("qf_sb", [128, 2 * NOWN])
                kf = SB2("kf_sb", [128, 2 * S])
                vt = SB2("vt_sb", [128, NTC * 260])
                qp = SB2("qp_sb", [128, 2 * NOWN])
                kpf = SB2("kpf_sb", [128, 2 * S])
                kpt = SB2("kpt_sb", [128, NTC * 256])
                stf = SB2("stf_sb", [128, 2 * 65], F32)
                stb = SB2("stb_sb", [128, 2 * 65])
                awm = SB2("awm_sb", [128, 2 * 128])
                ctm = SB2("ctm_sb", [128, 128])
                cto = SB2("cto_sb", [128, 128])
                rnv = SB2("rnv_sb", [128, 2], F32)
                ps_a = [PS2(f"ps_a{i}", [128, 512]) for i in range(2)]
                ps_ct = [PS2(f"ps_ct{i}", [128, 512]) for i in range(2)]
                ps_cb = [PS2(f"ps_cb{i}", [128, 512]) for i in range(2)]
                ps_tr = [PS2(f"ps_tr{i}", [128, 1024], BF16) for i in range(2)]

                for kc in range(8):
                    for wsb, c0 in ((wq, g4 * 256), (wk, 1024 + g4 * 256), (wv, 2048 + g4 * 256)):
                        p.inc("gpsimd", lambda e, d=wsb, k=kc, c=c0: e.dma_start(
                            out=d[:, k * 256:(k + 1) * 256],
                            in_=wqkv_h[k * 128:(k + 1) * 128, c:c + 256]), "sW", 16)
                p.wait("tensor", "sW"); p.wait("scalar", "sX")

                def qk_proj(dst, n_toks, wsb, bias_base, tok0):
                    for j in range(2):
                        for tt in range(n_toks // 512):
                            pb = ps_a[tt % 2]
                            p.bw("tensor", "Bqk", tt % 2, ("sSC",))
                            for kc in range(8):
                                p.inc("tensor", lambda e, b=pb, k=kc, j_=j, t_=tt: e.matmul(
                                    b[:, 0:512], wsb[:, k * 256 + j_ * 128:k * 256 + (j_ + 1) * 128],
                                    y[:, k * S + tok0 + t_ * 512:k * S + tok0 + (t_ + 1) * 512],
                                    start=(k == 0), stop=(k == 7)), "sPE", 1)
                            p.wait("scalar", "sPE")
                            p.inc("scalar", lambda e, b=pb, j_=j, t_=tt: e.activation(
                                dst[:, j_ * n_toks + t_ * 512:j_ * n_toks + (t_ + 1) * 512],
                                b[:, 0:512], AF.Identity,
                                bias=bq24[:, bias_base + g4 * 2 + j_:bias_base + g4 * 2 + j_ + 1]), "sSC", 1)
                            p.bd("Bqk", tt % 2, ("sSC",))
                qk_proj(qf, NOWN, wq, 0, FIRST_OWN * 128)
                qk_proj(kf, S, wk, 8, 0)

                p.wait("vector", "sPE")
                p.inc("vector", lambda e: e.memset(
                    vt[:, :].rearrange("p (t h c) -> p t h c", h=4, c=65)[:, :, :, 64:65], 1.0), "sV", 1)
                for tc in range(NTC):
                    pb = ps_cb[tc % 2]
                    p.bw("tensor", "Bv", tc % 2)
                    for kc in range(8):
                        p.inc("tensor", lambda e, b=pb, k=kc, t_=tc: e.matmul(
                            b[:, 0:256], y[:, k * S + t_ * 128:k * S + (t_ + 1) * 128],
                            wv[:, k * 256:(k + 1) * 256],
                            start=(k == 0), stop=False), "sPE", 1)
                    p.inc("tensor", lambda e, b=pb: e.matmul(
                        b[:, 0:256], ones1[0:1, 0:128], bqrow[0:1, 2048 + g4 * 256:2048 + (g4 + 1) * 256],
                        start=False, stop=True), "sPE", 1)
                    p.wait("vector", "sPE")
                    p.inc("vector", lambda e, b=pb, t_=tc: e.tensor_copy(
                        vt[:, t_ * 260:(t_ + 1) * 260].rearrange("p (h c) -> p h c", c=65)[:, :, 0:64],
                        b[:, 0:256].rearrange("p (h c) -> p h c", c=64)), "sV", 1)
                    p.bd("Bv", tc % 2)

                def p_proj(dst, srcf, n_toks):
                    for j in range(2):
                        he = (4 * g4 + 2 * j) * 64
                        ho = (4 * g4 + 2 * j + 1) * 64
                        for tt in range(n_toks // 512):
                            pa = ps_a[0]; pb2 = ps_a[1]
                            p.wait("tensor", "sSC")
                            p.inc("tensor", lambda e, j_=j, t_=tt, c=he: e.matmul(
                                pa[0:64, 0:512], proj[0:64, c:c + 64],
                                srcf[0:64, j_ * n_toks + t_ * 512:j_ * n_toks + (t_ + 1) * 512],
                                start=True, stop=True, tile_position=(0, 0)), "sPE", 1)
                            p.inc("tensor", lambda e, j_=j, t_=tt, c=ho: e.matmul(
                                pb2[0:64, 0:512], proj[64:128, c:c + 64],
                                srcf[64:128, j_ * n_toks + t_ * 512:j_ * n_toks + (t_ + 1) * 512],
                                start=True, stop=True, tile_position=(64, 0)), "sPE", 1)
                            p.wait("scalar", "sPE")
                            p.inc("scalar", lambda e, j_=j, t_=tt: e.activation(
                                dst[0:64, j_ * n_toks + t_ * 512:j_ * n_toks + (t_ + 1) * 512],
                                pa[0:64, 0:512], AF.Relu), "sSC", 1)
                            p.inc("scalar", lambda e, j_=j, t_=tt: e.activation(
                                dst[64:128, j_ * n_toks + t_ * 512:j_ * n_toks + (t_ + 1) * 512],
                                pb2[0:64, 0:512], AF.Relu), "sSC", 1)
                p_proj(qp, qf, NOWN)
                p_proj(kpf, kf, S)
                if fused and ctx[6] and g4 == 2:
                    dap2 = nc.dram_tensor("bqp_early", [128, 2 * NOWN], BF16, kind="ExternalOutput").ap()
                    p.wait("sync", "sSC")
                    p.inc("sync", lambda e, d=dap2: e.dma_start(out=d[:, :], in_=qp[:, :]), "sST", 16)

                for tc in range(NTC):
                    for h4 in range(4):
                        pb = ps_cb[h4 % 2]
                        hb = (h4 % 2) * 64
                        jj = h4 // 2
                        p.bw("tensor", "Bkpt", h4 % 2, ("sSC",))
                        p.inc("tensor", lambda e, t_=tc, h_=h4, hb_=hb, j_=jj, b=pb: e.matmul(
                            b[:, 0:64], kf[hb_:hb_ + 64, j_ * S + t_ * 128:j_ * S + (t_ + 1) * 128],
                            proj[hb_:hb_ + 64, (4 * g4 + h_) * 64:(4 * g4 + h_ + 1) * 64],
                            start=True, stop=True, tile_position=(hb_, 0)), "sPE", 1)
                        p.wait("scalar", "sPE")
                        p.inc("scalar", lambda e, t_=tc, h_=h4, b=pb: e.activation(
                            kpt[:, t_ * 256 + h_ * 64:t_ * 256 + (h_ + 1) * 64],
                            b[:, 0:64], AF.Relu), "sSC", 1)
                        p.bd("Bkpt", h4 % 2, ("sSC",))

                p.wait("vector", "sSC")
                p.inc("vector", lambda e: e.memset(stf[:, :], 0.0), "sV", 1)
                p.inc("vector", lambda e: e.memset(stb[:, :], 0.0), "sV", 1)
                for c in range(NTC):
                    own = c >= FIRST_OWN
                    if own:
                        for j in range(2):
                            for par in range(2):
                                h4 = 2 * j + par
                                hb = par * 64
                                pb = ps_a[par]
                                p.wait("tensor", "sV")
                                p.inc("tensor", lambda e, h_=h4, hb_=hb, j_=j, c_=c, b=pb: e.matmul(
                                    b[:, 0:128], kpf[hb_:hb_ + 64, j_ * S + c_ * 128:j_ * S + (c_ + 1) * 128],
                                    qp[hb_:hb_ + 64, j_ * NOWN + (c_ - FIRST_OWN) * 128:j_ * NOWN + (c_ - FIRST_OWN + 1) * 128],
                                    start=True, stop=True, tile_position=(hb_, 0)), "sPE", 1)
                            p.wait("vector", "sPE")
                            for par in range(2):
                                p.inc("vector", lambda e, par_=par: e.tensor_mul(
                                    awm[:, par_ * 128:(par_ + 1) * 128], ps_a[par_][:, 0:128], tri[:, :]), "sV", 1)
                            for par in range(2):
                                h4 = 2 * j + par
                                hb = par * 64
                                pc = ps_ct[par]
                                p.wait("tensor", "sV")
                                p.inc("tensor", lambda e, par_=par, h_=h4, c_=c, b=pc: e.matmul(
                                    b[:, 0:65], awm[:, par_ * 128:(par_ + 1) * 128],
                                    vt[:, c_ * 260 + h_ * 65:c_ * 260 + (h_ + 1) * 65],
                                    start=True, stop=False), "sPE", 1)
                                p.inc("tensor", lambda e, par_=par, hb_=hb, j_=j, c_=c, b=pc: e.matmul(
                                    b[:, 0:65], qp[hb_:hb_ + 64, j_ * NOWN + (c_ - FIRST_OWN) * 128:j_ * NOWN + (c_ - FIRST_OWN + 1) * 128],
                                    stb[hb_:hb_ + 64, j_ * 65:(j_ + 1) * 65],
                                    start=False, stop=True, tile_position=(hb_, 0)), "sPE", 1)
                            p.wait("vector", "sPE")
                            for par in range(2):
                                p.inc("vector", lambda e, par_=par, b=ps_ct[par]: e.tensor_scalar_add(
                                    rnv[:, par_:par_ + 1], b[:, 64:65], DELTA), "sV", 1)
                            p.inc("vector", lambda e: e.reciprocal(rnv[:, 0:2], rnv[:, 0:2]), "sV", 1)
                            for par in range(2):
                                p.inc("vector", lambda e, par_=par, b=ps_ct[par]: e.tensor_scalar(
                                    ctm[:, par_ * 64:(par_ + 1) * 64], b[:, 0:64],
                                    rnv[:, par_:par_ + 1], None, ALU.mult), "sV", 1)
                            pt = ps_tr[c % 2]
                            p.wait("tensor", "sV")
                            p.inc("tensor", lambda e, b=pt: e.transpose(
                                b[:, 0:128], ctm[:, :], ident[:, :]), "sPE", 1)
                            p.wait("vector", "sPE")
                            if fused:
                                p.wait("vector", "sST")
                                p.inc("vector", lambda e, b=pt: e.tensor_copy(cto[:, :], b[:, 0:128]), "sV", 1)
                                p.wait("sync", "sV")
                                p.inc("sync", lambda e, j_=j, c_=c: e.dma_start(
                                    out=ctx_scr[:, (2 * g4 + j_) * NOWN + (c_ - FIRST_OWN) * 128:
                                                (2 * g4 + j_) * NOWN + (c_ - FIRST_OWN + 1) * 128],
                                    in_=cto[:, :]), "sST", 16)
                            else:
                                p.inc("vector", lambda e, j_=j, c_=c, b=pt: e.tensor_copy(
                                    ctxf[:, (2 * g4 + j_) * T + (c_ - 16) * 128:(2 * g4 + j_) * T + (c_ - 15) * 128],
                                    b[:, 0:128]), "sV", 1)
                    for h4 in range(4):
                        pb = ps_cb[h4 % 2]
                        hb = (h4 % 2) * 64
                        jj = h4 // 2
                        p.wait("tensor", "sV")
                        p.inc("tensor", lambda e, t_=c, h_=h4, b=pb: e.matmul(
                            b[0:64, 0:65], kpt[:, t_ * 256 + h_ * 64:t_ * 256 + (h_ + 1) * 64],
                            vt[:, t_ * 260 + h_ * 65:t_ * 260 + (h_ + 1) * 65],
                            start=True, stop=True), "sPE", 1)
                        p.wait("vector", "sPE")
                        p.inc("vector", lambda e, hb_=hb, j_=jj, b=pb: e.tensor_add(
                            stf[hb_:hb_ + 64, j_ * 65:(j_ + 1) * 65],
                            stf[hb_:hb_ + 64, j_ * 65:(j_ + 1) * 65], b[0:64, 0:65]), "sV", 1)
                        p.inc("vector", lambda e, hb_=hb, j_=jj: e.tensor_copy(
                            stb[hb_:hb_ + 64, j_ * 65:(j_ + 1) * 65],
                            stf[hb_:hb_ + 64, j_ * 65:(j_ + 1) * 65]), "sV", 1)
                if fused and ctx[6] and g4 == 2:
                    for nm, buf in (("bqp_dbg", qp), ("bkpt_dbg", kpt), ("bvt_dbg", vt), ("bkpf_dbg", kpf), ("bqf_dbg", qf), ("bkf_dbg", kf)):
                        dap = nc.dram_tensor(nm, list(buf.shape), BF16, kind="ExternalOutput").ap()
                        p.wait("sync", "sV")
                        p.inc("sync", lambda e, d=dap, b=buf: e.dma_start(out=d[:, :], in_=b[:, :]), "sST", 16)
                    p.wait("sync", "sST")
                p.flush()
        s_y.close()

        with ExitStack() as sp:
            SB3 = lambda n, sh, dt=F32: sp.enter_context(nc.sbuf_tensor(n, sh, dt))
            wo = sp.enter_context(nc.sbuf_tensor("bwo_sb", [128, 8 * 1024], BF16))
            bo = sp.enter_context(nc.sbuf_tensor("bbo_sb", [1, 1024], BF16))
            xsl2 = [SB3(f"b3xsl{i}", [128, D]) for i in range(2)]
            x3t = [SB3(f"x3t{i}", [128, D]) for i in range(2)]
            ps_w = [sp.enter_context(nc.psum_tensor(f"bps_w{i}", [128, 512], F32)) for i in range(2)]
            ctl = [sp.enter_context(nc.sbuf_tensor(f"bctl{i}", [128, 8 * 128], BF16)) for i in range(2)]
            for kc in range(8):
                p.inc("gpsimd", lambda e, k=kc: e.dma_start(
                    out=wo[:, k * 1024:(k + 1) * 1024], in_=woutT_h[k * 128:(k + 1) * 128, :]), "sW", 16)
            p.inc("gpsimd", lambda e: e.dma_start(out=bo[:, :], in_=bout_h[:, :]), "sW", 16)
            p.wait("tensor", "sW")
            for tc in range(NOWN // 128):
                xb = xsl2[tc % 2]
                ob = x3t[tc % 2]
                p.wait("sync", "sV")
                x2row0 = (FIRST_OWN * 128) + tc * 128
                p.inc("sync", lambda e, r_=x2row0, d=xb: e.dma_start(
                    out=d[:, :], in_=x2f_h[r_:r_ + 128, :]), "sX", 16)
                if fused:
                    cb = ctl[tc % 2]
                    p.wait("sync", "sPE")
                    p.inc("sync", lambda e, t_=tc, d=cb: e.dma_start(
                        out=d[:, :].rearrange("p (kc n) -> p kc n", n=128),
                        in_=ctx_scr[:, :].rearrange("p (kc n) -> p kc n", n=NOWN)[
                            :, :, t_ * 128:(t_ + 1) * 128]), "sX", 16)
                    p.wait("tensor", "sX")
                for dt_ in range(2):
                    pb = ps_w[dt_]
                    p.bw("tensor", "Bwo", dt_)
                    for kc in range(8):
                        if fused:
                            cb = ctl[tc % 2]
                            p.inc("tensor", lambda e, b=pb, k=kc, d_=dt_, c_=cb: e.matmul(
                                b[:, 0:512], c_[:, k * 128:(k + 1) * 128],
                                wo[:, k * 1024 + d_ * 512:k * 1024 + (d_ + 1) * 512],
                                start=(k == 0), stop=False), "sPE", 1)
                        else:
                            p.inc("tensor", lambda e, b=pb, k=kc, t_=tc, d_=dt_: e.matmul(
                                b[:, 0:512], ctxf[:, k * T + t_ * 128:k * T + (t_ + 1) * 128],
                                wo[:, k * 1024 + d_ * 512:k * 1024 + (d_ + 1) * 512],
                                start=(k == 0), stop=False), "sPE", 1)
                    p.inc("tensor", lambda e, b=pb, d_=dt_: e.matmul(
                        b[:, 0:512], ones1[0:1, 0:128], bo[0:1, d_ * 512:(d_ + 1) * 512],
                        start=False, stop=True), "sPE", 1)
                    p.wait("vector", "sPE")
                    p.wait("vector", "sX")
                    p.wait("vector", "sST", max(0, p.sems["sST"].n - 16))
                    p.inc("vector", lambda e, b=pb, d_=dt_, o=ob, d2=xb: e.tensor_add(
                        o[:, d_ * 512:(d_ + 1) * 512], b[:, 0:512], d2[:, d_ * 512:(d_ + 1) * 512]), "sV", 1)
                    p.bd("Bwo", dt_)
                p.wait("sync", "sV")
                p.inc("sync", lambda e, t_=tc, o=ob: e.dma_start(
                    out=x3_h[t_ * 128:(t_ + 1) * 128, :], in_=o[:, :]), "sST", 16)
            p.flush()
        s_ctx.close()
        if not fused:
            p.wait("sync", "sST")
        p.flush()
    return nc


def _prep_B(inputs, x2_parts):
    f32 = lambda a: np.ascontiguousarray(a, dtype=np.float32)
    proj = np.asarray(inputs["L1_attn_proj"], dtype=np.float32)
    proj_dup = np.zeros((128, 1024), np.float32)
    for h in range(16):
        proj_dup[0:64, h * 64:(h + 1) * 64] = proj[h]
        proj_dup[64:128, h * 64:(h + 1) * 64] = proj[h]
    tri = np.tril(np.ones((128, 128), np.float32)).T.copy()
    shared = dict(
        wqkvT=f32(np.asarray(inputs["L1_attn_Wqkv"]).T),
        bq24=f32(np.asarray(inputs["L1_attn_bqkv"]).reshape(24, 128).T),
        bq_row=f32(np.asarray(inputs["L1_attn_bqkv"]).reshape(1, 3072)),
        proj_dup=proj_dup,
        mask_tri=tri,
        l1a_woutT=f32(np.asarray(inputs["L1_attn_Wout"]).T),
        l1a_bout=f32(np.asarray(inputs["L1_attn_bout"]).reshape(1, D)),
    )
    in_maps = []
    for c in range(8):
        b, hh = c // 2, c % 2
        x2f = np.zeros((S, D), np.float32)
        if hh == 1:
            x2f[0:T] = x2_parts[2 * b]
        x2f[T:S] = x2_parts[c]
        m = dict(shared); m.update(x2f=x2f)
        in_maps.append(m)
    return in_maps


EC = 4608
ECP = EC + 1
HID = 2730
NFC = 22

def _fcw(i):
    return 128 if i < 21 else 42

def xpos_tables(pos, is_q):
    n = len(pos)
    half = np.arange(0, 64, 2, dtype=np.float32)
    inv = 1.0 / 10000.0 ** (half / 64)
    t = np.asarray(pos, dtype=np.float32)
    fr = t[:, None] * inv[None, :]
    base = (half + 0.4 * 64) / (1.4 * 64)
    power = (t - S // 2) / float(512 // 2)
    sc = base[None, :] ** power[:, None]
    if not is_q:
        sc = 1.0 / sc
    cos = np.cos(fr) * sc
    sin = np.sin(fr) * sc
    scale = 8.0 if is_q else 1.0
    CS = np.empty((128, n), np.float32)
    SS = np.empty((128, n), np.float32)
    for b0 in (0, 64):
        CS[b0:b0 + 32] = cos.T; CS[b0 + 32:b0 + 64] = cos.T
        SS[b0:b0 + 32] = sin.T; SS[b0 + 32:b0 + 64] = -sin.T
    return (CS * scale).astype(np.float32), (SS * scale).astype(np.float32)


def build_C(ctx=None):
    fused = ctx is not None
    if fused:
        nc = ctx[0]
    else:
        nc = bass.Bass("TRN2", target_bir_lowering=False)
    ein = lambda n, sh, dt=F32: nc.dram_tensor(n, sh, dt, kind="ExternalInput").ap()
    if fused:
        x3scr_h = ctx[5]
        x3e_h = None
        x3own_h = None
    else:
        x3scr_h = None
        x3e_h = ein("x3e", [EC, D])
        x3own_h = ein("x3own", [T, D])
    wqkvT_h = [ein(f"f_wqkvT{l}", [D, 1536]) for l in range(2)]
    woT_h = [ein(f"f_woT{l}", [512, D]) for l in range(2)]
    w1T_h = [ein(f"f_w1T{l}", [D, 2 * HID]) for l in range(2)]
    b1c_h = [ein(f"f_b1c{l}", [128, 2 * NFC]) for l in range(2)]
    w2T_h = [ein(f"f_w2T{l}", [HID, D]) for l in range(2)]
    b2r_h = [ein(f"f_b2r{l}", [1, D]) for l in range(2)]
    lg8_h = [ein(f"f_lg8{l}", [128, 8]) for l in range(2)]
    lb8_h = [ein(f"f_lb8{l}", [128, 8]) for l in range(2)]
    qs2_h = [ein(f"f_qs2{l}", [128, 128]) for l in range(2)]
    ks2_h = [ein(f"f_ks2{l}", [128, 128]) for l in range(2)]
    qsr_h = [ein(f"f_qsr{l}", [1, 64]) for l in range(2)]
    ksr_h = [ein(f"f_ksr{l}", [1, 64]) for l in range(2)]
    qcs_h = [ein(f"f_qcs{l}", [128, EC]) for l in range(2)]
    qss_h = [ein(f"f_qss{l}", [128, EC]) for l in range(2)]
    kcs_h = [ein(f"f_kcs{l}", [128, EC]) for l in range(2)]
    kss_h = [ein(f"f_kss{l}", [128, EC]) for l in range(2)]
    pmk_h = ein("f_pmk", [128, 16])
    trim_h = ein("f_trim", [128, 8 * 512])
    trio_h = ein("f_trio", [128, 8 * 512])
    out_h = nc.dram_tensor("out", [S if fused else T, D], F32, kind="ExternalOutput").ap()
    xf = [nc.dram_tensor(f"c_xf{i}", [ECP, D], F32, kind="Internal").ap() for i in range(2)]
    xb0 = nc.dram_tensor("c_xb0", [128, 8 * ECP], BF16, kind="Internal").ap()

    if fused:
        Q0 = [512, 512]; NQ = [4096, 4096]
        KV0 = [0, 0]; NKV = [4608, 4608]
        F0 = [512, 512]; NF = [4096, 4096]
    else:
        Q0 = [1024, 2048]; NQ = [3584, 2560]
        KV0 = [0, 1536]; NKV = [4608, 3072]
        F0 = [1024, 2560]; NF = [3584, 2048]
    OWN0 = 2560

    with ExitStack() as gst:
        if fused:
            p = ctx[1]
            g = ctx[4]
        else:
            p = Prog(nc, gst)
            g = mk_globals(nc, p, gst)
        ident = g["ident"]; ones1 = g["ones1"]; epsb = g["epsb"]
        SBg = lambda n, sh, dt=BF16: gst.enter_context(nc.sbuf_tensor(n, sh, dt))
        negm = SBg("c_negm_sb", [128, 1], F32)
        negmp = SBg("negmp_sb", [128, 16], F32)
        pmk = SBg("pmk_sb", [128, 16], F32)
        trim = SBg("trim_sb", [128, 8 * 512], F32)
        trio = SBg("trio_sb", [128, 8 * 512], F32)
        mscr = SBg("mscr_sb", [1, 8], F32)
        ind2 = SBg("ind2_sb", [128, 2])
        p.inc("sync", lambda e: e.dma_start(out=trim[:, :], in_=trim_h[:, :]), "sX", 16)
        p.inc("sync", lambda e: e.dma_start(out=trio[:, :], in_=trio_h[:, :]), "sX", 16)
        p.inc("sync", lambda e: e.dma_start(out=pmk[:, :], in_=pmk_h[:, :]), "sX", 16)
        p.inc("vector", lambda e: e.memset(ind2[:, :], 0.0), "sV", 1)
        p.inc("vector", lambda e: e.memset(ind2[0:64, 0:1], 1.0), "sV", 1)
        p.inc("vector", lambda e: e.memset(ind2[64:128, 1:2], 1.0), "sV", 1)
        p.flush()

        with ExitStack() as sp:
            zslab = sp.enter_context(nc.sbuf_tensor("zslab_sb", [128, D], F32))
            zslab_b = sp.enter_context(nc.sbuf_tensor("zslabb_sb", [128, 8], BF16))
            p.inc("vector", lambda e: e.memset(zslab[:, :], 0.0), "sV", 1)
            p.inc("vector", lambda e: e.memset(zslab_b[:, :], 0.0), "sV", 1)
            p.wait("sync", "sV")
            p.inc("sync", lambda e: e.dma_start(out=xf[0][0:1, :], in_=zslab[0:1, :]), "sST", 16)
            p.inc("sync", lambda e: e.dma_start(out=xf[1][0:1, :], in_=zslab[0:1, :]), "sST", 16)
            p.inc("sync", lambda e: e.dma_start(out=xf[1][Q0[0]:Q0[0] + 1, :], in_=zslab[0:1, :]), "sST", 16)
            p.inc("sync", lambda e: e.dma_start(out=xf[1][Q0[1]:Q0[1] + 1, :], in_=zslab[0:1, :]), "sST", 16)
            p.inc("sync", lambda e: e.dma_start(
                out=xb0[:, :].rearrange("p (kc n) -> p kc n", n=ECP)[:, :, 0:1],
                in_=zslab_b[:, :].rearrange("p (kc o) -> p kc o", o=1),
                allow_slow_non_contiguous=True), "sST", 16)
            p.flush()

        def ln_stats_apply(xbuf, stat, q2s, zf):
            p.wait("scalar", "sX")
            p.wait("scalar", "sV")
            p.inc("scalar", lambda e: e.activation(q2s[:, 0:1024], xbuf[:, :], AF.Square,
                                                   accum_out=stat[:, 1:2]), "sSC", 1)
            p.wait("vector", "sSC")
            p.inc("vector", lambda e: e.reduce_sum(stat[:, 0:1], xbuf[:, :], axis=AX.X), "sV", 1)
            p.inc("vector", lambda e: e.tensor_scalar_mul(stat[:, 2:3], stat[:, 0:1], 1.0 / D), "sV", 1)
            p.inc("vector", lambda e: e.tensor_mul(stat[:, 3:4], stat[:, 2:3], stat[:, 2:3]), "sV", 1)
            p.inc("vector", lambda e: e.scalar_tensor_tensor(stat[:, 4:5], stat[:, 1:2], 1.0 / D,
                                                             stat[:, 3:4], ALU.mult, ALU.subtract), "sV", 1)
            p.wait("scalar", "sV")
            p.inc("scalar", lambda e: e.activation(stat[:, 5:6], stat[:, 4:5], AF.Sqrt, bias=epsb[:, 0:1]), "sSC", 1)
            p.wait("vector", "sSC")
            p.inc("vector", lambda e: e.reciprocal(stat[:, 4:5], stat[:, 5:6]), "sV", 1)
            p.inc("vector", lambda e: e.tensor_scalar(stat[:, 5:6], stat[:, 2:3], stat[:, 4:5], -1.0,
                                                      ALU.mult, ALU.mult), "sV", 1)
            p.wait("scalar", "sV")
            p.inc("scalar", lambda e: e.activation(zf[:, :], xbuf[:, :], AF.Identity,
                                                   bias=stat[:, 5:6], scale=stat[:, 4:5]), "sSC", 1)

        def store_master(idx, row0, sbf32, zb, fmsl, ps_tb, shadow):
            p.wait("sync", "sV")
            p.inc("sync", lambda e: e.dma_start(out=xf[idx][row0:row0 + 128, :], in_=sbf32[:, :]), "sST", 16)
            if shadow:
                p.wait("vector", "sST")
                p.inc("vector", lambda e: e.tensor_copy(zb[:, :], sbf32[:, :]), "sV", 1)
                p.wait("tensor", "sV")
                for kc in range(8):
                    pt = ps_tb[kc % 2]
                    p.wait("tensor", "sV")
                    p.inc("tensor", lambda e, k=kc, b=pt: e.transpose(
                        b[:, 0:128], zb[:, k * 128:(k + 1) * 128], ident[:, :]), "sPE", 1)
                    p.wait("vector", "sPE")
                    p.inc("vector", lambda e, k=kc, b=pt: e.tensor_copy(
                        fmsl[:, k * 128:(k + 1) * 128], b[:, 0:128]), "sV", 1)
                p.wait("sync", "sV")
                p.inc("sync", lambda e: e.dma_start(
                    out=xb0[:, :].rearrange("p (kc n) -> p kc n", n=ECP)[:, :, row0:row0 + 128],
                    in_=fmsl[:, :].rearrange("p (kc n) -> p kc n", n=128)), "sST", 16)

        with ExitStack() as sp:
            SBP = lambda n, sh, dt=F32: sp.enter_context(nc.sbuf_tensor(n, sh, dt))
            xsl = [SBP(f"cysl{i}", [128, D]) for i in range(2)]
            z32 = [SBP(f"cz32{i}", [128, D]) for i in range(2)]
            zb = sp.enter_context(nc.sbuf_tensor("cyzb_sb", [128, D], BF16))
            fmsl = sp.enter_context(nc.sbuf_tensor("cyfm_sb", [128, D], BF16))
            stat = SBP("cystat_sb", [128, 8])
            q2s = sp.enter_context(nc.sbuf_tensor("cyq2s_sb", [128, 1024], BF16))
            ps_tb = [sp.enter_context(nc.psum_tensor(f"cyps{i}", [128, 512], BF16)) for i in range(2)]
            for sl in range(EC // 128):
                xbuf = xsl[sl % 2]; zf = z32[sl % 2]
                if fused and sl < 4:
                    p.wait("vector", "sSC")
                    p.inc("vector", lambda e, d=xbuf: e.memset(d[:, :], 0.0), "sV", 1)
                else:
                    p.wait("sync", "sSC")
                    if fused:
                        p.inc("sync", lambda e, s_=sl, d=xbuf: e.dma_start(
                            out=d[:, :], in_=x3scr_h[(s_ - 4) * 128:(s_ - 3) * 128, :]), "sX", 16)
                    else:
                        p.inc("sync", lambda e, s_=sl, d=xbuf: e.dma_start(
                            out=d[:, :], in_=x3e_h[s_ * 128:(s_ + 1) * 128, :]), "sX", 16)
                p.wait("scalar", "sST", max(0, p.sems["sST"].n - 32))
                ln_stats_apply(xbuf, stat, q2s, zf)
                store_master(0, 1 + sl * 128, zf, zb, fmsl, ps_tb, True)
            p.flush()

        for l in range(2):
            q0, nq_l = Q0[l], NQ[l]
            kv0, nkv = KV0[l], NKV[l]
            f0, nf = F0[l], NF[l]

            with ExitStack() as sp:
                qsr = sp.enter_context(nc.sbuf_tensor(f"qsr{l}_sb", [1, 64], F32))
                ksr = sp.enter_context(nc.sbuf_tensor(f"ksr{l}_sb", [1, 64], F32))
                mbf = sp.enter_context(nc.sbuf_tensor(f"mbf{l}_sb", [1, 1], BF16))
                ps_m = sp.enter_context(nc.psum_tensor(f"psm{l}", [128, 8], F32))
                p.inc("sync", lambda e: e.dma_start(out=qsr[:, :], in_=qsr_h[l][:, :]), "sX", 16)
                p.inc("sync", lambda e: e.dma_start(out=ksr[:, :], in_=ksr_h[l][:, :]), "sX", 16)
                p.wait("vector", "sX")
                p.inc("vector", lambda e: e.tensor_reduce(mscr[0:1, 0:1], qsr[0:1, :], AX.X, ALU.max,
                                                          apply_absolute_value=True), "sV", 1)
                p.inc("vector", lambda e: e.tensor_reduce(mscr[0:1, 1:2], ksr[0:1, :], AX.X, ALU.max,
                                                          apply_absolute_value=True), "sV", 1)
                p.inc("vector", lambda e: e.tensor_mul(mscr[0:1, 2:3], mscr[0:1, 0:1], mscr[0:1, 1:2]), "sV", 1)
                p.inc("vector", lambda e: e.tensor_scalar_mul(mscr[0:1, 3:4], mscr[0:1, 2:3], -8.0), "sV", 1)
                p.inc("vector", lambda e: e.tensor_copy(mbf[0:1, 0:1], mscr[0:1, 3:4]), "sV", 1)
                p.wait("tensor", "sV")
                p.inc("tensor", lambda e: e.matmul(ps_m[:, 0:1], ones1[0:1, :], mbf[0:1, 0:1],
                                                   start=True, stop=True), "sPE", 1)
                p.wait("vector", "sPE")
                p.inc("vector", lambda e: e.tensor_copy(negm[:, 0:1], ps_m[:, 0:1]), "sV", 1)
                for w in range(16):
                    p.inc("vector", lambda e, w_=w: e.tensor_tensor(
                        negmp[:, w_:w_ + 1], negm[:, 0:1], pmk[:, w_:w_ + 1], ALU.add), "sV", 1)
                p.flush()

            s_qkv = ExitStack()
            kfm = s_qkv.enter_context(nc.sbuf_tensor(f"kfm{l}_sb", [128, 4 * nkv], BF16))
            vtm = s_qkv.enter_context(nc.sbuf_tensor(f"vtm{l}_sb", [128, (nkv // 128) * 520], BF16))
            qfm = s_qkv.enter_context(nc.sbuf_tensor(f"qfm{l}_sb", [128, 4 * nq_l], BF16))

            def qk_prep(dst, n_toks, tok0, wcol0, csh, ssh, s2h, do_v, tga):
                with ExitStack() as sp:
                    SBA = lambda n, sh, dt=BF16: sp.enter_context(nc.sbuf_tensor(n + tga, sh, dt))
                    PSA = lambda n, sh, dt=F32: sp.enter_context(nc.psum_tensor(n + tga, sh, dt))
                    wqk = [SBA(f"wqk{i}_sb", [128, 8 * 128]) for i in range(2)]
                    wv = SBA("wv_sb", [128, 8 * 512]) if do_v else None
                    cs_t = SBA("cs_sb", [128, n_toks])
                    ss_t = SBA("ss_sb", [128, n_toks])
                    s24 = SBA("s24_sb", [128, 128])
                    rhs = [SBA(f"rhs{i}_sb", [128, 8 * 512]) for i in range(2)]
                    kraw4 = SBA("kraw4_sb", [128, 4 * 512])
                    kscd = [SBA("kscd0_sb", [128, 512])] * 2
                    tmp = [SBA("tmp0_sb", [128, 512])] * 2
                    t2b = [SBA("t2b0_sb", [128, 512])] * 2
                    t1b = [SBA("t1b0_sb", [128, 512])] * 2
                    q2b = [SBA("q2b0_sb", [128, 512])] * 2
                    nrmf = SBA("nrmf_sb", [128, 512], F32)
                    rinv = SBA("rinv_sb", [128, 512], F32)
                    rnb = SBA("rnb_sb", [128, 512])
                    ps_r = [PSA(f"ps_r{i}", [128, 512]) for i in range(2)]
                    ps_n4 = PSA("ps_n4", [128, 512])
                    ps_b = [PSA(f"ps_b{i}", [128, 512]) for i in range(2)]
                    ps_v = [PSA(f"ps_v{i}", [128, 512]) for i in range(2)] if do_v else None
                    p.inc("gpsimd", lambda e: e.dma_start(out=cs_t[:, :], in_=csh[:, tok0:tok0 + n_toks]), "sX", 16)
                    p.inc("gpsimd", lambda e: e.dma_start(out=ss_t[:, :], in_=ssh[:, tok0:tok0 + n_toks]), "sX", 16)
                    p.inc("gpsimd", lambda e: e.dma_start(out=s24[:, :], in_=s2h[:, :]), "sW", 16)
                    if do_v:
                        for kc in range(8):
                            p.inc("gpsimd", lambda e, k=kc: e.dma_start(
                                out=wv[:, k * 512:(k + 1) * 512],
                                in_=wqkvT_h[l][k * 128:(k + 1) * 128, 1024:1536]), "sW", 16)
                        p.wait("vector", "sW")
                        p.inc("vector", lambda e: e.memset(
                            vtm[:, :].rearrange("p (t h c) -> p t h c", h=8, c=65)[:, :, :, 64:65], 1.0), "sV", 1)
                    ntt = n_toks // 512
                    rhs_cap = {}
                    pe_cap = {}
                    rnb_cap = {-1: 0}
                    qcap = {}

                    def emit_rhs(t_):
                        rb_ = rhs[t_ % 2]
                        p.wait("sync", "sPE", pe_cap.get(t_ - 2, 0))
                        for kc in range(8):
                            off = 0 if kc < 4 else -1
                            p.inc("sync", lambda e, k=kc, tx=t_, o=off: e.dma_start(
                                out=rb_[:, k * 512:(k + 1) * 512],
                                in_=xb0[:, :].rearrange("p (kc n) -> p kc n", n=ECP)[
                                    :, k, 1 + tok0 + tx * 512 + o:1 + tok0 + (tx + 1) * 512 + o]), "sX", 16)
                        rhs_cap[t_] = p.sems["sX"].n

                    emit_rhs(0)
                    for tt in range(n_toks // 512):
                        rb = rhs[tt % 2]
                        p.wait("tensor", "sX", rhs_cap[tt])
                        if do_v:
                            p.wait("tensor", "sW")
                            for tc4 in range(4):
                                pv = ps_v[tc4 % 2]
                                p.bw("tensor", "Cv", tc4 % 2)
                                for kc in range(8):
                                    p.inc("tensor", lambda e, b=pv, k=kc, t4=tc4, r_=rb: e.matmul(
                                        b[:, 0:512], r_[:, k * 512 + t4 * 128:k * 512 + (t4 + 1) * 128],
                                        wv[:, k * 512:(k + 1) * 512],
                                        start=(k == 0), stop=(k == 7)), "sPE", 1)
                                p.wait("vector", "sPE")
                                p.inc("vector", lambda e, b=pv, t_=tt, t4=tc4: e.tensor_copy(
                                    vtm[:, (t_ * 4 + t4) * 520:(t_ * 4 + t4 + 1) * 520].rearrange(
                                        "p (h c) -> p h c", c=65)[:, :, 0:64],
                                    b[:, 0:512].rearrange("p (h c) -> p h c", c=64)), "sV", 1)
                                p.bd("Cv", tc4 % 2)
                        for j in range(4):
                            wb = wqk[j % 2]
                            p.bw("gpsimd", "Cwq", j % 2, ("sPE",))
                            for kc in range(8):
                                p.inc("gpsimd", lambda e, k=kc, j_=j, d=wb: e.dma_start(
                                    out=d[:, k * 128:(k + 1) * 128],
                                    in_=wqkvT_h[l][k * 128:(k + 1) * 128,
                                                   wcol0 + j_ * 128:wcol0 + (j_ + 1) * 128]), "sW", 16)
                            p.wait("tensor", "sW")
                            pb = ps_r[j % 2]
                            p.bw("tensor", "Cr", j % 2, ("sV",))
                            for kc in range(8):
                                p.inc("tensor", lambda e, b=pb, k=kc, d=wb, r_=rb: e.matmul(
                                    b[:, 0:512], d[:, k * 128:(k + 1) * 128],
                                    r_[:, k * 512:(k + 1) * 512],
                                    start=(k == 0), stop=(k == 7)), "sPE", 1)
                            mm_tail = p.sems["sPE"].n
                            p.bd("Cwq", j % 2, ("sPE",))
                            p.wait("vector", "sPE", mm_tail)
                            p.inc("vector", lambda e, b=pb, j_=j: e.tensor_copy(
                                kraw4[:, j_ * 512:(j_ + 1) * 512], b[:, 0:512]), "sV", 1)
                            p.bd("Cr", j % 2, ("sV",))
                            p.wait("scalar", "sV")
                            p.wait("scalar", "sPE", qcap.get(0, 0))
                            p.inc("scalar", lambda e, j_=j, qb=q2b[j % 2]: e.activation(
                                qb[:, :], kraw4[:, j_ * 512:(j_ + 1) * 512], AF.Square), "sSC", 1)
                            p.wait("tensor", "sSC")
                            p.wait("tensor", "sV", rnb_cap[tt - 1])
                            p.inc("tensor", lambda e, j_=j, qb=q2b[j % 2]: e.matmul(
                                ps_n4[32 * j_:32 * j_ + 2, 0:512], ind2[:, 0:2], qb[:, :],
                                start=True, stop=True, tile_position=(0, 32 * j_),
                                skip_group_check=True), "sPE", 1)
                            qcap[0] = p.sems["sPE"].n
                        pe_cap[tt] = p.sems["sPE"].n
                        if tt + 1 < n_toks // 512:
                            emit_rhs(tt + 1)
                        p.wait("scalar", "sPE")
                        p.inc("scalar", lambda e: e.activation(nrmf[:, :], ps_n4[:, 0:512], AF.Sqrt,
                                                               bias=g["eps24"][:, 0:1]), "sSC", 1)
                        p.wait("vector", "sSC")
                        p.inc("vector", lambda e: e.reciprocal(rinv[:, :], nrmf[:, :]), "sV", 1)
                        p.inc("vector", lambda e: e.tensor_copy(rnb[:, :], rinv[:, :]), "sV", 1)
                        rnb_cap[tt] = p.sems["sV"].n
                        for j in range(4):
                            pc = ps_b[j % 2]
                            p.bw("tensor", "Cb", j % 2, ("sV",))
                            p.wait("tensor", "sV", rnb_cap[tt])
                            p.inc("tensor", lambda e, j_=j, b=pc: e.matmul(
                                b[:, 0:512], s24[32 * j_:32 * j_ + 2, 0:128],
                                rnb[32 * j_:32 * j_ + 2, :],
                                start=True, stop=True, tile_position=(32 * j_, 0),
                                skip_group_check=True), "sPE", 1)
                            psb_tail = p.sems["sPE"].n
                            p.wait("vector", "sPE", psb_tail)
                            ks = kscd[j % 2]
                            p.inc("vector", lambda e, b=pc, j_=j, k_=ks: e.tensor_mul(
                                k_[:, :], kraw4[:, j_ * 512:(j_ + 1) * 512], b[:, 0:512]), "sV", 1)
                            p.bd("Cb", j % 2, ("sV",))
                            tb2 = t2b[j % 2]; tb1 = t1b[j % 2]; tpb = tmp[j % 2]
                            p.inc("vector", lambda e, t_=tt, k_=ks, tp=tpb: e.tensor_mul(
                                tp[:, :], k_[:, :], ss_t[:, t_ * 512:(t_ + 1) * 512]), "sV", 1)
                            for b0, sb0 in ((0, 32), (32, 0), (64, 96), (96, 64)):
                                p.inc("vector", lambda e, a=b0, b=sb0, tp=tpb, t2=tb2: e.tensor_copy(
                                    t2[a:a + 32, :], tp[b:b + 32, :]), "sV", 1)
                            p.inc("vector", lambda e, t_=tt, k_=ks, t1=tb1: e.tensor_mul(
                                t1[:, :], k_[:, :], cs_t[:, t_ * 512:(t_ + 1) * 512]), "sV", 1)
                            p.inc("vector", lambda e, j_=j, t_=tt, t1=tb1, t2=tb2: e.tensor_add(
                                dst[:, j_ * n_toks + t_ * 512:j_ * n_toks + (t_ + 1) * 512],
                                t1[:, :], t2[:, :]), "sV", 1)
                    p.flush()

            qk_prep(kfm, nkv, kv0, 512, kcs_h[l], kss_h[l], ks2_h[l], True, f"ka{l}")
            qk_prep(qfm, nq_l, q0, 0, qcs_h[l], qss_h[l], qs2_h[l], False, f"qa{l}")

            s_of = ExitStack()
            ofm = s_of.enter_context(nc.sbuf_tensor(f"ofm{l}_sb", [128, 4 * nq_l], BF16, side="right"))
            with ExitStack() as sp:
                tgb = f"b{l}"
                SBB = lambda n, sh, dt=BF16: sp.enter_context(nc.sbuf_tensor(n + tgb, sh, dt))
                PSB = lambda n, sh, dt=F32: sp.enter_context(nc.psum_tensor(n + tgb, sh, dt))
                eab = [SBB(f"eab{i}", [128, 1024]) for i in range(2)]
                smf = SBB("smf_sb", [128, 1024], F32)
                rsc = SBB("rsc_sb", [1, 1024], F32)
                rbh = SBB("rbh_sb", [1, 1024])
                rlf = SBB("rlf_sb", [1, 1024], F32)
                rbl = SBB("rbl_sb", [1, 1024])
                dcp = SBB("dcp_sb", [1, 1024], F32)
                rbs = SBB("rbs_sb", [64, 512], F32)
                ps_s = [PSB(f"ps_s{i}", [128, 1024]) for i in range(2)]
                ps_oa = [PSB(f"ps_oa{i}", [128, 512]) for i in range(2)]
                ps_ob = [PSB(f"ps_ob{i}", [128, 512]) for i in range(2)]
                exp_tail = [0, 0]
                omm_tail = [0, 0]
                oev_tail = [0, 0]
                for j in range(4):
                    for w in range(nq_l // 512):
                        wslot = (8 if l == 1 else 0) + w
                        poa = ps_oa[w % 2]; pob = ps_ob[w % 2]
                        for kt in range(8):
                            kcol = q0 - kv0 + 512 * (w - 1) + 128 * kt
                            pb = ps_s[kt % 2]
                            p.wait("tensor", "sSC", exp_tail[kt % 2])
                            p.inc("tensor", lambda e, b=pb, j_=j, w_=w, kc_=kcol: e.matmul(
                                b[:, 0:512], kfm[0:64, j_ * nkv + kc_:j_ * nkv + kc_ + 128],
                                qfm[0:64, j_ * nq_l + w_ * 512:j_ * nq_l + (w_ + 1) * 512],
                                start=True, stop=True, tile_position=(0, 0)), "sPE", 1)
                            p.inc("tensor", lambda e, b=pb, j_=j, w_=w, kc_=kcol: e.matmul(
                                b[:, 512:1024], kfm[64:128, j_ * nkv + kc_:j_ * nkv + kc_ + 128],
                                qfm[64:128, j_ * nq_l + w_ * 512:j_ * nq_l + (w_ + 1) * 512],
                                start=True, stop=True, tile_position=(64, 0)), "sPE", 1)
                            p.wait("vector", "sPE")
                            p.wait("vector", "sPE", omm_tail[kt % 2])
                            p.inc("vector", lambda e, b=pb, k_=kt: e.tensor_add(
                                smf[:, 0:512], b[:, 0:512], trio[:, k_ * 512:(k_ + 1) * 512]), "sV", 1)
                            p.inc("vector", lambda e, b=pb, k_=kt: e.tensor_add(
                                smf[:, 512:1024], b[:, 512:1024], trio[:, k_ * 512:(k_ + 1) * 512]), "sV", 1)
                            p.wait("scalar", "sV")
                            if kt < 4:
                                p.inc("scalar", lambda e, i=kt % 2, ws=wslot: e.activation(
                                    eab[i][:, :], smf[:, :], AF.Exp, bias=negmp[:, ws:ws + 1]), "sSC", 1)
                            else:
                                p.inc("scalar", lambda e, i=kt % 2: e.activation(
                                    eab[i][:, :], smf[:, :], AF.Exp, bias=negm[:, 0:1]), "sSC", 1)
                            exp_tail[kt % 2] = p.sems["sSC"].n
                            p.wait("tensor", "sSC", exp_tail[kt % 2])
                            if kt == 0:
                                p.wait("tensor", "sV", oev_tail[w % 2])
                            vcol = (kcol // 128) * 520
                            p.inc("tensor", lambda e, k=kt, i=kt % 2, vc=vcol, j_=j, po=poa: e.matmul(
                                po[0:65, 0:512], vtm[:, vc + (2 * j_) * 65:vc + (2 * j_ + 1) * 65],
                                eab[i][:, 0:512],
                                start=(k == 0), stop=(k == 7)), "sPE", 1)
                            p.inc("tensor", lambda e, k=kt, i=kt % 2, vc=vcol, j_=j, po=pob: e.matmul(
                                po[0:65, 0:512], vtm[:, vc + (2 * j_ + 1) * 65:vc + (2 * j_ + 2) * 65],
                                eab[i][:, 512:1024],
                                start=(k == 0), stop=(k == 7)), "sPE", 1)
                            omm_tail[kt % 2] = p.sems["sPE"].n
                        p.wait("vector", "sPE")
                        p.inc("vector", lambda e, po=poa: e.tensor_copy(dcp[0:1, 0:512], po[64:65, 0:512]), "sV", 1)
                        p.inc("vector", lambda e, po=pob: e.tensor_copy(dcp[0:1, 512:1024], po[64:65, 0:512]), "sV", 1)
                        p.inc("vector", lambda e: e.reciprocal(rsc[0:1, :], dcp[0:1, :]), "sV", 1)
                        p.inc("vector", lambda e: e.tensor_copy(rbh[0:1, :], rsc[0:1, :]), "sV", 1)
                        p.inc("vector", lambda e: e.tensor_sub(rlf[0:1, :], rsc[0:1, :], rbh[0:1, :]), "sV", 1)
                        p.inc("vector", lambda e: e.tensor_copy(rbl[0:1, :], rlf[0:1, :]), "sV", 1)
                        for half, (pso, c0) in enumerate(((poa, 0), (pob, 512))):
                            p.wait("tensor", "sV")
                            p.inc("tensor", lambda e, c=c0, po=pso: e.matmul(
                                po[64:128, 0:512], ones1[0:1, 0:64], rbh[0:1, c:c + 512],
                                start=True, stop=False, tile_position=(0, 64),
                                skip_group_check=True), "sPE", 1)
                            p.inc("tensor", lambda e, c=c0, po=pso: e.matmul(
                                po[64:128, 0:512], ones1[0:1, 0:64], rbl[0:1, c:c + 512],
                                start=False, stop=True, tile_position=(0, 64),
                                skip_group_check=True), "sPE", 1)
                            p.wait("vector", "sPE")
                            p.inc("vector", lambda e, po=pso: e.tensor_copy(
                                rbs[0:64, :], po[64:128, 0:512]), "sV", 1)
                            p.inc("vector", lambda e, j_=j, w_=w, hb=half, po=pso: e.scalar_tensor_tensor(
                                ofm[hb * 64:(hb + 1) * 64, j_ * nq_l + w_ * 512:j_ * nq_l + (w_ + 1) * 512],
                                po[0:64, 0:512], 1.0, rbs[0:64, :], ALU.mult, ALU.mult), "sV", 1)
                        oev_tail[w % 2] = p.sems["sV"].n
                p.flush()
            s_qkv.close()

            with ExitStack() as sp:
                tgc = f"c{l}"
                wo = sp.enter_context(nc.sbuf_tensor("cwo_sb" + tgc, [128, 4 * 1024], BF16))
                xin = [sp.enter_context(nc.sbuf_tensor(f"cxin{i}" + tgc, [128, D], F32)) for i in range(2)]
                xm = [sp.enter_context(nc.sbuf_tensor(f"cxm{i}" + tgc, [128, D], F32)) for i in range(2)]
                ps_w = [sp.enter_context(nc.psum_tensor(f"cpsw{i}" + tgc, [128, 512], F32)) for i in range(2)]
                for kc in range(4):
                    p.inc("gpsimd", lambda e, k=kc: e.dma_start(
                        out=wo[:, k * 1024:(k + 1) * 1024], in_=woT_h[l][k * 128:(k + 1) * 128, :]), "sW", 16)
                p.wait("tensor", "sW")
                for tc in range(nq_l // 128):
                    xb_ = xin[tc % 2]
                    ob = xm[tc % 2]
                    p.wait("sync", "sV")
                    p.inc("sync", lambda e, t_=tc, d=xb_: e.dma_start(
                        out=d[:, :], in_=xf[0][1 + q0 + t_ * 128:1 + q0 + (t_ + 1) * 128, :]), "sX", 16)
                    for dt_ in range(2):
                        pb = ps_w[dt_]
                        p.bw("tensor", "Cwo", dt_)
                        for kc in range(4):
                            p.inc("tensor", lambda e, b=pb, k=kc, t_=tc, d_=dt_: e.matmul(
                                b[:, 0:512], ofm[:, k * nq_l + t_ * 128:k * nq_l + (t_ + 1) * 128],
                                wo[:, k * 1024 + d_ * 512:k * 1024 + (d_ + 1) * 512],
                                start=(k == 0), stop=(k == 3)), "sPE", 1)
                        p.wait("vector", "sPE")
                        p.wait("vector", "sX")
                        p.wait("vector", "sST", max(0, p.sems["sST"].n - 16))
                        p.inc("vector", lambda e, b=pb, d_=dt_, o=ob, d2=xb_: e.tensor_add(
                            o[:, d_ * 512:(d_ + 1) * 512], b[:, 0:512], d2[:, d_ * 512:(d_ + 1) * 512]), "sV", 1)
                        p.bd("Cwo", dt_)
                    p.wait("sync", "sV")
                    p.inc("sync", lambda e, t_=tc, o=ob: e.dma_start(
                        out=xf[1][1 + q0 + t_ * 128:1 + q0 + (t_ + 1) * 128, :], in_=o[:, :]), "sST", 16)
                p.flush()
            s_of.close()

            s_w2 = ExitStack()
            w2sb = s_w2.enter_context(nc.sbuf_tensor(f"w2sb{l}", [128, NFC * 1024], BF16))
            with ExitStack() as spw:
                for fc in range(NFC):
                    cw = _fcw(fc)
                    p.inc("gpsimd", lambda e, f_=fc, c_=cw: e.dma_start(
                        out=w2sb[0:c_, f_ * 1024:(f_ + 1) * 1024],
                        in_=w2T_h[l][f_ * 128:f_ * 128 + c_, :]), "sW", 16)
                p.flush()
            subs = []
            t0s = 0
            while t0s < nf:
                subs.append((t0s, min(1024, nf - t0s)))
                t0s += 1024
            for (st0, snt) in subs:
                with ExitStack() as sp:
                    tgd = f"d{l}_{st0}"
                    SBD = lambda n, sh, dt=BF16: sp.enter_context(nc.sbuf_tensor(n + tgd, sh, dt))
                    PSD = lambda n, sh, dt=F32: sp.enter_context(nc.psum_tensor(n + tgd, sh, dt))
                    hn = SBD("hn_sb", [128, 8 * snt])
                    acts = SBD("acts_sb", [128, NFC * snt])
                    lg8 = SBD("lg8_sb", [128, 8], F32); lb8 = SBD("lb8_sb", [128, 8], F32)
                    b1c = SBD("b1c_sb", [128, 2 * NFC], F32)
                    b2r = SBD("b2r_sb", [1, D])
                    wf = [SBD(f"wf{i}_sb", [128, 2 * 1024]) for i in range(2)]
                    xsl = [SBD(f"dxsl{i}", [128, D], F32) for i in range(2)]
                    zb = SBD("dzb_sb", [128, D])
                    stat = SBD("dstat_sb", [128, 8], F32)
                    q2s = SBD("dq2s_sb", [128, 1024])
                    atile = SBD("atile_sb", [128, 512])
                    gtile = SBD("gtile_sb", [128, 512])
                    x3sl = [SBD(f"x3sl{i}", [128, D], F32) for i in range(2)]
                    yout = [SBD(f"yout{i}", [128, D], F32) for i in range(2)]
                    fmsl = SBD("dfm_sb", [128, D])
                    ps_g = [PSD(f"dpsg{i}", [128, 1024]) for i in range(2)]
                    ps_d = [PSD(f"dpsd{i}", [128, 512]) for i in range(2)]
                    ps_tb = [PSD(f"dpst{i}", [128, 512], BF16) for i in range(2)]
                    p.inc("sync", lambda e: e.dma_start(out=lg8[:, :], in_=lg8_h[l][:, :]), "sX", 16)
                    p.inc("sync", lambda e: e.dma_start(out=lb8[:, :], in_=lb8_h[l][:, :]), "sX", 16)
                    p.inc("sync", lambda e: e.dma_start(out=b1c[:, :], in_=b1c_h[l][:, :]), "sX", 16)
                    p.inc("gpsimd", lambda e: e.dma_start(out=b2r[:, :], in_=b2r_h[l][:, :]), "sW", 16)
                    for sl in range(snt // 128):
                        xbuf = xsl[sl % 2]
                        r0 = 1 + f0 + st0 + sl * 128
                        p.wait("sync", "sSC")
                        p.inc("sync", lambda e, r_=r0, d=xbuf: e.dma_start(
                            out=d[:, 0:512], in_=xf[1][r_:r_ + 128, 0:512]), "sX", 16)
                        p.inc("sync", lambda e, r_=r0, d=xbuf: e.dma_start(
                            out=d[:, 512:1024], in_=xf[1][r_ - 1:r_ + 127, 512:1024]), "sX", 16)
                        p.wait("scalar", "sX")
                        ln_stats_apply(xbuf, stat, q2s, xbuf)
                        p.wait("vector", "sSC")
                        p.inc("vector", lambda e, d=xbuf: e.tensor_copy(zb[:, :], d[:, :]), "sV", 1)
                        p.wait("tensor", "sV")
                        for kc in range(8):
                            pt = ps_tb[kc % 2]
                            p.wait("tensor", "sV")
                            p.inc("tensor", lambda e, k=kc, b=pt: e.transpose(
                                b[:, 0:128], zb[:, k * 128:(k + 1) * 128], ident[:, :]), "sPE", 1)
                            p.wait("vector", "sPE")
                            p.inc("vector", lambda e, k=kc, b=pt, s_=sl: e.tensor_scalar(
                                hn[:, k * snt + s_ * 128:k * snt + (s_ + 1) * 128], b[:, 0:128],
                                lg8[:, k:k + 1], lb8[:, k:k + 1], ALU.mult, ALU.add), "sV", 1)
                    NTT = snt // 512
                    for fc in range(NFC):
                        cw = _fcw(fc)
                        wb = wf[fc % 2]
                        p.bw("gpsimd", "Cwf", fc % 2, ("sPE",))
                        for kc in range(8):
                            p.inc("gpsimd", lambda e, k=kc, f_=fc, c_=cw, d=wb: e.dma_start(
                                out=d[:, k * 128:k * 128 + c_],
                                in_=w1T_h[l][k * 128:(k + 1) * 128, f_ * 128:f_ * 128 + c_]), "sW", 16)
                            p.inc("gpsimd", lambda e, k=kc, f_=fc, c_=cw, d=wb: e.dma_start(
                                out=d[:, 1024 + k * 128:1024 + k * 128 + c_],
                                in_=w1T_h[l][k * 128:(k + 1) * 128, HID + f_ * 128:HID + f_ * 128 + c_]), "sW", 16)
                        p.wait("tensor", "sW")
                        for tt in range(NTT):
                            pb = ps_g[tt % 2]
                            p.bw("tensor", "Cw1", tt % 2, ("sV", "sSC"))
                            for kc in range(8):
                                p.inc("tensor", lambda e, b=pb, k=kc, c_=cw, d=wb, t_=tt: e.matmul(
                                    b[0:c_, 0:512], d[:, k * 128:k * 128 + c_],
                                    hn[:, k * snt + t_ * 512:k * snt + (t_ + 1) * 512],
                                    start=(k == 0), stop=(k == 7)), "sPE", 1)
                            for kc in range(8):
                                p.inc("tensor", lambda e, b=pb, k=kc, c_=cw, d=wb, t_=tt: e.matmul(
                                    b[0:c_, 512:1024], d[:, 1024 + k * 128:1024 + k * 128 + c_],
                                    hn[:, k * snt + t_ * 512:k * snt + (t_ + 1) * 512],
                                    start=(k == 0), stop=(k == 7)), "sPE", 1)
                            p.wait("scalar", "sPE")
                            p.wait("scalar", "sV")
                            p.inc("scalar", lambda e, b=pb, c_=cw, f_=fc: e.activation(
                                atile[0:c_, :], b[0:c_, 0:512], AF.Identity,
                                bias=b1c[0:c_, f_:f_ + 1]), "sSC", 1)
                            p.inc("scalar", lambda e, b=pb, c_=cw, f_=fc: e.activation(
                                gtile[0:c_, :], b[0:c_, 512:1024], AF.Gelu,
                                bias=b1c[0:c_, NFC + f_:NFC + f_ + 1]), "sSC", 1)
                            p.wait("vector", "sSC")
                            p.inc("vector", lambda e, c_=cw, f_=fc, t_=tt: e.tensor_mul(
                                acts[0:c_, f_ * snt + t_ * 512:f_ * snt + (t_ + 1) * 512],
                                atile[0:c_, :], gtile[0:c_, :]), "sV", 1)
                            p.bd("Cw1", tt % 2, ("sV", "sSC"))
                        p.bd("Cwf", fc % 2, ("sPE",))
                    p.wait("tensor", "sW")
                    for tc in range(snt // 128):
                        xmb = xsl[tc % 2]
                        ob = yout[tc % 2]
                        r0 = 1 + f0 + st0 + tc * 128
                        p.wait("sync", "sV")
                        p.inc("sync", lambda e, r_=r0, d=xmb: e.dma_start(
                            out=d[:, :], in_=xf[1][r_:r_ + 128, :]), "sX", 16)
                        if l == 1:
                            x3b = x3sl[tc % 2]
                            x3src = x3scr_h if fused else x3own_h
                            p.inc("sync", lambda e, t_=st0 + tc * 128, d=x3b, s3=x3src: e.dma_start(
                                out=d[:, :], in_=s3[t_:t_ + 128, :]), "sX", 16)
                        for dt_ in range(2):
                            pb = ps_d[dt_]
                            p.bw("tensor", "Cw2", dt_)
                            for fc in range(NFC):
                                cw = _fcw(fc)
                                p.inc("tensor", lambda e, b=pb, f_=fc, c_=cw, t_=tc, d_=dt_: e.matmul(
                                    b[:, 0:512], acts[0:c_, f_ * snt + t_ * 128:f_ * snt + (t_ + 1) * 128],
                                    w2sb[0:c_, f_ * 1024 + d_ * 512:f_ * 1024 + (d_ + 1) * 512],
                                    start=(f_ == 0), stop=False), "sPE", 1)
                            p.inc("tensor", lambda e, b=pb, d_=dt_: e.matmul(
                                b[:, 0:512], ones1[0:1, 0:128], b2r[0:1, d_ * 512:(d_ + 1) * 512],
                                start=False, stop=True), "sPE", 1)
                            p.wait("vector", "sPE")
                            p.wait("vector", "sX")
                            p.wait("vector", "sST")
                            p.inc("vector", lambda e, b=pb, d_=dt_, o=ob, d2=xmb: e.tensor_add(
                                o[:, d_ * 512:(d_ + 1) * 512], b[:, 0:512], d2[:, d_ * 512:(d_ + 1) * 512]), "sV", 1)
                            if l == 1:
                                x3b = x3sl[tc % 2]
                                p.inc("vector", lambda e, d_=dt_, o=ob, d3=x3b: e.tensor_add(
                                    o[:, d_ * 512:(d_ + 1) * 512], o[:, d_ * 512:(d_ + 1) * 512],
                                    d3[:, d_ * 512:(d_ + 1) * 512]), "sV", 1)
                            p.bd("Cw2", dt_)
                        if l == 0:
                            store_master(0, r0, ob, zb, fmsl, ps_tb, True)
                        else:
                            p.wait("sync", "sV")
                            p.inc("sync", lambda e, t_=st0 + tc * 128, o=ob: e.dma_start(
                                out=out_h[t_:t_ + 128, :], in_=o[:, :]), "sST", 16)
                    p.flush()
            s_w2.close()
        p.wait("sync", "sST")
        p.flush()
    return nc


def build_fused(dbg=False):
    nc = bass.Bass("TRN2", target_bir_lowering=False)
    x2_scr = nc.dram_tensor("x2_scr", [S, D], F32, kind="Internal").ap()
    x3_scr = nc.dram_tensor("x3_scr", [S, D], F32, kind="Internal").ap()
    if dbg:
        x2_dbg = nc.dram_tensor("x2_dbg", [S, D], F32, kind="ExternalOutput").ap()
        x3_dbg = nc.dram_tensor("x3_dbg", [S, D], F32, kind="ExternalOutput").ap()
    with ExitStack() as gst:
        p = Prog(nc, gst)
        g = mk_globals(nc, p, gst)
        ctx = (nc, p, gst, x2_scr, g, x3_scr, dbg)
        build_A(S, ctx=ctx)
        build_B(ctx=ctx)
        build_C(ctx=ctx)
        if dbg:
            p.inc("sync", lambda e: e.dma_start(out=x2_dbg[:, :], in_=x2_scr[:, :]), "sST", 16)
            p.inc("sync", lambda e: e.dma_start(out=x3_dbg[:, :], in_=x3_scr[:, :]), "sST", 16)
            p.wait("sync", "sST")
        p.flush()
    return nc

def _prep_C(inputs, x3_parts):
    f32 = lambda a: np.ascontiguousarray(a, dtype=np.float32)
    wqkv = np.asarray(inputs["L1_ffn_wqkv"], np.float32)
    wo = np.asarray(inputs["L1_ffn_wo"], np.float32)
    w1 = np.asarray(inputs["L1_ffn_w1"], np.float32)
    b1 = np.asarray(inputs["L1_ffn_b1"], np.float32)
    w2 = np.asarray(inputs["L1_ffn_w2"], np.float32)
    b2 = np.asarray(inputs["L1_ffn_b2"], np.float32)
    lg = np.asarray(inputs["L1_ffn_ln_g"], np.float32)
    lb = np.asarray(inputs["L1_ffn_ln_b"], np.float32)
    qs = np.asarray(inputs["L1_ffn_q_scale"], np.float32)
    ks = np.asarray(inputs["L1_ffn_k_scale"], np.float32)
    shared = {}
    for l in range(2):
        shared[f"f_wqkvT{l}"] = f32(wqkv[l])
        shared[f"f_woT{l}"] = f32(wo[l])
        shared[f"f_w1T{l}"] = f32(w1[l])
        b1c = np.zeros((128, 2 * NFC), np.float32)
        for fc in range(NFC):
            cw = _fcw(fc)
            b1c[0:cw, fc] = b1[l][fc * 128:fc * 128 + cw]
            b1c[0:cw, NFC + fc] = b1[l][HID + fc * 128:HID + fc * 128 + cw]
        shared[f"f_b1c{l}"] = b1c
        shared[f"f_w2T{l}"] = f32(w2[l])
        shared[f"f_b2r{l}"] = f32(b2[l].reshape(1, D))
        shared[f"f_lg8{l}"] = f32(lg[l].reshape(8, 128).T)
        shared[f"f_lb8{l}"] = f32(lb[l].reshape(8, 128).T)
        qs2 = np.zeros((128, 128), np.float32)
        ks2 = np.zeros((128, 128), np.float32)
        for jj in range(4):
            qs2[32 * jj, 0:64] = qs[l]; qs2[32 * jj + 1, 64:128] = qs[l]
            ks2[32 * jj, 0:64] = ks[l]; ks2[32 * jj + 1, 64:128] = ks[l]
        shared[f"f_qs2{l}"] = qs2
        shared[f"f_ks2{l}"] = ks2
        shared[f"f_qsr{l}"] = f32(qs[l].reshape(1, 64))
        shared[f"f_ksr{l}"] = f32(ks[l].reshape(1, 64))
    trim = np.zeros((128, 8 * 512), np.float32)
    trio = np.zeros((128, 8 * 512), np.float32)
    kr = np.arange(128)[:, None]; qr = np.arange(512)[None, :]
    for kt in range(8):
        if kt < 4:
            keep = (qr <= 128 * kt + kr)
        else:
            keep = (128 * (kt - 4) + kr <= qr)
        trim[:, kt * 512:(kt + 1) * 512] = keep
        trio[:, kt * 512:(kt + 1) * 512] = -30000.0 * (~keep)
    shared["f_trim"] = trim
    shared["f_trio"] = trio
    x3f = np.zeros((B, S, D), np.float32)
    for c in range(8):
        b, hh = c // 2, c % 2
        x3f[b, hh * T:(hh + 1) * T] = x3_parts[c]
    in_maps = []
    for c in range(8):
        b, hh = c // 2, c % 2
        hT = hh * T
        pos = hT - 2560 + np.arange(EC)
        x3e = np.zeros((EC, D), np.float32)
        lo, hi = max(0, 2560 - hT), EC
        x3e[lo:hi] = x3f[b, pos[lo]:pos[hi - 1] + 1]
        m = dict(shared)
        qcs0, qss0 = xpos_tables(pos, True)
        kcs0, kss0 = xpos_tables(pos, False)
        pmkv = np.zeros((128, 16), np.float32)
        if hh == 0:
            pmkv[:, 3] = -30000.0
            pmkv[:, 8 + 1] = -30000.0
        m.update(x3e=x3e, x3own=np.ascontiguousarray(x3f[b, hT:hT + T]),
                 f_pmk=pmkv)
        for l in range(2):
            m[f"f_qcs{l}"] = qcs0; m[f"f_qss{l}"] = qss0
            m[f"f_kcs{l}"] = kcs0; m[f"f_kss{l}"] = kss0
        in_maps.append(m)
    return in_maps


_NC_CACHE = {}

def _get_programs():
    if "A" not in _NC_CACHE:
        _NC_CACHE["A"] = build_A(T)
        _NC_CACHE["B"] = build_B()
        _NC_CACHE["C"] = build_C()
    return _NC_CACHE["A"], _NC_CACHE["B"], _NC_CACHE["C"]


FUSED = True

def kernel(**inputs):
    if FUSED:
        return kernel_fused(**inputs)
    return kernel_unfused(**inputs)


def kernel_unfused(**inputs):
    ncA, ncB, ncC = _get_programs()
    resA = run_bass_kernel_spmd(ncA, _prep_A(inputs), core_ids=list(range(8)))
    x2_parts = [np.asarray(resA.results[c]["x2"]) for c in range(8)]
    resB = run_bass_kernel_spmd(ncB, _prep_B(inputs, x2_parts), core_ids=list(range(8)))
    x3_parts = [np.asarray(resB.results[c]["x3"]) for c in range(8)]
    resC = run_bass_kernel_spmd(ncC, _prep_C(inputs, x3_parts), core_ids=list(range(8)))
    out = np.empty((B, S, D), np.float32)
    for c in range(8):
        b, hh = c // 2, c % 2
        out[b, hh * T:(hh + 1) * T] = np.asarray(resC.results[c]["out"])
    return out

def _prep_F(inputs):
    in_maps_a = _prep_A(inputs, nq=S)
    b_shared = _prep_B(inputs, [np.zeros((T, D), np.float32)] * 8)[0]
    for k in ("x2f",):
        b_shared.pop(k, None)
    c_maps = None
    f32 = lambda a: np.ascontiguousarray(a, dtype=np.float32)
    cm = _prep_C(inputs, [np.zeros((T, D), np.float32)] * 8)[0]
    for k in ("x3e", "x3own"):
        cm.pop(k, None)
    pos = -512 + np.arange(EC)
    qcs0, qss0 = xpos_tables(pos, True)
    kcs0, kss0 = xpos_tables(pos, False)
    pmkv = np.zeros((128, 16), np.float32)
    pmkv[:, 0] = -30000.0
    pmkv[:, 8] = -30000.0
    cm["f_pmk"] = pmkv
    for l in range(2):
        cm[f"f_qcs{l}"] = qcs0; cm[f"f_qss{l}"] = qss0
        cm[f"f_kcs{l}"] = kcs0; cm[f"f_kss{l}"] = kss0
    in_maps = []
    for c in range(8):
        m = dict(in_maps_a[c])
        m.update(b_shared)
        m.update(cm)
        in_maps.append(m)
    return in_maps


def kernel_fused(**inputs):
    if "F" not in _NC_CACHE:
        _NC_CACHE["F"] = build_fused()
    ncf = _NC_CACHE["F"]
    res = run_bass_kernel_spmd(ncf, _prep_F(inputs), core_ids=list(range(8)))
    out = np.empty((B, S, D), np.float32)
    for c in range(8):
        b, hh = c // 2, c % 2
        out[b, hh * T:(hh + 1) * T] = np.asarray(res.results[c]["out"])[hh * T:(hh + 1) * T]
    return out
